# version 08 (final)
# speedup vs baseline: 1.6202x; 1.0151x over previous
import numpy as np
import ml_dtypes
import concourse.bass as bass
import concourse.mybir as mybir
from concourse.bass_core_rust import IndirectOffsetOnAxis

F32 = mybir.dt.float32
BF16 = mybir.dt.bfloat16
U32 = mybir.dt.uint32
AF = mybir.ActivationFunctionType
ALU = mybir.AluOpType
AX = mybir.AxisListType

NCORES = 8
SEQ = 8192
B = 2
NTOK = B * SEQ
D = 1024
R = NTOK // NCORES
RT = R // 128
DCH = D // 128

BF = ml_dtypes.bfloat16


def bfr(x):
    return np.asarray(x, dtype=BF)


class Sem:
    def __init__(self, nc, name):
        self.h = nc.alloc_semaphore(name)
        self.v = 0

    def dma(self, inst):
        inst.then_inc(self.h, 16)
        self.v += 16
        return self.v

    def eng(self, inst):
        inst.then_inc(self.h, 1)
        self.v += 1
        return self.v


class Sem2:

    def __init__(self, nc, name):
        self.s = [Sem(nc, name + "_0"), Sem(nc, name + "_1")]
        self.marks = {}

    def dma(self, inst, t):
        r = self.s[t % 2].dma(inst)
        self.marks[t] = r
        return r

    def mark(self, t):
        self.marks[t] = self.s[t % 2].v

    def wait(self, eng, t):
        eng.wait_ge(self.s[t % 2].h, self.marks.get(t, 16 * (t // 2 + 1)))


class Prog:

    def __init__(self, name):
        self.name = name
        self.nc = bass.Bass("TRN2", target_bir_lowering=False)
        self.ins = {}
        self._semn = 0

    def sem(self, tag=None):
        self._semn += 1
        return Sem(self.nc, f"s{self._semn}_{tag or 'x'}")

    def inp(self, name, shape, dtype=F32):
        return self.nc.dram_tensor(name, list(shape), dtype, kind="ExternalInput").ap()

    def out(self, name, shape, dtype=F32):
        return self.nc.dram_tensor(name, list(shape), dtype, kind="ExternalOutput").ap()

    def dram(self, name, shape, dtype=F32):
        return self.nc.dram_tensor(name, list(shape), dtype, kind="Internal").ap()

    def sb(self, name, shape, dtype=F32):
        return self.nc.alloc_sbuf_tensor(name, list(shape), dtype)

    def ps(self, name, shape, dtype=F32):
        return self.nc.alloc_psum_tensor(name, list(shape), dtype)


def host_ident():
    return np.eye(128, dtype=BF)


def emit_consts(p, identf=False):
    nc = p.nc
    ident_in = p.inp("c_ident", [128, 128], BF16)
    ident = p.sb("ident", [128, 128], BF16)
    idf = p.sb("identf", [128, 128], F32) if identf else None
    ones = p.sb("ones", [1, 512], BF16)
    ld = p.sem("cld")
    cv = p.sem("ccv")
    with nc.Block() as blk:
        @blk.sync
        def _(sync):
            ld.dma(sync.dma_start(out=ident[:], in_=ident_in[:]))
            sync.wait_ge(ld.h, ld.v)

        @blk.vector
        def _(v):
            v.memset(ones[:], 1.0)
            if identf:
                v.wait_ge(ld.h, 16)
                cv.eng(v.tensor_copy(out=idf[:], in_=ident[:]))
    return dict(ident=ident, identf=idf, ones=ones)



def stage_ln_dram(p, x_rows_ap, xnbf, ntiles, dch=DCH, eps=1e-5, name=None,
                  xout_sb=None, pp=None):
    nc = p.nc
    W = dch * 128
    nm = name or f"ln{p._semn}"
    if xout_sb is None:
        xt_bufs = [p.sb(f"{nm}_x{i}", [128, W], F32) for i in range(2)]
        def xtile(t):
            return xt_bufs[t % 2][:]
    else:
        def xtile(t):
            return xout_sb[:, t * W:(t + 1) * W]
    scr = pp["big"] if (pp is not None and W <= 1024) else p.ps(f"{nm}_scr", [128, W], F32)
    sums = p.sb(f"{nm}_sum", [128, ntiles], F32)
    sumsq = p.sb(f"{nm}_sq", [128, ntiles], F32)
    stat = p.sb(f"{nm}_st", [128, 4 * ntiles], F32)
    ld = Sem2(nc, nm + "ld")
    cs = p.sem(nm + "cs")
    cv = p.sem(nm + "cv")

    with nc.Block() as blk:
        @blk.sync
        def _(sync):
            for t in range(ntiles):
                if xout_sb is None and t >= 2:
                    sync.wait_ge(cs.h, 2 * (t - 1))
                ld.dma(sync.dma_start(out=xtile(t),
                                      in_=x_rows_ap[t * 128:(t + 1) * 128, :]), t)

        @blk.scalar
        def _(s):
            for t in range(ntiles):
                ld.wait(s, t)
                if cs.v:
                    s.wait_ge(cs.h, cs.v)
                cs.eng(s.activation(out=scr[:, 0:W], in_=xtile(t), func=AF.Identity,
                                    accum_out=sums[:, t:t + 1]))
                s.wait_ge(cs.h, cs.v)
                cs.eng(s.activation(out=scr[:, 0:W], in_=xtile(t), func=AF.Square,
                                    accum_out=sumsq[:, t:t + 1]))
        cs_stats = cs.v

        @blk.vector
        def _(v):
            NT = ntiles
            mean = stat[:, 0:NT]
            rstd = stat[:, NT:2 * NT]
            nmr = stat[:, 2 * NT:3 * NT]
            v.wait_ge(cs.h, cs_stats)
            cv.eng(v.tensor_scalar_mul(mean, sums[:], 1.0 / W))
            v.wait_ge(cv.h, cv.v)
            cv.eng(v.tensor_scalar_mul(sumsq[:], sumsq[:], 1.0 / W))
            v.wait_ge(cv.h, cv.v)
            cv.eng(v.scalar_tensor_tensor(out=rstd, in0=mean, scalar=-1.0,
                                          in1=mean, op0=ALU.mult, op1=ALU.mult))
            v.wait_ge(cv.h, cv.v)
            cv.eng(v.tensor_add(rstd, rstd, sumsq[:]))
            v.wait_ge(cv.h, cv.v)
            cv.eng(v.tensor_scalar_add(rstd, rstd, eps))
        cv_var = cv.v

        @blk.scalar
        def _(s):
            NT = ntiles
            s.wait_ge(cv.h, cv_var)
            cs.eng(s.activation(out=stat[:, 3 * NT:4 * NT], in_=stat[:, NT:2 * NT],
                                func=AF.Ln))
            s.wait_ge(cs.h, cs.v)
            cs.eng(s.activation(out=stat[:, NT:2 * NT], in_=stat[:, 3 * NT:4 * NT],
                                func=AF.Exp, scale=-0.5))
        cs_sqrt = cs.v

        @blk.vector
        def _(v):
            NT = ntiles
            mean = stat[:, 0:NT]
            rstd = stat[:, NT:2 * NT]
            nmr = stat[:, 2 * NT:3 * NT]
            v.wait_ge(cs.h, cs_sqrt)
            cv.eng(v.scalar_tensor_tensor(out=nmr, in0=mean, scalar=-1.0,
                                          in1=rstd, op0=ALU.mult, op1=ALU.mult))
        cv_nmr = cv.v

    NT = ntiles
    rstd = stat[:, NT:2 * NT]
    nmr = stat[:, 2 * NT:3 * NT]
    with nc.Block() as blk:
        ld2 = Sem2(nc, nm + "ld2")
        if xout_sb is None:
            @blk.sync
            def _(sync):
                for t in range(ntiles):
                    if t >= 2:
                        sync.wait_ge(cs.h, cs.v + t - 1)
                    ld2.dma(sync.dma_start(out=xtile(t),
                                           in_=x_rows_ap[t * 128:(t + 1) * 128, :]), t)

        @blk.scalar
        def _(s):
            for t in range(ntiles):
                if xout_sb is None:
                    ld2.wait(s, t)
                cs.eng(s.activation(out=xnbf[:, t * W:(t + 1) * W],
                                    in_=xtile(t), func=AF.Identity,
                                    scale=rstd[:, t:t + 1], bias=nmr[:, t:t + 1]))
    return


def stage_transpose(p, consts, src_bf, dst_bf, ntiles, dch=DCH, pp=None):
    nc = p.nc
    W = dch * 128
    NT = ntiles * 128
    pb = pp["bf"] if pp is not None else [p.ps(f"tp{p._semn}_{i}", [128, 512], BF16) for i in range(2)]
    tp = p.sem("tp")
    ev = p.sem("tpev")
    nchunks = ntiles * dch
    with nc.Block() as blk:
        @blk.tensor
        def _(t):
            for i in range(nchunks):
                k, ti = divmod(i, ntiles)
                g, j = divmod(i, 4)
                if g >= 2:
                    t.wait_ge(ev.h, g - 1)
                tp_inst = t.transpose(out=pb[g % 2][:, j * 128:(j + 1) * 128],
                                      in_=src_bf[:, ti * W + k * 128: ti * W + (k + 1) * 128],
                                      identity=consts["ident"][:])
                if j == 3 or i == nchunks - 1:
                    tp.eng(tp_inst)

        @blk.vector
        def _(v):
            ngroups = (nchunks + 3) // 4
            for g in range(ngroups):
                v.wait_ge(tp.h, g + 1)
                lo = g * 4
                hi = min(lo + 4, nchunks)
                ev.eng(v.tensor_copy(out=dst_bf[:, lo * 128:hi * 128],
                                     in_=pb[g % 2][:, 0:(hi - lo) * 128]))
    return tp, tp.v


def mm_acc(t, psum, pairs, rank1=None):
    n = len(pairs)
    last = None
    for k, (l, r) in enumerate(pairs):
        is_last = (k == n - 1) and rank1 is None
        last = t.matmul(out=psum, lhsT=l, rhs=r, start=(k == 0), stop=is_last)
    if rank1 is not None:
        last = t.matmul(out=psum, lhsT=rank1[0], rhs=rank1[1], start=False, stop=True)
    return last


def psum_pool(p):
    return dict(
        a=[p.ps(f"pp_a{i}", [128, 512], F32) for i in range(4)],
        bf=[p.ps(f"pp_bf{i}", [128, 512], BF16) for i in range(2)],
        big=p.ps("pp_big", [128, 1024], F32),
    )
    return tp, tp.v


def build_l0ffn(Rr=R, E=8, ET=12, DFF=2048, Dm=D, resid_scale=0.5, pre=False):
    p = Prog("l0ffn")
    nc = p.nc
    nt = Rr // 128
    dch = Dm // 128
    fch = DFF // 128
    NTK = nt * 128
    ndg = Dm // 512
    nfg = DFF // 512
    ntg = (nt + 3) // 4

    x_in = p.inp("x", [Rr, Dm])
    gw_in = p.inp("gate_w", [ET, Dm])
    eb_in = p.inp("ebias", [1, ET])
    w1_in = p.inp("w1", [E, DFF, Dm], BF16 if pre else F32)
    w2_in = p.inp("w2", [E, Dm, DFF], BF16 if pre else F32)
    xout = p.out("xout", [Rr, Dm])

    if not pre:
        w1bf_d = p.dram("w1bf_d", [DFF, Dm], BF16)
        w2bf_d = p.dram("w2bf_d", [Dm, DFF], BF16)

    xn = p.sb("xn", [128, nt * Dm], BF16)
    xnT = p.sb("xnT", [128, nt * Dm], BF16)
    w1T = p.sb("w1T", [128, dch * DFF], BF16)
    w2T = p.sb("w2T", [128, fch * Dm], BF16)
    hT = p.sb("hT", [128, fch * 512], BF16)
    gwT = p.sb("gwT", [128, dch * ET], BF16)
    ebf = p.sb("ebf", [1, ET], BF16)
    ebff = p.sb("ebff", [1, ET], F32)
    lg = p.sb("lg", [128, 2 * ET], F32)
    mx8 = p.sb("mx8", [128, 8], F32)
    t0 = p.sb("t0", [128, 8], F32)
    wv = p.sb("wv", [128, nt * ET], F32)
    wz = p.sb("wz", [128, nt], F32)
    xfin = p.sb("xfin", [128, 2 * Dm], F32)
    xob = p.sb("xob", [128, 2 * Dm], F32)

    pp = psum_pool(p)
    consts = emit_consts(p)

    stage_ln_dram(p, x_in, xn, nt, dch=dch, name="l0ln", pp=pp)
    tpsem, tptot = stage_transpose(p, consts, xn, xnT, nt, dch=dch, pp=pp)

    g1 = p.sem("g1")
    tl = p.sem("tl")
    gv = p.sem("gv")
    gs = p.sem("gs")
    gwsb = p.sb("gwsb", [ET, Dm], BF16)
    with nc.Block() as blk:
        @blk.gpsimd
        def _(g):
            g1.dma(g.dma_start(out=gwsb[:], in_=gw_in[:]))
            g1.dma(g.dma_start(out=ebff[:], in_=eb_in[:]))

        @blk.tensor
        def _(t):
            t.wait_ge(g1.h, g1.v)
            for d in range(dch):
                tl.eng(t.transpose(out=pp["bf"][d % 2][:, 0:ET],
                                   in_=gwsb[:, d * 128:(d + 1) * 128],
                                   identity=consts["ident"][0:ET, 0:ET]))
                t.wait_ge(gv.h, d + 1)

        @blk.vector
        def _(v):
            for d in range(dch):
                v.wait_ge(tl.h, d + 1)
                gv.eng(v.tensor_copy(out=gwT[:, d * ET:(d + 1) * ET],
                                     in_=pp["bf"][d % 2][:, 0:ET]))
            v.wait_ge(g1.h, g1.v)
            gv.eng(v.tensor_copy(out=ebf[:], in_=ebff[:]))
        gv_eb = gv.v
    tl_base = tl.v
    with nc.Block() as blk:
        @blk.tensor
        def _(t):
            t.wait_ge(gv.h, gv_eb)
            for ti in range(nt):
                if ti >= 2:
                    t.wait_ge(gv.h, gv_eb + (ti - 1) * 8)
                pairs = [(xnT[:, d * NTK + ti * 128: d * NTK + (ti + 1) * 128],
                          gwT[:, d * ET:(d + 1) * ET]) for d in range(dch)]
                tl.eng(mm_acc(t, pp["a"][ti % 2][:, 0:ET], pairs,
                              rank1=(consts["ones"][0:1, 0:128], ebf[:])))

        @blk.vector
        def _(v):
            for ti in range(nt):
                v.wait_ge(tl.h, tl_base + ti + 1)
                lgs = lg[:, (ti % 2) * ET:(ti % 2 + 1) * ET]
                gv.eng(v.tensor_copy(out=lgs, in_=pp["a"][ti % 2][:, 0:ET]))
                v.wait_ge(gv.h, gv.v)
                gv.eng(v.max(out=mx8[:], in_=lgs))
                v.wait_ge(gv.h, gv.v)
                gv.eng(v.tensor_sub(t0[:, 2:3], mx8[:, 0:1], mx8[:, 1:2]))
                v.wait_ge(gs.h, ti + 1)
                gv.eng(v.tensor_scalar(out=t0[:, 1:2], in0=t0[:, 0:1],
                                       scalar1=-1.0, scalar2=1.0,
                                       op0=ALU.mult, op1=ALU.add))
                v.wait_ge(gv.h, gv.v)
                wvt = wv[:, ti * ET:(ti + 1) * ET]
                gv.eng(v.tensor_scalar(out=wvt, in0=lgs, scalar1=mx8[:, 0:1],
                                       scalar2=t0[:, 0:1], op0=ALU.is_equal,
                                       op1=ALU.mult))
                v.wait_ge(gv.h, gv.v)
                gv.eng(v.tensor_scalar(out=lgs, in0=lgs, scalar1=mx8[:, 1:2],
                                       scalar2=t0[:, 1:2], op0=ALU.is_equal,
                                       op1=ALU.mult))
                v.wait_ge(gv.h, gv.v)
                gv.eng(v.tensor_add(wvt, wvt, lgs))
                v.wait_ge(gv.h, gv.v)
                gv.eng(v.tensor_reduce(out=wz[:, ti:ti + 1],
                                       in_=wvt[:, E:ET], axis=AX.X, op=ALU.add))

        @blk.scalar
        def _(s):
            for ti in range(nt):
                s.wait_ge(gv.h, gv_eb + ti * 8 + 3)
                gs.eng(s.activation(out=t0[:, 0:1], in_=t0[:, 2:3],
                                    func=AF.Sigmoid))
    gv_gate = gv.v

    with nc.Block() as blk:
        @blk.vector
        def _(v):
            v.wait_ge(tpsem.h, tptot)
            for ti in range(nt):
                v.wait_ge(gv.h, gv.v)
                gv.eng(v.tensor_scalar_mul(xn[:, ti * Dm:(ti + 1) * Dm],
                                           xn[:, ti * Dm:(ti + 1) * Dm],
                                           wz[:, ti:ti + 1]))
    acc = xn

    wg = p.sem("wg")
    wg2 = p.sem("wg2")
    wt = p.sem("wt")
    wt2 = p.sem("wt2")
    th = p.sem("th")
    sh = p.sem("sh")
    ty = p.sem("ty")
    vy = p.sem("vy")
    for e in range(E):
        with nc.Block() as blk:
            th_prev = th.v
            ty_prev = ty.v
            wt_prev = wt.v
            wt2_prev = wt2.v
            if not pre:
                @blk.gpsimd
                def _(g):
                    g.wait_ge(wt.h, wt_prev)
                    wg.dma(g.dma_start(out=w1bf_d[:], in_=w1_in[e]))
                    g.wait_ge(wt2.h, wt2_prev)
                    wg2.dma(g.dma_start(out=w2bf_d[:], in_=w2_in[e]))

            w1src = w1_in[e] if pre else w1bf_d
            w2src = w2_in[e] if pre else w2bf_d

            @blk.sync
            def _(sync, w1src=w1src, w2src=w2src):
                sync.wait_ge(th.h, th_prev)
                if not pre:
                    sync.wait_ge(wg.h, wg.v)
                for d in range(dch):
                    wt.dma(sync.dma_start(out=w1T[:, d * DFF:(d + 1) * DFF],
                                          in_=w1src[:, d * 128:(d + 1) * 128],
                                          transpose=True))
                sync.wait_ge(ty.h, ty_prev)
                if not pre:
                    sync.wait_ge(wg2.h, wg2.v)
                for f in range(fch):
                    wt2.dma(sync.dma_start(out=w2T[:, f * Dm:(f + 1) * Dm],
                                          in_=w2src[:, f * 128:(f + 1) * 128],
                                          transpose=True))
            wt_w1 = wt.v
            wt_all = wt2.v

            th_base = th.v
            sh_base = sh.v
            ty_base = ty.v
            vy_base = vy.v

            @blk.tensor
            def _(t):
                t.wait_ge(wt.h, wt_w1)
                for tg in range(ntg):
                    tw = min(512, NTK - tg * 512)
                    for fc in range(fch):
                        ih = tg * fch + fc
                        if ih >= 2:
                            t.wait_ge(sh.h, sh_base + ih - 1)
                        elif e > 0:
                            t.wait_ge(sh.h, sh_base)
                        pairs = [(w1T[:, d * DFF + fc * 128: d * DFF + (fc + 1) * 128],
                                  xnT[:, d * NTK + tg * 512: d * NTK + tg * 512 + tw])
                                 for d in range(dch)]
                        th.eng(mm_acc(t, pp["a"][ih % 2][:, 0:tw], pairs))
                    if tg == 0:
                        t.wait_ge(wt2.h, wt_all)
                    t.wait_ge(sh.h, sh_base + (tg + 1) * fch)
                    nts = (tw + 127) // 128
                    for ts in range(nts):
                        for dg in range(ndg):
                            iy = (tg * 4 + ts) * ndg + dg
                            if iy >= 2:
                                t.wait_ge(vy.h, vy_base + iy - 1)
                            elif e > 0:
                                t.wait_ge(vy.h, vy_base)
                            pairs = [(hT[:, fc * 512 + ts * 128: fc * 512 + (ts + 1) * 128],
                                      w2T[:, fc * Dm + dg * 512: fc * Dm + (dg + 1) * 512])
                                     for fc in range(fch)]
                            ty.eng(mm_acc(t, pp["a"][2 + iy % 2][:], pairs))

            @blk.scalar
            def _(s):
                for tg in range(ntg):
                    tw = min(512, NTK - tg * 512)
                    for fc in range(fch):
                        ih = tg * fch + fc
                        s.wait_ge(th.h, th_base + ih + 1)
                        if ih == 0 and e > 0:
                            s.wait_ge(ty.h, ty_base)
                        sh.eng(s.activation(out=hT[:, fc * 512: fc * 512 + tw],
                                            in_=pp["a"][ih % 2][:, 0:tw],
                                            func=AF.Silu))

            @blk.vector
            def _(v):
                for tg in range(ntg):
                    tw = min(512, NTK - tg * 512)
                    nts = (tw + 127) // 128
                    for ts in range(nts):
                        ti = tg * 4 + ts
                        for dg in range(ndg):
                            iy = ti * ndg + dg
                            v.wait_ge(ty.h, ty_base + iy + 1)
                            v.wait_ge(vy.h, vy.v)
                            accsl = acc[:, ti * Dm + dg * 512: ti * Dm + (dg + 1) * 512]
                            vy.eng(v.scalar_tensor_tensor(
                                out=accsl, in0=pp["a"][2 + iy % 2][:],
                                scalar=wv[:, ti * ET + e: ti * ET + e + 1],
                                in1=accsl, op0=ALU.mult, op1=ALU.add))

    stage_residual_out(p, acc, x_in, xout, nt, Dm, resid_scale, xfin, xob)
    return p


def stage_residual_out(p, acc, x_in, xout, nt, Dm, resid_scale, xfin, xob):
    nc = p.nc
    fl = Sem2(nc, f"fl{p._semn}")
    fv = p.sem("fv")
    fo = Sem2(nc, f"fo{p._semn}")
    with nc.Block() as blk:
        @blk.sync
        def _(sync):
            for ti in range(nt):
                if ti >= 2:
                    sync.wait_ge(fv.h, ti - 1)
                fl.dma(sync.dma_start(out=xfin[:, (ti % 2) * Dm:(ti % 2 + 1) * Dm],
                                      in_=x_in[ti * 128:(ti + 1) * 128, :]), ti)

        @blk.vector
        def _(v):
            for ti in range(nt):
                fl.wait(v, ti)
                if ti >= 2:
                    fo.wait(v, ti - 2)
                fv.eng(v.scalar_tensor_tensor(
                    out=xob[:, (ti % 2) * Dm:(ti % 2 + 1) * Dm],
                    in0=acc[:, ti * Dm:(ti + 1) * Dm], scalar=resid_scale,
                    in1=xfin[:, (ti % 2) * Dm:(ti % 2 + 1) * Dm],
                    op0=ALU.mult, op1=ALU.add))

        @blk.gpsimd
        def _(g):
            for ti in range(nt):
                g.wait_ge(fv.h, ti + 1)
                fo.dma(g.dma_start(out=xout[ti * 128:(ti + 1) * 128, :],
                                   in_=xob[:, (ti % 2) * Dm:(ti % 2 + 1) * Dm]), ti)
            g.wait_ge(fo.s[0].h, fo.s[0].v)
            g.wait_ge(fo.s[1].h, fo.s[1].v)


def build_l2ffn(Rr=R, E=14, DFF=2048, Dm=D, resid_scale=0.5, pre=False):
    p = Prog("l2ffn")
    nc = p.nc
    nt = Rr // 128
    dch = Dm // 128
    fch = DFF // 128
    NTK = nt * 128
    ndg = Dm // 512
    ntg = (nt + 3) // 4
    WDT = BF16 if pre else F32

    x_in = p.inp("x", [Rr, Dm])
    rw_in = p.inp("router_w", [Dm, E])
    w1_in = p.inp("w1", [E, Dm, DFF], WDT)
    b1_in = p.inp("b1", [E, DFF])
    w2_in = p.inp("w2", [E, DFF, Dm], WDT)
    b2_in = p.inp("b2", [E, Dm])
    xout = p.out("xout", [Rr, Dm])

    xn = p.sb("xn", [128, nt * Dm], BF16)
    xnT = p.sb("xnT", [128, nt * Dm], BF16)
    acc = xn
    w1sb = p.sb("w1sb", [128, dch * DFF], BF16)
    w2sb = p.sb("w2sb", [128, fch * Dm], BF16)
    b1r = p.sb("b1r", [1, DFF], BF16)
    b2r = p.sb("b2r", [1, Dm], BF16)
    hT = p.sb("hT", [128, fch * 512], BF16)
    rwsb = p.sb("rwsb", [128, dch * E], BF16)
    lg = p.sb("lg", [128, 2 * E], F32)
    mx8 = p.sb("mx8", [128, 8], F32)
    t0 = p.sb("t0", [128, 8], F32)
    wv = p.sb("wv", [128, nt * E], F32)
    exsc = p.sb("exsc", [128, E], F32)
    xfin = p.sb("xfin", [128, 2 * Dm], F32)
    xob = p.sb("xob", [128, 2 * Dm], F32)

    pp = psum_pool(p)
    consts = emit_consts(p)
    stage_ln_dram(p, x_in, xn, nt, dch=dch, name="l2ln", pp=pp)
    stage_transpose(p, consts, xn, xnT, nt, dch=dch, pp=pp)

    g1 = p.sem("g1")
    tl = p.sem("tl")
    gv = p.sem("gv")
    gs = p.sem("gs")
    with nc.Block() as blk:
        @blk.gpsimd
        def _(g):
            for d in range(dch):
                g1.dma(g.dma_start(out=rwsb[:, d * E:(d + 1) * E],
                                   in_=rw_in[d * 128:(d + 1) * 128, :]))

        @blk.tensor
        def _(t):
            t.wait_ge(g1.h, g1.v)
            for ti in range(nt):
                if ti >= 2:
                    t.wait_ge(gv.h, (ti - 1) * 5)
                pairs = [(xnT[:, d * NTK + ti * 128: d * NTK + (ti + 1) * 128],
                          rwsb[:, d * E:(d + 1) * E]) for d in range(dch)]
                tl.eng(mm_acc(t, pp["a"][ti % 2][:, 0:E], pairs))

        @blk.vector
        def _(v):
            for ti in range(nt):
                v.wait_ge(tl.h, ti + 1)
                lgs = lg[:, (ti % 2) * E:(ti % 2 + 1) * E]
                gv.eng(v.tensor_copy(out=lgs, in_=pp["a"][ti % 2][:, 0:E]))
                v.wait_ge(gv.h, gv.v)
                gv.eng(v.max(out=mx8[:], in_=lgs))
                v.wait_ge(gv.h, gv.v)
                gv.eng(v.tensor_scalar_mul(t0[:, 0:1], mx8[:, 0:1], -1.0))
                v.wait_ge(gs.h, ti + 1)
                gv.eng(v.reciprocal(t0[:, 2:3], t0[:, 1:2]))
                v.wait_ge(gv.h, gv.v)
                wvt = wv[:, ti * E:(ti + 1) * E]
                gv.eng(v.tensor_scalar(out=wvt, in0=lgs, scalar1=mx8[:, 0:1],
                                       scalar2=t0[:, 2:3], op0=ALU.is_equal,
                                       op1=ALU.mult))
        gv_gate = gv.v

        @blk.scalar
        def _(s):
            for ti in range(nt):
                s.wait_ge(gv.h, ti * 5 + 3)
                if gs.v:
                    s.wait_ge(gs.h, gs.v)
                lgs = lg[:, (ti % 2) * E:(ti % 2 + 1) * E]
                gs.eng(s.activation(out=exsc[:], in_=lgs, func=AF.Exp,
                                    bias=t0[:, 0:1], accum_out=t0[:, 1:2]))

    wg = p.sem("wg")
    wg2 = p.sem("wg2")
    wgb = p.sem("wgb")
    th = p.sem("th")
    sh = p.sem("sh")
    ty = p.sem("ty")
    vy = p.sem("vy")
    for e in range(E):
        with nc.Block() as blk:
            th_prev = th.v
            ty_prev = ty.v
            if pre:
                @blk.sync
                def _(sync):
                    sync.wait_ge(th.h, th_prev)
                    for d in range(dch):
                        wg.dma(sync.dma_start(out=w1sb[:, d * DFF:(d + 1) * DFF],
                                              in_=w1_in[e, d * 128:(d + 1) * 128, :]))

                @blk.scalar
                def _(sc):
                    sc.wait_ge(ty.h, ty_prev)
                    for f in range(fch):
                        wg2.dma(sc.dma_start(out=w2sb[:, f * Dm:(f + 1) * Dm],
                                             in_=w2_in[e, f * 128:(f + 1) * 128, :]))

                @blk.gpsimd
                def _(g):
                    g.wait_ge(th.h, th_prev)
                    g.wait_ge(ty.h, ty_prev)
                    wgb.dma(g.dma_start(out=b1r[:], in_=b1_in[e:e + 1, :]))
                    wgb.dma(g.dma_start(out=b2r[:], in_=b2_in[e:e + 1, :]))
            else:
                @blk.gpsimd
                def _(g):
                    g.wait_ge(th.h, th_prev)
                    g.wait_ge(ty.h, ty_prev)
                    for d in range(dch):
                        wg.dma(g.dma_start(out=w1sb[:, d * DFF:(d + 1) * DFF],
                                           in_=w1_in[e, d * 128:(d + 1) * 128, :]))
                    for f in range(fch):
                        wg2.dma(g.dma_start(out=w2sb[:, f * Dm:(f + 1) * Dm],
                                            in_=w2_in[e, f * 128:(f + 1) * 128, :]))
                    wgb.dma(g.dma_start(out=b1r[:], in_=b1_in[e:e + 1, :]))
                    wgb.dma(g.dma_start(out=b2r[:], in_=b2_in[e:e + 1, :]))
            th_base = th.v
            sh_base = sh.v
            ty_base = ty.v
            vy_base = vy.v

            @blk.tensor
            def _(t):
                t.wait_ge(wg.h, wg.v)
                t.wait_ge(wgb.h, wgb.v)
                for tg in range(ntg):
                    tw = min(512, NTK - tg * 512)
                    for fc in range(fch):
                        ih = tg * fch + fc
                        if ih >= 2:
                            t.wait_ge(sh.h, sh_base + ih - 1)
                        elif e > 0:
                            t.wait_ge(sh.h, sh_base)
                        pairs = [(w1sb[:, d * DFF + fc * 128: d * DFF + (fc + 1) * 128],
                                  xnT[:, d * NTK + tg * 512: d * NTK + tg * 512 + tw])
                                 for d in range(dch)]
                        th.eng(mm_acc(t, pp["a"][ih % 2][:, 0:tw], pairs,
                                      rank1=(b1r[0:1, fc * 128:(fc + 1) * 128],
                                             consts["ones"][0:1, 0:tw])))
                    if tg == 0:
                        t.wait_ge(wg2.h, wg2.v)
                    t.wait_ge(sh.h, sh_base + (tg + 1) * fch)
                    nts = (tw + 127) // 128
                    for ts in range(nts):
                        for dg in range(ndg):
                            iy = (tg * 4 + ts) * ndg + dg
                            if iy >= 2:
                                t.wait_ge(vy.h, vy_base + iy - 1)
                            elif e > 0:
                                t.wait_ge(vy.h, vy_base)
                            pairs = [(hT[:, fc * 512 + ts * 128: fc * 512 + (ts + 1) * 128],
                                      w2sb[:, fc * Dm + dg * 512: fc * Dm + (dg + 1) * 512])
                                     for fc in range(fch)]
                            ty.eng(mm_acc(t, pp["a"][2 + iy % 2][:], pairs,
                                          rank1=(consts["ones"][0:1, 0:128],
                                                 b2r[0:1, dg * 512:(dg + 1) * 512])))

            @blk.scalar
            def _(s):
                for tg in range(ntg):
                    tw = min(512, NTK - tg * 512)
                    for fc in range(fch):
                        ih = tg * fch + fc
                        s.wait_ge(th.h, th_base + ih + 1)
                        if ih == 0 and e > 0:
                            s.wait_ge(ty.h, ty_base)
                        sh.eng(s.activation(out=hT[:, fc * 512: fc * 512 + tw],
                                            in_=pp["a"][ih % 2][:, 0:tw],
                                            func=AF.Gelu))

            @blk.vector
            def _(v):
                for tg in range(ntg):
                    tw = min(512, NTK - tg * 512)
                    nts = (tw + 127) // 128
                    for ts in range(nts):
                        ti = tg * 4 + ts
                        for dg in range(ndg):
                            iy = ti * ndg + dg
                            v.wait_ge(ty.h, ty_base + iy + 1)
                            v.wait_ge(vy.h, vy.v)
                            accsl = acc[:, ti * Dm + dg * 512: ti * Dm + (dg + 1) * 512]
                            if e == 0:
                                vy.eng(v.tensor_scalar_mul(
                                    accsl, pp["a"][2 + iy % 2][:],
                                    wv[:, ti * E + e: ti * E + e + 1]))
                            else:
                                vy.eng(v.scalar_tensor_tensor(
                                    out=accsl, in0=pp["a"][2 + iy % 2][:],
                                    scalar=wv[:, ti * E + e: ti * E + e + 1],
                                    in1=accsl, op0=ALU.mult, op1=ALU.add))

    stage_residual_out(p, acc, x_in, xout, nt, Dm, resid_scale, xfin, xob)
    return p


def stage_pretranspose_weight(p, w_in_ap, wbf_d, wT_sb, rows, cols, gsem, tsem):
    nc = p.nc
    cch = cols // 128
    with nc.Block() as blk:
        @blk.gpsimd
        def _(g):
            gsem.dma(g.dma_start(out=wbf_d[:], in_=w_in_ap))

        @blk.sync
        def _(sync):
            sync.wait_ge(gsem.h, gsem.v)
            for c in range(cch):
                tsem.dma(sync.dma_start(out=wT_sb[:, c * rows:(c + 1) * rows],
                                        in_=wbf_d[:, c * 128:(c + 1) * 128],
                                        transpose=True))
            sync.wait_ge(tsem.h, tsem.v)


def build_l2scan1(Rr=R, Dm=D):
    p = Prog("l2scan1")
    nc = p.nc
    nt = Rr // 128
    dch = Dm // 128
    NTK = nt * 128
    ntg = NTK // 512

    x_in = p.inp("x", [Rr, Dm])
    wf_in = p.inp("W_f", [Dm, Dm])
    bf_in = p.inp("b_f", [1, Dm])
    wi_in = p.inp("W_i", [Dm, Dm])
    bi_in = p.inp("b_i", [1, Dm])
    wh_in = p.inp("W_h", [Dm, Dm])
    bh_in = p.inp("b_h", [1, Dm])
    lo_out = p.out("localout", [Dm, Rr])
    pp_out = p.out("prefP", [Dm, Rr], BF16)
    fc_out = p.out("FC", [Dm, 2])

    wbf_d = p.dram("wbf_d", [Dm, Dm], BF16)

    xn = p.sb("xn", [128, nt * Dm], BF16)
    xnT = p.sb("xnT", [128, nt * Dm], BF16)
    wfT = p.sb("wfT", [128, dch * Dm], BF16)
    wiT = p.sb("wiT", [128, dch * Dm], BF16)
    whT = p.sb("whT", [128, dch * Dm], BF16)
    brow = p.sb("brow", [1, 3 * Dm], BF16)
    browf = p.sb("browf", [1, 3 * Dm], F32)
    fpre = p.sb("fpre", [128, NTK], F32)
    ipre = p.sb("ipre", [128, NTK], F32)
    hpre = p.sb("hpre", [128, NTK], F32)
    assert nt * Dm // 2 >= 2 * NTK
    xn_f32 = xn.bitcast(F32)
    tA = xn_f32[:, 0:NTK]
    tB = xn_f32[:, NTK:2 * NTK]
    fg = p.sb("fg", [128, NTK], F32)
    ug = p.sb("ug", [128, NTK], F32)
    lob = p.sb("lob", [128, NTK], F32)
    pb_ = p.sb("pb_", [128, NTK], F32)
    pbh = p.sb("pbh", [128, NTK], BF16)
    fcb = p.sb("fcb", [128, 2], F32)

    pp = psum_pool(p)
    consts = emit_consts(p)
    stage_ln_dram(p, x_in, xn, nt, dch=dch, name="s1ln", pp=pp)
    stage_transpose(p, consts, xn, xnT, nt, dch=dch, pp=pp)

    wg = p.sem("wg")
    wt = p.sem("wt")
    stage_pretranspose_weight(p, wf_in[:], wbf_d, wfT, Dm, Dm, wg, wt)
    stage_pretranspose_weight(p, wi_in[:], wbf_d, wiT, Dm, Dm, wg, wt)
    stage_pretranspose_weight(p, wh_in[:], wbf_d, whT, Dm, Dm, wg, wt)

    bl = p.sem("bl")
    bv = p.sem("bv")
    with nc.Block() as blk:
        @blk.sync
        def _(sync):
            bl.dma(sync.dma_start(out=browf[0:1, 0:Dm], in_=bf_in[:]))
            bl.dma(sync.dma_start(out=browf[0:1, Dm:2 * Dm], in_=bi_in[:]))
            bl.dma(sync.dma_start(out=browf[0:1, 2 * Dm:3 * Dm], in_=bh_in[:]))

        @blk.vector
        def _(v):
            v.wait_ge(bl.h, bl.v)
            bv.eng(v.tensor_copy(out=brow[:], in_=browf[:]))

    tm = p.sem("tm")
    se = p.sem("se")
    ve = p.sem("ve")
    go = Sem2(nc, "go")
    for ec in range(dch):
        with nc.Block() as blk:
            tm_base = tm.v
            se_base = se.v

            @blk.tensor
            def _(t):
                for tg in range(ntg):
                    for j, (wT, boff) in enumerate([(wfT, 0), (wiT, 1), (whT, 2)]):
                        im = tg * 3 + j
                        if im >= 3:
                            t.wait_ge(se.h, se_base + im - 2)
                        pairs = [(wT[:, d * Dm + ec * 128: d * Dm + (ec + 1) * 128],
                                  xnT[:, d * NTK + tg * 512:d * NTK + (tg + 1) * 512])
                                 for d in range(dch)]
                        tm.eng(mm_acc(t, pp["a"][im % 3][:], pairs,
                                      rank1=(brow[0:1, boff * Dm + ec * 128: boff * Dm + (ec + 1) * 128],
                                             consts["ones"][0:1, 0:512])))

            @blk.scalar
            def _(s):
                for tg in range(ntg):
                    for j, dst in enumerate([fpre, ipre, hpre]):
                        im = tg * 3 + j
                        s.wait_ge(tm.h, tm_base + im + 1)
                        se.eng(s.activation(out=dst[:, tg * 512:(tg + 1) * 512],
                                            in_=pp["a"][im % 3][:], func=AF.Identity))
            se_evac = se.v

            @blk.scalar
            def _(s):
                s.wait_ge(se.h, se_evac)
                se.eng(s.activation(out=tA, in_=fpre[:], func=AF.Abs))
                se.eng(s.activation(out=tB, in_=ipre[:], func=AF.Abs))
                s.wait_ge(se.h, se.v)
                se.eng(s.activation(out=tA, in_=tA, func=AF.Exp, scale=-1.0))
                se.eng(s.activation(out=tB, in_=tB, func=AF.Exp, scale=-1.0))
                s.wait_ge(se.h, se.v)
                se.eng(s.activation(out=tA, in_=tA, func=AF.Ln, bias=1.0))
                se.eng(s.activation(out=tB, in_=tB, func=AF.Ln, bias=1.0))
                s.wait_ge(se.h, se.v)
                se.eng(s.activation(out=fg[:], in_=fpre[:], func=AF.Relu, scale=-1.0))
                se.eng(s.activation(out=ug[:], in_=ipre[:], func=AF.Relu, scale=-1.0))
            se_sp = se.v

            @blk.vector
            def _(v):
                v.wait_ge(se.h, se_sp)
                ve.eng(v.tensor_add(tA, tA, fg[:]))
                ve.eng(v.tensor_add(tB, tB, ug[:]))
                v.wait_ge(ve.h, ve.v)
                ve.eng(v.tensor_sub(tA, tA, tB))
            ve_diff = ve.v

            @blk.scalar
            def _(s):
                s.wait_ge(ve.h, ve_diff)
                se.eng(s.activation(out=fg[:], in_=tA, func=AF.Sigmoid, scale=-1.0))
                se.eng(s.activation(out=ug[:], in_=tA, func=AF.Sigmoid))
                se.eng(s.activation(out=tB, in_=hpre[:], func=AF.Sigmoid))
            se_sig = se.v

            @blk.vector
            def _(v):
                v.wait_ge(se.h, se_sig)
                ve.eng(v.tensor_scalar(out=tA, in0=hpre[:], scalar1=0.0,
                                       scalar2=None, op0=ALU.is_ge))
                v.wait_ge(ve.h, ve.v)
                ve.eng(v.tensor_scalar_add(hpre[:], hpre[:], 0.5))
                v.wait_ge(ve.h, ve.v)
                ve.eng(v.tensor_sub(hpre[:], hpre[:], tB))
                v.wait_ge(ve.h, ve.v)
                ve.eng(v.tensor_mul(hpre[:], hpre[:], tA))
                v.wait_ge(ve.h, ve.v)
                ve.eng(v.tensor_add(tB, tB, hpre[:]))
                v.wait_ge(ve.h, ve.v)
                ve.eng(v.tensor_mul(ug[:], ug[:], tB))
                v.wait_ge(ve.h, ve.v)
                ve.eng(v.tensor_tensor_scan(out=lob[:], data0=fg[:], data1=ug[:],
                                            initial=0.0, op0=ALU.mult, op1=ALU.add))
                v.wait_ge(ve.h, ve.v)
                ve.eng(v.memset(tB, 0.0))
                v.wait_ge(ve.h, ve.v)
                ve.eng(v.tensor_tensor_scan(out=pb_[:], data0=fg[:], data1=tB,
                                            initial=1.0, op0=ALU.mult, op1=ALU.add))
                v.wait_ge(ve.h, ve.v)
                ve.eng(v.tensor_copy(out=pbh[:], in_=pb_[:]))
                v.wait_ge(ve.h, ve.v)
                ve.eng(v.tensor_copy(out=fcb[:, 0:1], in_=pb_[:, NTK - 1:NTK]))
                v.wait_ge(ve.h, ve.v)
                ve.eng(v.tensor_copy(out=fcb[:, 1:2], in_=lob[:, NTK - 1:NTK]))
            ve_all = ve.v

            @blk.sync
            def _(sync):
                sync.wait_ge(ve.h, ve_all)
                go.dma(sync.dma_start(out=lo_out[ec * 128:(ec + 1) * 128, :],
                                      in_=lob[:]), 2 * ec)
                go.dma(sync.dma_start(out=pp_out[ec * 128:(ec + 1) * 128, :],
                                      in_=pbh[:]), 2 * ec)
                go.dma(sync.dma_start(out=fc_out[ec * 128:(ec + 1) * 128, :],
                                      in_=fcb[:]), 2 * ec + 1)
                sync.wait_ge(go.s[0].h, go.s[0].v)
                sync.wait_ge(go.s[1].h, go.s[1].v)
    return p


def build_l2scan2(Rr=R, Dm=D, NPREV=3):
    p = Prog("l2scan2")
    nc = p.nc
    nt = Rr // 128
    dch = Dm // 128
    NTK = nt * 128

    x_in = p.inp("x", [Rr, Dm])
    lo_in = p.inp("localout", [Dm, Rr])
    pp_in = p.inp("prefP", [Dm, Rr], BF16)
    fc_in = p.inp("FCprev", [NPREV, Dm, 2])
    xout = p.out("xout", [Rr, Dm])

    att = p.sb("att", [128, nt * Dm], BF16)
    lob = p.sb("lob", [128, NTK], F32)
    pbh = p.sb("pbh", [128, NTK], BF16)
    fcs = p.sb("fcs", [128, 2 * 4], F32)
    hin = p.sb("hin", [128, 1], F32)
    corr = p.sb("corr", [128, NTK], F32)
    cbf = p.sb("cbf", [128, NTK], BF16)
    xfin = p.sb("xfin", [128, 2 * Dm], F32)
    xob = p.sb("xob", [128, 2 * Dm], F32)

    pp = psum_pool(p)
    consts = emit_consts(p)

    ld = Sem2(nc, "ld")
    lv = p.sem("lv")
    tp = p.sem("tp")
    ev = p.sem("ev")
    for ec in range(dch):
        with nc.Block() as blk:
            @blk.sync
            def _(sync):
                ld.dma(sync.dma_start(out=lob[:], in_=lo_in[ec * 128:(ec + 1) * 128, :]), 0)
                ld.dma(sync.dma_start(out=pbh[:], in_=pp_in[ec * 128:(ec + 1) * 128, :]), 1)
                for j in range(NPREV):
                    ld.dma(sync.dma_start(out=fcs[:, 2 * j:2 * j + 2],
                                          in_=fc_in[j, ec * 128:(ec + 1) * 128, :]), 0)

            lv_base = lv.v

            @blk.vector
            def _(v):
                v.wait_ge(ld.s[0].h, ld.s[0].v)
                v.wait_ge(ld.s[1].h, ld.s[1].v)
                lv.eng(v.tensor_copy(out=hin[:], in_=fcs[:, 1:2]))
                for j in range(1, NPREV):
                    v.wait_ge(lv.h, lv.v)
                    lv.eng(v.tensor_mul(hin[:], hin[:], fcs[:, 2 * j:2 * j + 1]))
                    v.wait_ge(lv.h, lv.v)
                    lv.eng(v.tensor_add(hin[:], hin[:], fcs[:, 2 * j + 1:2 * j + 2]))
                v.wait_ge(lv.h, lv.v)
                lv.eng(v.tensor_scalar(out=corr[:], in0=pbh[:], scalar1=hin[:],
                                       scalar2=None, op0=ALU.mult))
                v.wait_ge(lv.h, lv.v)
                lv.eng(v.tensor_add(corr[:], corr[:], lob[:]))
                v.wait_ge(lv.h, lv.v)
                lv.eng(v.tensor_copy(out=cbf[:], in_=corr[:]))
            lv_all = lv.v

            @blk.tensor
            def _(t):
                t.wait_ge(lv.h, lv_all)
                for ti in range(nt):
                    i = ec * nt + ti
                    g = i // 4
                    if i >= 8:
                        t.wait_ge(ev.h, 4 * (g - 1))
                    tp_inst = t.transpose(out=pp["bf"][g % 2][:, (i % 4) * 128:(i % 4 + 1) * 128],
                                          in_=cbf[:, ti * 128:(ti + 1) * 128],
                                          identity=consts["ident"][:])
                    if i % 4 == 3:
                        tp.eng(tp_inst)

            @blk.vector
            def _(v):
                for ti in range(nt):
                    i = ec * nt + ti
                    if i % 4 == 3:
                        g = i // 4
                        v.wait_ge(tp.h, g + 1)
                        for k in range(4):
                            tii = ti - 3 + k
                            ev.eng(v.tensor_copy(
                                out=att[:, tii * Dm + ec * 128:tii * Dm + (ec + 1) * 128],
                                in_=pp["bf"][g % 2][:, k * 128:(k + 1) * 128]))

    stage_residual_out(p, att, x_in, xout, nt, Dm, 1.0, xfin, xob)
    return p


def build_l0attA(Rr=R, Dm=D):
    p = Prog("attA")
    nc = p.nc
    nt = Rr // 128
    dch = Dm // 128
    NTK = nt * 128
    x_in = p.inp("x", [Rr, Dm])
    xnT_s = p.out("xnT_s", [Dm, Rr], BF16)
    xn = p.sb("xn", [128, nt * Dm], BF16)
    xnT = p.sb("xnT", [128, nt * Dm], BF16)
    pp = psum_pool(p)
    consts = emit_consts(p)
    stage_ln_dram(p, x_in, xn, nt, dch=dch, name="aln", pp=pp)
    stage_transpose(p, consts, xn, xnT, nt, dch=dch, pp=pp)
    so = Sem2(nc, "so")
    with nc.Block() as blk:
        @blk.sync
        def _(sync):
            for d in range(dch):
                so.dma(sync.dma_start(out=xnT_s[d * 128:(d + 1) * 128, :],
                                      in_=xnT[:, d * NTK:(d + 1) * NTK]), d)
            sync.wait_ge(so.s[0].h, so.s[0].v)
            sync.wait_ge(so.s[1].h, so.s[1].v)
    return p


def build_l0attB(NT=NTOK, Dm=D, HL=2, HD=64, SEQL=SEQ):
    p = Prog("l0attB")
    nc = p.nc
    HC = HL * HD
    ntg = NT // 512
    nch = NT // 128
    ncu = SEQL // 128
    nb = NT // SEQL

    xnT_g = p.inp("xnT_g", [Dm, NT], BF16)
    wq_in = p.inp("Wq", [Dm, HC])
    wk_in = p.inp("Wk", [Dm, HC])
    wv_in = p.inp("Wv", [Dm, HC])
    wn_in = p.inp("Wn", [Dm, HL])
    bq_in = p.inp("bq", [1, HC])
    bk_in = p.inp("bk", [1, HC])
    bv_in = p.inp("bv", [1, HC])
    bn_in = p.inp("bn", [1, HL])
    ctx_s = p.out("ctx_s", [NT, HC], BF16)

    dch = Dm // 128
    wq = p.sb("wq", [128, dch * HC], BF16)
    wk = p.sb("wk", [128, dch * HC], BF16)
    wv = p.sb("wv", [128, dch * HC], BF16)
    wn = p.sb("wn", [128, dch * HL], BF16)
    brows = p.sb("brows", [1, 3 * HC + HL], BF16)
    browsf = p.sb("browsf", [1, 3 * HC + HL], F32)
    xch = p.sb("xch", [128, 2 * 512 * dch], BF16)
    qT = p.sb("qT", [128, NT], BF16)
    kT = p.sb("kT", [128, NT], BF16)
    ktm = p.sb("ktm", [128, nch * HC], BF16)
    vtm = p.sb("vtm", [128, nch * HC], BF16)
    nsc = p.sb("nsc", [HL, 512], F32)
    ensb = p.sb("ensb", [128, nch * HL], F32)
    scr = p.sb("scr", [128, 512], BF16)
    stm = p.sb("stm", [128, 2 * 128], BF16)
    stm2 = p.sb("stm2", [128, 2 * 128], BF16)
    msb = p.sb("msb", [128, 2 * 64], BF16)
    macc = p.sb("macc", [128, 64], F32)
    ctxu = p.sb("ctxu", [128, 2 * ncu * HD], BF16)

    pp = psum_pool(p)
    consts = emit_consts(p, identf=True)

    lw = p.sem("lw")
    lb = p.sem("lb")
    bv_ = p.sem("bv")
    with nc.Block() as blk:
        @blk.gpsimd
        def _(g):
            for d in range(dch):
                lw.dma(g.dma_start(out=wq[:, d * HC:(d + 1) * HC],
                                   in_=wq_in[d * 128:(d + 1) * 128, :]))
                lw.dma(g.dma_start(out=wk[:, d * HC:(d + 1) * HC],
                                   in_=wk_in[d * 128:(d + 1) * 128, :]))
                lw.dma(g.dma_start(out=wv[:, d * HC:(d + 1) * HC],
                                   in_=wv_in[d * 128:(d + 1) * 128, :]))
                lw.dma(g.dma_start(out=wn[:, d * HL:(d + 1) * HL],
                                   in_=wn_in[d * 128:(d + 1) * 128, :]))

        @blk.sync
        def _(sync):
            lb.dma(sync.dma_start(out=browsf[0:1, 0:HC], in_=bq_in[:]))
            lb.dma(sync.dma_start(out=browsf[0:1, HC:2 * HC], in_=bk_in[:]))
            lb.dma(sync.dma_start(out=browsf[0:1, 2 * HC:3 * HC], in_=bv_in[:]))
            lb.dma(sync.dma_start(out=browsf[0:1, 3 * HC:3 * HC + HL], in_=bn_in[:]))

        @blk.vector
        def _(v):
            v.wait_ge(lb.h, lb.v)
            bv_.eng(v.tensor_copy(out=brows[:], in_=browsf[:]))

    lx = Sem2(nc, "lx")
    tp_ = p.sem("tp")
    ae = p.sem("ae")
    te = p.sem("te")
    ve_ = p.sem("ve")
    with nc.Block() as blk:
        @blk.sync
        def _(sync):
            for tg in range(ntg):
                if tg >= 2:
                    sync.wait_ge(tp_.h, (tg - 1) * 4)
                for d in range(dch):
                    lx.dma(sync.dma_start(
                        out=xch[:, ((tg % 2) * dch + d) * 512:((tg % 2) * dch + d + 1) * 512],
                        in_=xnT_g[d * 128:(d + 1) * 128, tg * 512:(tg + 1) * 512]), tg)
                lx.mark(tg)

        @blk.tensor
        def _(t):
            t.wait_ge(lw.h, lw.v)
            t.wait_ge(bv_.h, 1)
            for tg in range(ntg):
                lx.wait(t, tg)
                if tg >= 1:
                    t.wait_ge(ae.h, 4 * tg)
                xc = lambda d: xch[:, ((tg % 2) * dch + d) * 512:((tg % 2) * dch + d + 1) * 512]
                for j, (w, boff) in enumerate([(wq, 0), (wk, HC), (wv, 2 * HC)]):
                    pairs = [(w[:, d * HC:(d + 1) * HC], xc(d)) for d in range(dch)]
                    tp_.eng(mm_acc(t, pp["a"][j][:], pairs,
                                   rank1=(brows[0:1, boff:boff + HC],
                                          consts["ones"][0:1, 0:512])))
                pairs = [(wn[:, d * HL:(d + 1) * HL], xc(d)) for d in range(dch)]
                tp_.eng(mm_acc(t, pp["big"][0:HL, 0:512], pairs,
                               rank1=(brows[0:1, 3 * HC:3 * HC + HL],
                                      consts["ones"][0:1, 0:512])))
                t.wait_ge(ae.h, 4 * tg + 2)
                if tg >= 1:
                    t.wait_ge(ve_.h, 3 * tg - 2)
                for s4 in range(4):
                    te.eng(t.transpose(out=pp["bf"][0][:, s4 * 128:(s4 + 1) * 128],
                                       in_=kT[:, tg * 512 + s4 * 128: tg * 512 + (s4 + 1) * 128],
                                       identity=consts["ident"][:]))
                t.wait_ge(ae.h, 4 * tg + 3)
                if tg >= 1:
                    t.wait_ge(ve_.h, 3 * tg - 1)
                for s4 in range(4):
                    te.eng(t.transpose(out=pp["bf"][1][:, s4 * 128:(s4 + 1) * 128],
                                       in_=scr[:, s4 * 128:(s4 + 1) * 128],
                                       identity=consts["ident"][:]))
                t.wait_ge(ae.h, 4 * tg + 4)
                if tg >= 1:
                    t.wait_ge(ve_.h, 3 * tg)
                for s4 in range(4):
                    te.eng(t.transpose(out=pp["big"][:, 512 + s4 * HL: 512 + (s4 + 1) * HL],
                                       in_=nsc[:, s4 * 128:(s4 + 1) * 128],
                                       identity=consts["identf"][0:HL, 0:HL]))

        @blk.scalar
        def _(s):
            for tg in range(ntg):
                s.wait_ge(tp_.h, 4 * tg + 1)
                ae.eng(s.activation(out=qT[:, tg * 512:(tg + 1) * 512],
                                    in_=pp["a"][0][:], func=AF.Identity))
                s.wait_ge(tp_.h, 4 * tg + 2)
                ae.eng(s.activation(out=kT[:, tg * 512:(tg + 1) * 512],
                                    in_=pp["a"][1][:], func=AF.Identity))
                s.wait_ge(tp_.h, 4 * tg + 3)
                if tg >= 1:
                    s.wait_ge(te.h, 12 * tg - 4)
                ae.eng(s.activation(out=scr[:], in_=pp["a"][2][:], func=AF.Identity))
                s.wait_ge(tp_.h, 4 * tg + 4)
                if tg >= 1:
                    s.wait_ge(te.h, 12 * tg)
                ae.eng(s.activation(out=nsc[:], in_=pp["big"][0:HL, 0:512],
                                    func=AF.Exp, scale=-1.0))

        @blk.vector
        def _(v):
            for tg in range(ntg):
                v.wait_ge(te.h, 12 * tg + 4)
                ve_.eng(v.tensor_copy(out=ktm[:, (tg * 4) * HC:(tg * 4 + 4) * HC],
                                      in_=pp["bf"][0][:, 0:512]))
                v.wait_ge(te.h, 12 * tg + 8)
                ve_.eng(v.tensor_copy(out=vtm[:, (tg * 4) * HC:(tg * 4 + 4) * HC],
                                      in_=pp["bf"][1][:, 0:512]))
                v.wait_ge(te.h, 12 * tg + 12)
                ve_.eng(v.tensor_copy(out=ensb[:, (tg * 4) * HL:(tg * 4 + 4) * HL],
                                      in_=pp["big"][:, 512:512 + 4 * HL]))

    tS = p.sem("tS")
    tM = p.sem("tM")
    tC = p.sem("tC")
    vS = p.sem("vS")
    vM = p.sem("vM")
    gS = p.sem("gS")
    aC = p.sem("aC")
    co = Sem2(nc, "co")
    for u in range(nb * HL):
        b, hl = divmod(u, HL)
        base = b * SEQL
        cb = base // 128
        with nc.Block() as blk:
            tS_b, tM_b, tC_b = tS.v, tM.v, tC.v
            vS_b, vM_b, gS_b, aC_b = vS.v, vM.v, gS.v, aC.v

            @blk.tensor
            def _(t):
                for i in range(ncu):
                    ch = cb + i
                    if i >= 2:
                        t.wait_ge(vS.h, vS_b + i - 1)
                    tS.eng(t.matmul(out=pp["a"][1 + i % 2][:, 0:128],
                                    lhsT=kT[hl * HD:(hl + 1) * HD,
                                            base + i * 128: base + (i + 1) * 128],
                                    rhs=qT[hl * HD:(hl + 1) * HD,
                                           base + i * 128: base + (i + 1) * 128],
                                    start=True, stop=True))
                    if i >= 1:
                        t.wait_ge(vM.h, vM_b + 2 * i + 1)
                    tM.eng(t.matmul(out=pp["a"][0][hl * HD:(hl + 1) * HD, 0:HD],
                                    lhsT=ktm[:, ch * HC + hl * HD: ch * HC + (hl + 1) * HD],
                                    rhs=vtm[:, ch * HC + hl * HD: ch * HC + (hl + 1) * HD],
                                    start=True, stop=True,
                                    tile_position=(0, hl * HD)))
                    t.wait_ge(gS.h, gS_b + i + 1)
                    if i >= 1:
                        t.wait_ge(vM.h, vM_b + 2 * i + 2)
                    if i >= 2:
                        t.wait_ge(aC.h, aC_b + i - 1)
                    cb_ap = (pp["a"][3][:, 0:HD] if i % 2 == 0
                             else pp["big"][:, 0:HD])
                    mm1 = t.matmul(out=cb_ap,
                                   lhsT=stm2[:, (i % 2) * 128:(i % 2) * 128 + 128],
                                   rhs=vtm[:, ch * HC + hl * HD: ch * HC + (hl + 1) * HD],
                                   start=True, stop=(i == 0))
                    if i == 0:
                        tC.eng(mm1)
                    else:
                        tC.eng(t.matmul(out=cb_ap,
                                        lhsT=qT[hl * HD:(hl + 1) * HD,
                                                base + i * 128: base + (i + 1) * 128],
                                        rhs=msb[hl * HD:(hl + 1) * HD,
                                                (i % 2) * HD:(i % 2) * HD + HD],
                                        start=False, stop=True))

            @blk.vector
            def _(v):
                vM.eng(v.memset(msb[hl * HD:(hl + 1) * HD, 0:HD], 0.0))
                v.wait_ge(vM.h, vM.v)
                vM.eng(v.memset(macc[hl * HD:(hl + 1) * HD, :], 0.0))
                for i in range(ncu):
                    mslc = macc[hl * HD:(hl + 1) * HD, :]
                    v.wait_ge(tS.h, tS_b + i + 1)
                    if i >= 2:
                        v.wait_ge(gS.h, gS_b + i - 1)
                    vS.eng(v.tensor_copy(out=stm[:, (i % 2) * 128:(i % 2) * 128 + 128],
                                         in_=pp["a"][1 + i % 2][:, 0:128]))
                    if i >= 1:
                        if i >= 2:
                            v.wait_ge(tC.h, tC_b + i - 1)
                        v.wait_ge(vM.h, vM.v)
                        vM.eng(v.tensor_copy(
                            out=msb[hl * HD:(hl + 1) * HD, (i % 2) * HD:(i % 2) * HD + HD],
                            in_=mslc))
                    v.wait_ge(tM.h, tM_b + i + 1)
                    v.wait_ge(vM.h, vM.v)
                    vM.eng(v.tensor_add(mslc, mslc,
                                        pp["a"][0][hl * HD:(hl + 1) * HD, 0:HD]))

            @blk.gpsimd
            def _(g):
                for i in range(ncu):
                    g.wait_ge(vS.h, vS_b + i + 1)
                    if i >= 2:
                        g.wait_ge(tC.h, tC_b + i - 1)
                    gS.eng(g.affine_select(
                        out=stm2[:, (i % 2) * 128:(i % 2) * 128 + 128],
                        in_=stm[:, (i % 2) * 128:(i % 2) * 128 + 128],
                        pattern=[[1, 128]], compare_op=ALU.is_ge,
                        fill=0.0, base=0, channel_multiplier=-1))

            @blk.scalar
            def _(s):
                if u >= 2:
                    co.wait(s, u - 2)
                ub = (u % 2) * ncu * HD
                for i in range(ncu):
                    s.wait_ge(tC.h, tC_b + i + 1)
                    src = (pp["a"][3][:, 0:HD] if i % 2 == 0
                           else pp["big"][:, 0:HD])
                    aC.eng(s.activation(out=ctxu[:, ub + i * HD: ub + (i + 1) * HD],
                                        in_=src, func=AF.Identity,
                                        scale=ensb[:, (cb + i) * HL + hl:(cb + i) * HL + hl + 1]))

            @blk.sync
            def _(sync):
                sync.wait_ge(aC.h, aC_b + ncu)
                co.dma(sync.dma_start(
                    out=bass.AP(ctx_s.tensor, base * HC + hl * HD,
                                [[HC, 128], [128 * HC, ncu], [1, HD]]),
                    in_=bass.AP(ctxu, (u % 2) * ncu * HD,
                                [[2 * ncu * HD, 128], [HD, ncu], [1, HD]])), u)
    with nc.Block() as blk:
        @blk.sync
        def _(sync):
            sync.wait_ge(co.s[0].h, co.s[0].v)
            sync.wait_ge(co.s[1].h, co.s[1].v)
    return p


def build_l0attC(Rr=R, Dm=D, CW=D, resid_scale=1.0):
    p = Prog("l0attC")
    nc = p.nc
    nt = Rr // 128
    cch = CW // 128
    ndg = Dm // 512
    x_in = p.inp("x", [Rr, Dm])
    ctx_in = p.inp("ctx", [Rr, CW], BF16)
    wo_in = p.inp("Wo", [CW, Dm])
    bo_in = p.inp("bo", [1, Dm])
    xout = p.out("xout", [Rr, Dm])

    wo = p.sb("wo_sb", [128, cch * Dm], BF16)
    bo = p.sb("bo_sb", [1, Dm], BF16)
    bof = p.sb("bof_sb", [1, Dm], F32)
    cxt = p.sb("cxt", [128, 2 * CW], BF16)
    cxT = p.sb("cxT", [128, 2 * CW], BF16)
    xfin = p.sb("xfin", [128, 2 * Dm], F32)
    xob = p.sb("xob", [128, 2 * Dm], F32)

    pp = psum_pool(p)
    consts = emit_consts(p)

    lw = p.sem("lw")
    lb = p.sem("lb")
    bv_ = p.sem("bv")
    with nc.Block() as blk:
        @blk.gpsimd
        def _(g):
            for c in range(cch):
                lw.dma(g.dma_start(out=wo[:, c * Dm:(c + 1) * Dm],
                                   in_=wo_in[c * 128:(c + 1) * 128, :]))

        @blk.sync
        def _(sync):
            lb.dma(sync.dma_start(out=bof[:], in_=bo_in[:]))

        @blk.vector
        def _(v):
            v.wait_ge(lb.h, lb.v)
            bv_.eng(v.tensor_copy(out=bo[:], in_=bof[:]))

    lc = Sem2(nc, "lc")
    lx2 = Sem2(nc, "lx2")
    tt = p.sem("tt")
    vv = p.sem("vv")
    tm = p.sem("tm")
    vr = p.sem("vr")
    oo = Sem2(nc, "oo")
    with nc.Block() as blk:
        @blk.sync
        def _(sync):
            for ti in range(nt):
                if ti >= 2:
                    sync.wait_ge(tt.h, (ti - 1) * cch)
                lc.dma(sync.dma_start(out=cxt[:, (ti % 2) * CW:(ti % 2 + 1) * CW],
                                      in_=ctx_in[ti * 128:(ti + 1) * 128, :]), ti)
                if ti >= 2:
                    sync.wait_ge(vr.h, (ti - 1) * ndg)
                lx2.dma(sync.dma_start(out=xfin[:, (ti % 2) * Dm:(ti % 2 + 1) * Dm],
                                       in_=x_in[ti * 128:(ti + 1) * 128, :]), ti)

        @blk.tensor
        def _(t):
            t.wait_ge(lw.h, lw.v)
            t.wait_ge(bv_.h, 1)
            for ti in range(nt):
                lc.wait(t, ti)
                for c in range(cch):
                    i = ti * cch + c
                    if i >= 2:
                        t.wait_ge(vv.h, i - 1)
                    tt.eng(t.transpose(out=pp["bf"][i % 2][:, 0:128],
                                       in_=cxt[:, (ti % 2) * CW + c * 128:(ti % 2) * CW + (c + 1) * 128],
                                       identity=consts["ident"][:]))
                t.wait_ge(vv.h, (ti + 1) * cch)
                for dg in range(ndg):
                    im = ti * ndg + dg
                    if im >= 2:
                        t.wait_ge(vr.h, im - 1)
                    pairs = [(cxT[:, (ti % 2) * CW + c * 128:(ti % 2) * CW + (c + 1) * 128],
                              wo[:, c * Dm + dg * 512: c * Dm + (dg + 1) * 512])
                             for c in range(cch)]
                    tm.eng(mm_acc(t, pp["a"][im % 2][:], pairs,
                                  rank1=(consts["ones"][0:1, 0:128],
                                         bo[0:1, dg * 512:(dg + 1) * 512])))

        @blk.vector
        def _(v):
            for ti in range(nt):
                for c in range(cch):
                    i = ti * cch + c
                    v.wait_ge(tt.h, i + 1)
                    vv.eng(v.tensor_copy(
                        out=cxT[:, (ti % 2) * CW + c * 128:(ti % 2) * CW + (c + 1) * 128],
                        in_=pp["bf"][i % 2][:, 0:128]))
                lx2.wait(v, ti)
                if ti >= 2:
                    oo.wait(v, ti - 2)
                for dg in range(ndg):
                    im = ti * ndg + dg
                    v.wait_ge(tm.h, im + 1)
                    if resid_scale == 1.0:
                        vr.eng(v.tensor_add(
                            xob[:, (ti % 2) * Dm + dg * 512:(ti % 2) * Dm + (dg + 1) * 512],
                            pp["a"][im % 2][:],
                            xfin[:, (ti % 2) * Dm + dg * 512:(ti % 2) * Dm + (dg + 1) * 512]))
                    else:
                        vr.eng(v.scalar_tensor_tensor(
                            out=xob[:, (ti % 2) * Dm + dg * 512:(ti % 2) * Dm + (dg + 1) * 512],
                            in0=pp["a"][im % 2][:], scalar=resid_scale,
                            in1=xfin[:, (ti % 2) * Dm + dg * 512:(ti % 2) * Dm + (dg + 1) * 512],
                            op0=ALU.mult, op1=ALU.add))

        @blk.gpsimd
        def _(g):
            for ti in range(nt):
                g.wait_ge(vr.h, (ti + 1) * ndg)
                oo.dma(g.dma_start(out=xout[ti * 128:(ti + 1) * 128, :],
                                   in_=xob[:, (ti % 2) * Dm:(ti % 2 + 1) * Dm]), ti)
            g.wait_ge(oo.s[0].h, oo.s[0].v)
            g.wait_ge(oo.s[1].h, oo.s[1].v)
    return p


def build_l3attW(Dm=D, DC=4096, DR=512):
    p = Prog("l3attW")
    nc = p.nc
    cch = DC // 128
    dch = Dm // 128
    wdq_in = p.inp("W_DQ", [Dm, DC])
    wuq_in = p.inp("W_UQ", [DC, Dm])
    wqr_in = p.inp("W_QR", [DC, DR])
    weff_out = p.out("W_eff", [Dm, Dm], BF16)
    weffr_out = p.out("W_effR", [Dm, DR], BF16)

    wdq_d = p.dram("wdq_d", [Dm, DC], BF16)
    wdqt = p.sb("wdqt", [128, 2 * Dm], BF16)
    rhsb = p.sb("rhsb", [128, 2 * 512], BF16)
    outb = p.sb("outb", [128, 4 * 512], BF16)

    pp = psum_pool(p)
    consts = emit_consts(p)

    g1 = p.sem("g1")
    with nc.Block() as blk:
        @blk.gpsimd
        def _(g):
            g1.dma(g.dma_start(out=wdq_d[:], in_=wdq_in[:]))
            g.wait_ge(g1.h, g1.v)

    groups = [(False, c0, min(512, Dm - c0)) for c0 in range(0, Dm, 512)]
    groups += [(True, c0, min(512, DR - c0)) for c0 in range(0, DR, 512)]
    tm = p.sem("tm")
    ev = p.sem("ev")
    oo = p.sem("oo")
    nblk = 0
    for og, (is_r, col0, gw) in enumerate(groups):
        src_in = wqr_in if is_r else wuq_in
        for dh in range(2):
            wt = Sem2(nc, f"wt{nblk}")
            wr = Sem2(nc, f"wr{nblk}")
            nblk += 1
            with nc.Block() as blk:
                tm_b = tm.v

                @blk.sync
                def _(sync):
                    for c in range(cch):
                        if c >= 2:
                            sync.wait_ge(tm.h, tm_b + c - 1)
                        wt.dma(sync.dma_start(out=wdqt[:, (c % 2) * Dm:(c % 2 + 1) * Dm],
                                              in_=wdq_d[:, c * 128:(c + 1) * 128],
                                              transpose=True), c)

                @blk.gpsimd
                def _(g):
                    for c in range(cch):
                        if c >= 2:
                            g.wait_ge(tm.h, tm_b + c - 1)
                        wr.dma(g.dma_start(out=rhsb[:, (c % 2) * 512:(c % 2) * 512 + gw],
                                           in_=src_in[c * 128:(c + 1) * 128, col0:col0 + gw]), c)

                @blk.tensor
                def _(t):
                    for c in range(cch):
                        wt.wait(t, c)
                        wr.wait(t, c)
                        for j in range(4):
                            dc = dh * 4 + j
                            ins = t.matmul(out=pp["a"][j][:, 0:gw],
                                           lhsT=wdqt[:, (c % 2) * Dm + dc * 128:(c % 2) * Dm + (dc + 1) * 128],
                                           rhs=rhsb[:, (c % 2) * 512:(c % 2) * 512 + gw],
                                           start=(c == 0), stop=(c == cch - 1))
                        tm.eng(ins)

                tm_done = tm.v

                @blk.scalar
                def _(s):
                    s.wait_ge(tm.h, tm_done)
                    for j in range(4):
                        ev.eng(s.activation(out=outb[:, j * 512:j * 512 + gw],
                                            in_=pp["a"][j][:, 0:gw], func=AF.Identity))

                ev_done = ev.v

                @blk.sync
                def _(sync):
                    sync.wait_ge(ev.h, ev_done)
                    dst = weffr_out if is_r else weff_out
                    for j in range(4):
                        dc = dh * 4 + j
                        oo.dma(sync.dma_start(
                            out=dst[dc * 128:(dc + 1) * 128, col0:col0 + gw],
                            in_=outb[:, j * 512:j * 512 + gw]))
                    sync.wait_ge(oo.h, oo.v)
    return p


def build_l3attW2(RO=512, DC=4096, GW=512):
    p = Prog("l3attW2")
    nc = p.nc
    cch = DC // 128
    a_in = p.inp("Arows", [RO, DC])
    b_in = p.inp("Bcols", [DC, GW])
    out = p.out("Wpart", [RO, GW], BF16)
    a_d = p.dram("a_d", [RO, DC], BF16)
    wdqt = p.sb("wdqt2", [128, 2 * RO], BF16)
    rhsb = p.sb("rhsb2", [128, 2 * GW], BF16)
    outb = p.sb("outb2", [128, 4 * GW], BF16)
    pp = psum_pool(p)
    g1 = p.sem("g1")
    tm = p.sem("tm")
    ev = p.sem("ev")
    oo = p.sem("oo")
    with nc.Block() as blk:
        @blk.gpsimd
        def _(g):
            g1.dma(g.dma_start(out=a_d[:], in_=a_in[:]))
            g.wait_ge(g1.h, g1.v)
    wt = Sem2(nc, "wt")
    wr = Sem2(nc, "wr")
    with nc.Block() as blk:
        @blk.sync
        def _(sync):
            for c in range(cch):
                if c >= 2:
                    sync.wait_ge(tm.h, c - 1)
                wt.dma(sync.dma_start(out=wdqt[:, (c % 2) * RO:(c % 2 + 1) * RO],
                                      in_=a_d[:, c * 128:(c + 1) * 128],
                                      transpose=True), c)

        @blk.gpsimd
        def _(g):
            for c in range(cch):
                if c >= 2:
                    g.wait_ge(tm.h, c - 1)
                wr.dma(g.dma_start(out=rhsb[:, (c % 2) * GW:(c % 2 + 1) * GW],
                                   in_=b_in[c * 128:(c + 1) * 128, :]), c)

        @blk.tensor
        def _(t):
            for c in range(cch):
                wt.wait(t, c)
                wr.wait(t, c)
                for j in range(4):
                    ins = t.matmul(out=pp["a"][j][:, 0:GW],
                                   lhsT=wdqt[:, (c % 2) * RO + j * 128:
                                             (c % 2) * RO + (j + 1) * 128],
                                   rhs=rhsb[:, (c % 2) * GW:(c % 2 + 1) * GW],
                                   start=(c == 0), stop=(c == cch - 1))
                tm.eng(ins)

        @blk.scalar
        def _(s):
            s.wait_ge(tm.h, cch)
            for j in range(4):
                ev.eng(s.activation(out=outb[:, j * GW:(j + 1) * GW],
                                    in_=pp["a"][j][:, 0:GW], func=AF.Identity))

        @blk.sync
        def _(sync):
            sync.wait_ge(ev.h, 4)
            for j in range(4):
                oo.dma(sync.dma_start(out=out[j * 128:(j + 1) * 128, :],
                                      in_=outb[:, j * GW:(j + 1) * GW]))
            sync.wait_ge(oo.h, oo.v)
    return p


def build_l3attA(Rr=R, Dm=D, DR=512, DKV=256, DHR=32):
    p = Prog("l3attA")
    nc = p.nc
    nt = Rr // 128
    dch = Dm // 128
    NTK = nt * 128
    CW = Dm + DR + DKV + DHR
    x_in = p.inp("x", [Rr, Dm])
    weff_in = p.inp("W_eff", [Dm, Dm], BF16)
    weffr_in = p.inp("W_effR", [Dm, DR], BF16)
    wdkv_in = p.inp("W_DKV", [Dm, DKV])
    wkr_in = p.inp("W_KR", [Dm, DHR])
    sq_in = p.inp("scaleq", [DR, Rr], BF16)
    sk_in = p.inp("scalek", [DHR, Rr], BF16)
    qcT_s = p.out("qcT_s", [Dm, Rr], BF16)
    qrT_s = p.out("qrT_s", [DR, Rr], BF16)
    ckvT_s = p.out("ckvT_s", [DKV, Rr], BF16)
    krT_s = p.out("krT_s", [DHR, Rr], BF16)

    xn = p.sb("xn", [128, nt * Dm], BF16)
    xnT = p.sb("xnT", [128, nt * Dm], BF16)
    wall = p.sb("wall", [128, dch * CW], BF16)
    sqb = p.sb("sqb", [128, (DR // 128) * NTK], BF16)
    skb = p.sb("skb", [DHR, NTK], BF16)
    ob = p.sb("ob", [128, 2 * 512], BF16)

    pp = psum_pool(p)
    consts = emit_consts(p)
    stage_ln_dram(p, x_in, xn, nt, dch=dch, name="a3ln", pp=pp)
    stage_transpose(p, consts, xn, xnT, nt, dch=dch, pp=pp)

    lw = p.sem("lw")
    ls = p.sem("ls")
    with nc.Block() as blk:
        @blk.gpsimd
        def _(g):
            for d in range(dch):
                lw.dma(g.dma_start(out=wall[:, d * CW:d * CW + Dm],
                                   in_=weff_in[d * 128:(d + 1) * 128, :]))
                lw.dma(g.dma_start(out=wall[:, d * CW + Dm:d * CW + Dm + DR],
                                   in_=weffr_in[d * 128:(d + 1) * 128, :]))
                lw.dma(g.dma_start(out=wall[:, d * CW + Dm + DR:d * CW + Dm + DR + DKV],
                                   in_=wdkv_in[d * 128:(d + 1) * 128, :]))
                lw.dma(g.dma_start(out=wall[:, d * CW + Dm + DR + DKV:d * CW + CW],
                                   in_=wkr_in[d * 128:(d + 1) * 128, :]))

        @blk.sync
        def _(sync):
            for j in range(DR // 128):
                ls.dma(sync.dma_start(out=sqb[:, j * NTK:(j + 1) * NTK],
                                      in_=sq_in[j * 128:(j + 1) * 128, :]))
            ls.dma(sync.dma_start(out=skb[:], in_=sk_in[:]))

    chunks = []
    for j in range(Dm // 128):
        chunks.append((qcT_s, j * 128, j * 128, 128, None))
    for j in range(DR // 128):
        chunks.append((qrT_s, j * 128, Dm + j * 128, 128, ("q", j)))
    for j in range(DKV // 128):
        chunks.append((ckvT_s, j * 128, Dm + DR + j * 128, 128, None))
    chunks.append((krT_s, 0, Dm + DR + DKV, DHR, ("k", 0)))

    tm = p.sem("tm")
    sc_ = p.sem("sc")
    vv = p.sem("vv")
    so = Sem2(nc, "so")
    with nc.Block() as blk:
        @blk.tensor
        def _(t):
            t.wait_ge(lw.h, lw.v)
            t.wait_ge(ls.h, ls.v)
            im = 0
            for (dst, dr0, co0, cw, scl) in chunks:
                for tg in range(nt // 4):
                    if im >= 2:
                        t.wait_ge(vv.h, im - 1)
                    pairs = [(wall[:, d * CW + co0: d * CW + co0 + cw],
                              xnT[:, d * NTK + tg * 512:d * NTK + (tg + 1) * 512])
                             for d in range(dch)]
                    tm.eng(mm_acc(t, pp["a"][im % 2][0:cw, 0:512], pairs))
                    im += 1

        @blk.scalar
        def _(s):
            im = 0
            for (dst, dr0, co0, cw, scl) in chunks:
                for tg in range(nt // 4):
                    s.wait_ge(tm.h, im + 1)
                    if im >= 2:
                        so.wait(s, im - 2)
                    sc_.eng(s.activation(out=ob[:, (im % 2) * 512:(im % 2 + 1) * 512][0:cw, :],
                                         in_=pp["a"][im % 2][0:cw, 0:512],
                                         func=AF.Identity))
                    im += 1

        @blk.vector
        def _(v):
            im = 0
            for ci, (dst, dr0, co0, cw, scl) in enumerate(chunks):
                for tg in range(nt // 4):
                    v.wait_ge(sc_.h, im + 1)
                    obs = ob[:, (im % 2) * 512:(im % 2 + 1) * 512][0:cw, :]
                    if scl is not None:
                        kind, j = scl
                        tbl = (sqb[:, j * NTK + tg * 512: j * NTK + (tg + 1) * 512]
                               if kind == "q" else
                               skb[0:DHR, tg * 512:(tg + 1) * 512])
                        v.wait_ge(vv.h, vv.v)
                        vv.eng(v.tensor_mul(obs, obs, tbl))
                    else:
                        vv.eng(v.engine_nop())
                    im += 1

        @blk.sync
        def _(sync):
            im = 0
            for ci, (dst, dr0, co0, cw, scl) in enumerate(chunks):
                for tg in range(nt // 4):
                    sync.wait_ge(vv.h, im + 1)
                    so.dma(sync.dma_start(
                        out=dst[dr0:dr0 + cw, tg * 512:(tg + 1) * 512],
                        in_=ob[:, (im % 2) * 512:(im % 2 + 1) * 512][0:cw, :]), im)
                    im += 1
            sync.wait_ge(so.s[0].h, so.s[0].v)
            sync.wait_ge(so.s[1].h, so.s[1].v)
    return p


def build_l3attB(SEQL=SEQ, HL=4, HS=64, DHR=32, DKV=256, Dm=D):
    p = Prog("l3attB")
    nc = p.nc
    HD = HS + DHR
    kch = DKV // 128
    nqg = SEQL // 512
    ntl = SEQL // 128
    qin = [p.inp(f"qhT_{j}", [HD, SEQL], BF16) for j in range(HL)]
    ckv_in = p.inp("ckvT", [DKV, SEQL], BF16)
    kr_in = p.inp("krT", [DHR, SEQL], BF16)
    wuk_in = p.inp("W_UK4", [DKV, HL * HS])
    wuv_in = p.inp("W_UV4", [DKV, HL * HS])
    ctxT_s = p.out("ctxT_s", [HL * HS, SEQL], BF16)

    ckv = p.sb("ckv", [128, kch * SEQL], BF16)
    wuk = p.sb("wuk", [128, kch * HL * HS], BF16)
    wuv = p.sb("wuv", [128, kch * HL * HS], BF16)
    qhT = p.sb("qhT", [HD, SEQL], BF16)
    khT = p.sb("khT", [HD, SEQL], BF16)
    vtm = p.sb("vtm", [128, ntl * (HS + 1)], BF16)
    expb = p.sb("expb", [128, 4 * 512], BF16)
    ctf = p.sb("ctf", [65, 512], F32)
    rb = p.sb("rb", [65, 512], BF16)
    onesT = p.sb("onesT", [65, 64], BF16)
    cto = p.sb("cto", [64, SEQL], BF16)

    pp = psum_pool(p)
    consts = emit_consts(p)

    lw = p.sem("lw")
    iv = p.sem("iv")
    with nc.Block() as blk:
        @blk.gpsimd
        def _(g):
            for c in range(kch):
                lw.dma(g.dma_start(out=ckv[:, c * SEQL:(c + 1) * SEQL],
                                   in_=ckv_in[c * 128:(c + 1) * 128, :]))
                lw.dma(g.dma_start(out=wuk[:, c * HL * HS:(c + 1) * HL * HS],
                                   in_=wuk_in[c * 128:(c + 1) * 128, :]))
                lw.dma(g.dma_start(out=wuv[:, c * HL * HS:(c + 1) * HL * HS],
                                   in_=wuv_in[c * 128:(c + 1) * 128, :]))

        @blk.vector
        def _(v):
            iv.eng(v.memset(onesT[:], 1.0))
            v.wait_ge(iv.h, iv.v)
            iv.eng(v.memset(bass.AP(vtm, HS, [[ntl * (HS + 1), 128], [HS + 1, ntl], [1, 1]]), 1.0))

    hq = Sem2(nc, "hq")
    hk = Sem2(nc, "hk")
    tk = p.sem("tk")
    ak = p.sem("ak")
    tv = p.sem("tv")
    av = p.sem("av")
    tS = p.sem("tS")
    aS = p.sem("aS")
    gS = p.sem("gS")
    tc = p.sem("tc")
    ac = p.sem("ac")
    vn = p.sem("vn")
    tr = p.sem("tr")
    vno = p.sem("vno")
    co = Sem2(nc, "co")

    def ctx_mm_count(qg_end):
        return sum((qg + 1) * 4 for qg in range(qg_end))

    for h in range(HL):
        with nc.Block() as blk:
            tk_b, ak_b, tv_b, av_b = tk.v, ak.v, tv.v, av.v
            tS_b, aS_b, gS_b, tc_b, ac_b = tS.v, aS.v, gS.v, tc.v, ac.v
            vn_b, tr_b, vno_b = vn.v, tr.v, vno.v

            @blk.sync
            def _(sync):
                hq.dma(sync.dma_start(out=qhT[:], in_=qin[h][:]), 0)
                hk.dma(sync.dma_start(out=khT[HS:HD, :], in_=kr_in[:]), 1)

            @blk.tensor
            def _(t):
                if h == 0:
                    t.wait_ge(lw.h, lw.v)
                    t.wait_ge(iv.h, 2)
                for tg in range(nqg):
                    if tg >= 2:
                        t.wait_ge(ak.h, ak_b + tg - 1)
                    pairs = [(wuk[:, c * HL * HS + h * HS: c * HL * HS + (h + 1) * HS],
                              ckv[:, c * SEQL + tg * 512: c * SEQL + (tg + 1) * 512])
                             for c in range(kch)]
                    tk.eng(mm_acc(t, pp["a"][tg % 2][0:HS, 0:512], pairs))
                for tl in range(ntl):
                    if tl >= 2:
                        t.wait_ge(av.h, av_b + tl - 1)
                    pairs = [(ckv[:, c * SEQL + tl * 128: c * SEQL + (tl + 1) * 128],
                              wuv[:, c * HL * HS + h * HS: c * HL * HS + (h + 1) * HS])
                             for c in range(kch)]
                    tv.eng(mm_acc(t, pp["a"][2 + tl % 2][:, 0:HS], pairs))
                hq.wait(t, 0)
                hk.wait(t, 1)
                t.wait_ge(ak.h, ak_b + nqg)
                t.wait_ge(av.h, av_b + ntl)
                i = 0
                for qg in range(nqg):
                    nkt = (qg + 1) * 4
                    if qg >= 1:
                        t.wait_ge(ac.h, ac_b + qg)
                    for g0 in range(0, nkt, 4):
                        for kt in range(g0, g0 + 4):
                            j = i + kt
                            if j >= 4:
                                t.wait_ge(aS.h, aS_b + j - 3)
                            tS.eng(t.matmul(out=pp["a"][j % 4][:, 0:512],
                                            lhsT=khT[:, kt * 128:(kt + 1) * 128],
                                            rhs=qhT[:, qg * 512:(qg + 1) * 512],
                                            start=True, stop=True))
                        for kt in range(g0, g0 + 4):
                            j = i + kt
                            is_diag = kt >= qg * 4
                            if is_diag:
                                t.wait_ge(gS.h, gS_b + 4 * qg + (kt - 4 * qg) + 1)
                            else:
                                t.wait_ge(aS.h, aS_b + j + 1)
                            tc.eng(t.matmul(out=pp["big"][0:65, 0:512],
                                            lhsT=vtm[:, kt * (HS + 1):(kt + 1) * (HS + 1)],
                                            rhs=expb[:, (j % 4) * 512:(j % 4 + 1) * 512],
                                            start=(kt == 0), stop=(kt == nkt - 1)))
                    i += nkt
                    t.wait_ge(vn.h, vn_b + 2 * qg + 2)
                    if qg >= 1:
                        t.wait_ge(vno.h, vno_b + qg)
                    tr.eng(t.matmul(out=pp["big"][0:64, 512:1024],
                                    lhsT=onesT[64:65, 0:64],
                                    rhs=rb[64:65, 0:512],
                                    start=True, stop=True))

            @blk.scalar
            def _(s):
                for tg in range(nqg):
                    s.wait_ge(tk.h, tk_b + tg + 1)
                    ak.eng(s.activation(out=khT[0:HS, tg * 512:(tg + 1) * 512],
                                        in_=pp["a"][tg % 2][0:HS, 0:512],
                                        func=AF.Identity))
                for tl in range(ntl):
                    s.wait_ge(tv.h, tv_b + tl + 1)
                    av.eng(s.activation(out=vtm[:, tl * (HS + 1):tl * (HS + 1) + HS],
                                        in_=pp["a"][2 + tl % 2][:, 0:HS],
                                        func=AF.Identity))
                i = 0
                for qg in range(nqg):
                    nkt = (qg + 1) * 4
                    for kt in range(nkt):
                        j = i + kt
                        s.wait_ge(tS.h, tS_b + j + 1)
                        if j >= 4:
                            s.wait_ge(tc.h, tc_b + j - 3)
                        aS.eng(s.activation(out=expb[:, (j % 4) * 512:(j % 4 + 1) * 512],
                                            in_=pp["a"][j % 4][:, 0:512],
                                            func=AF.Exp, scale=float(1.0 / np.sqrt(HD))))
                    i += nkt

            @blk.gpsimd
            def _(g):
                i = 0
                for qg in range(nqg):
                    nkt = (qg + 1) * 4
                    for kt in range(nkt):
                        j = i + kt
                        if kt >= qg * 4:
                            g.wait_ge(aS.h, aS_b + j + 1)
                            gS.eng(g.affine_select(
                                out=expb[:, (j % 4) * 512:(j % 4 + 1) * 512],
                                in_=expb[:, (j % 4) * 512:(j % 4 + 1) * 512],
                                pattern=[[1, 512]], compare_op=ALU.is_ge,
                                fill=0.0, base=qg * 512 - kt * 128,
                                channel_multiplier=-1))
                    i += nkt

            @blk.vector
            def _(v):
                if h >= 1:
                    co.wait(v, h - 1)
                for qg in range(nqg):
                    v.wait_ge(tc.h, tc_b + ctx_mm_count(qg + 1))
                    if qg >= 1:
                        v.wait_ge(vno.h, vno_b + qg)
                    v.wait_ge(vn.h, vn.v)
                    ac.eng(v.tensor_copy(out=ctf[:], in_=pp["big"][0:65, 0:512]))
                    if qg >= 1:
                        v.wait_ge(tr.h, tr_b + qg)
                    v.wait_ge(ac.h, ac.v)
                    vn.eng(v.reciprocal(ctf[64:65, :], ctf[64:65, :]))
                    v.wait_ge(vn.h, vn.v)
                    vn.eng(v.tensor_copy(out=rb[64:65, :], in_=ctf[64:65, :]))
                    v.wait_ge(tr.h, tr_b + qg + 1)
                    vno.eng(v.tensor_tensor(out=cto[:, qg * 512:(qg + 1) * 512],
                                            in0=ctf[0:64, :],
                                            in1=pp["big"][0:64, 512:1024],
                                            op=ALU.mult))

            @blk.sync
            def _(sync):
                sync.wait_ge(vno.h, vno_b + nqg)
                co.dma(sync.dma_start(out=ctxT_s[h * HS:(h + 1) * HS, :],
                                      in_=cto[:]), h)
    with nc.Block() as blk:
        @blk.sync
        def _(sync):
            sync.wait_ge(co.s[0].h, co.s[0].v)
            sync.wait_ge(co.s[1].h, co.s[1].v)
    return p


def build_attCT(Rr=R, Dm=D, CW=D, resid_scale=1.0, with_bias=True):
    p = Prog("attCT")
    nc = p.nc
    nt = Rr // 128
    cch = CW // 128
    ndg = Dm // 512
    ntg = nt // 4
    x_in = p.inp("x", [Rr, Dm])
    ctxT_in = p.inp("ctxT_r", [CW, Rr], BF16)
    wo_in = p.inp("Wo", [CW, Dm])
    bo_in = p.inp("bo", [1, Dm])
    xout = p.out("xout", [Rr, Dm])

    wo = p.sb("wo_sb", [128, cch * Dm], BF16)
    bo = p.sb("bo_sb", [1, Dm], BF16)
    bof = p.sb("bof_sb", [1, Dm], F32)
    ctT = p.sb("ctT", [128, 2 * cch * 512], BF16)
    xfin = p.sb("xfin", [128, 2 * Dm], F32)
    xob = p.sb("xob", [128, 2 * Dm], F32)

    pp = psum_pool(p)
    consts = emit_consts(p)

    lw = p.sem("lw")
    lb = p.sem("lb")
    bv_ = p.sem("bv")
    with nc.Block() as blk:
        @blk.gpsimd
        def _(g):
            for c in range(cch):
                lw.dma(g.dma_start(out=wo[:, c * Dm:(c + 1) * Dm],
                                   in_=wo_in[c * 128:(c + 1) * 128, :]))

        @blk.sync
        def _(sync):
            lb.dma(sync.dma_start(out=bof[:], in_=bo_in[:]))

        @blk.vector
        def _(v):
            v.wait_ge(lb.h, lb.v)
            bv_.eng(v.tensor_copy(out=bo[:], in_=bof[:]))

    lc = Sem2(nc, "lc")
    lx2 = Sem2(nc, "lx2")
    tm = p.sem("tm")
    vr = p.sem("vr")
    oo = Sem2(nc, "oo")
    with nc.Block() as blk:
        @blk.sync
        def _(sync):
            for tg in range(ntg):
                if tg >= 2:
                    sync.wait_ge(tm.h, (tg - 1) * 4 * ndg)
                for c in range(cch):
                    lc.dma(sync.dma_start(
                        out=ctT[:, ((tg % 2) * cch + c) * 512:((tg % 2) * cch + c + 1) * 512],
                        in_=ctxT_in[c * 128:(c + 1) * 128, tg * 512:(tg + 1) * 512]), tg)
                lc.mark(tg)
                for ts in range(4):
                    ti = tg * 4 + ts
                    if ti >= 2:
                        sync.wait_ge(vr.h, (ti - 1) * ndg)
                    lx2.dma(sync.dma_start(out=xfin[:, (ti % 2) * Dm:(ti % 2 + 1) * Dm],
                                           in_=x_in[ti * 128:(ti + 1) * 128, :]), ti)

        @blk.tensor
        def _(t):
            t.wait_ge(lw.h, lw.v)
            t.wait_ge(bv_.h, 1)
            for tg in range(ntg):
                lc.wait(t, tg)
                for ts in range(4):
                    ti = tg * 4 + ts
                    for dg in range(ndg):
                        im = ti * ndg + dg
                        if im >= 2:
                            t.wait_ge(vr.h, im - 1)
                        pairs = [(ctT[:, ((tg % 2) * cch + c) * 512 + ts * 128:
                                       ((tg % 2) * cch + c) * 512 + (ts + 1) * 128],
                                  wo[:, c * Dm + dg * 512: c * Dm + (dg + 1) * 512])
                                 for c in range(cch)]
                        tm.eng(mm_acc(t, pp["a"][im % 2][:], pairs,
                                      rank1=(consts["ones"][0:1, 0:128],
                                             bo[0:1, dg * 512:(dg + 1) * 512])))

        @blk.vector
        def _(v):
            for ti in range(nt):
                lx2.wait(v, ti)
                if ti >= 2:
                    oo.wait(v, ti - 2)
                for dg in range(ndg):
                    im = ti * ndg + dg
                    v.wait_ge(tm.h, im + 1)
                    if resid_scale == 1.0:
                        vr.eng(v.tensor_add(
                            xob[:, (ti % 2) * Dm + dg * 512:(ti % 2) * Dm + (dg + 1) * 512],
                            pp["a"][im % 2][:],
                            xfin[:, (ti % 2) * Dm + dg * 512:(ti % 2) * Dm + (dg + 1) * 512]))
                    else:
                        vr.eng(v.scalar_tensor_tensor(
                            out=xob[:, (ti % 2) * Dm + dg * 512:(ti % 2) * Dm + (dg + 1) * 512],
                            in0=pp["a"][im % 2][:], scalar=resid_scale,
                            in1=xfin[:, (ti % 2) * Dm + dg * 512:(ti % 2) * Dm + (dg + 1) * 512],
                            op0=ALU.mult, op1=ALU.add))

        @blk.gpsimd
        def _(g):
            for ti in range(nt):
                g.wait_ge(vr.h, (ti + 1) * ndg)
                oo.dma(g.dma_start(out=xout[ti * 128:(ti + 1) * 128, :],
                                   in_=xob[:, (ti % 2) * Dm:(ti % 2 + 1) * Dm]), ti)
            g.wait_ge(oo.s[0].h, oo.s[0].v)
            g.wait_ge(oo.s[1].h, oo.s[1].v)
    return p


def build_l1attA(Rr=R, Dm=D, H=16):
    p = Prog("l1attA")
    nc = p.nc
    nt = Rr // 128
    dch = Dm // 128
    NTK = nt * 128
    x_in = p.inp("x", [Rr, Dm])
    wa_in = p.inp("waT", [Dm, H])
    xnT_s = p.out("xnT_s", [Dm, Rr], BF16)
    zT_s = p.out("zT_s", [H, Rr])
    xn = p.sb("xn", [128, nt * Dm], BF16)
    xnT = p.sb("xnT", [128, nt * Dm], BF16)
    wa = p.sb("wa", [128, dch * H], BF16)
    zb = p.sb("zb", [H, NTK], F32)
    pp = psum_pool(p)
    consts = emit_consts(p)
    stage_ln_dram(p, x_in, xn, nt, dch=dch, name="aln", pp=pp)
    stage_transpose(p, consts, xn, xnT, nt, dch=dch, pp=pp)
    lw = p.sem("lw")
    tz = p.sem("tz")
    az = p.sem("az")
    so = Sem2(nc, "so")
    with nc.Block() as blk:
        @blk.gpsimd
        def _(g):
            for d in range(dch):
                lw.dma(g.dma_start(out=wa[:, d * H:(d + 1) * H],
                                   in_=wa_in[d * 128:(d + 1) * 128, :]))

        @blk.tensor
        def _(t):
            t.wait_ge(lw.h, lw.v)
            for tg in range(nt // 4):
                if tg >= 2:
                    t.wait_ge(az.h, tg - 1)
                pairs = [(wa[:, d * H:(d + 1) * H],
                          xnT[:, d * NTK + tg * 512:d * NTK + (tg + 1) * 512])
                         for d in range(dch)]
                tz.eng(mm_acc(t, pp["a"][tg % 2][0:H, 0:512], pairs))

        @blk.scalar
        def _(s):
            for tg in range(nt // 4):
                s.wait_ge(tz.h, tg + 1)
                az.eng(s.activation(out=zb[:, tg * 512:(tg + 1) * 512],
                                    in_=pp["a"][tg % 2][0:H, 0:512],
                                    func=AF.Identity, scale=0.125))

        @blk.sync
        def _(sync):
            for d in range(dch):
                so.dma(sync.dma_start(out=xnT_s[d * 128:(d + 1) * 128, :],
                                      in_=xnT[:, d * NTK:(d + 1) * NTK]), d)
            sync.wait_ge(az.h, nt // 4)
            so.dma(sync.dma_start(out=zT_s[:], in_=zb[:]), dch)
            sync.wait_ge(so.s[0].h, so.s[0].v)
            sync.wait_ge(so.s[1].h, so.s[1].v)
    return p


def build_l1attB(SEQL=SEQ, Dm=D, HL=4, HD=64, H=16):
    p = Prog("l1attB")
    nc = p.nc
    HC = HL * HD
    dch = Dm // 128
    cch = HC // 128
    assert cch == 2
    ntl = SEQL // 128
    HSEQ = SEQL // 2
    xnT_b = p.inp("xnT_b", [Dm, SEQL], BF16)
    z_in = p.inp("zT4", [HL, SEQL])
    wv_in = p.inp("wvr", [HC, Dm])
    ctxT_s = p.out("ctxT_s", [HC, SEQL], BF16)

    wvbf_d = p.dram("wvbf_d", [HC, Dm], BF16)
    zbuf_d = [p.dram(f"zbuf_d{j}", [1, 2 * SEQL], BF16) for j in range(HL)]

    wvT = p.sb("wvT", [128, dch * HC], BF16)
    xch = p.sb("xch", [128, 2 * 512 * dch], BF16)
    vT = p.sb("vT", [128, cch * SEQL], BF16)
    vtm = p.sb("vtm", [128, ntl * HC], BF16)
    zrow = p.sb("zrow", [1, SEQL], F32)
    zst = p.sb("zst", [1, 8], F32)
    zbf = p.sb("zbf", [1, SEQL], BF16)
    zzero = p.sb("zzero", [1, SEQL], BF16)
    strips = p.sb("strips", [128, 2 * 4096], BF16)
    idxb = p.sb("idxb", [128, 2 * ntl], U32)
    otb = p.sb("otb", [64, SEQL], BF16)

    pa = [p.ps(f"cp{i}", [128, 512], F32) for i in range(8)]
    pabf = [pa[i].bitcast(BF16) for i in range(8)]

    consts = emit_consts(p)

    wg = p.sem("wg")
    wt = p.sem("wt")
    stage_pretranspose_weight(p, wv_in[:], wvbf_d, wvT, HC, Dm, wg, wt)

    lx = Sem2(nc, "lx")
    tv = p.sem("tv")
    av = p.sem("av")
    te = p.sem("te")
    ve_ = p.sem("ve")
    with nc.Block() as blk:
        @blk.sync
        def _(sync):
            for tg in range(SEQL // 512):
                if tg >= 2:
                    sync.wait_ge(tv.h, (tg - 1) * cch)
                for d in range(dch):
                    lx.dma(sync.dma_start(
                        out=xch[:, ((tg % 2) * dch + d) * 512:((tg % 2) * dch + d + 1) * 512],
                        in_=xnT_b[d * 128:(d + 1) * 128, tg * 512:(tg + 1) * 512]), tg)
                lx.mark(tg)

        @blk.tensor
        def _(t):
            t.wait_ge(wt.h, wt.v)
            for tg in range(SEQL // 512):
                lx.wait(t, tg)
                for cc in range(cch):
                    im = tg * cch + cc
                    if im >= 2:
                        t.wait_ge(av.h, im - 1)
                    pairs = [(wvT[:, d * HC + cc * 128: d * HC + (cc + 1) * 128],
                              xch[:, ((tg % 2) * dch + d) * 512:((tg % 2) * dch + d + 1) * 512])
                             for d in range(dch)]
                    tv.eng(mm_acc(t, pa[im % 2][:], pairs))
                t.wait_ge(av.h, (tg + 1) * cch)
                if tg >= 1:
                    t.wait_ge(ve_.h, 8 * tg)
                for s4 in range(4):
                    te.eng(t.transpose(
                        out=pabf[2][:, s4 * 128:(s4 + 1) * 128],
                        in_=vT[:, tg * 512 + s4 * 128: tg * 512 + (s4 + 1) * 128],
                        identity=consts["ident"][:]))
                for s4 in range(4):
                    te.eng(t.transpose(
                        out=pabf[3][:, s4 * 128:(s4 + 1) * 128],
                        in_=vT[:, SEQL + tg * 512 + s4 * 128: SEQL + tg * 512 + (s4 + 1) * 128],
                        identity=consts["ident"][:]))

        @blk.scalar
        def _(s):
            for tg in range(SEQL // 512):
                for cc in range(cch):
                    im = tg * cch + cc
                    s.wait_ge(tv.h, im + 1)
                    av.eng(s.activation(out=vT[:, cc * SEQL + tg * 512: cc * SEQL + (tg + 1) * 512],
                                        in_=pa[im % 2][:], func=AF.Identity))

        @blk.vector
        def _(v):
            for tg in range(SEQL // 512):
                v.wait_ge(te.h, 8 * tg + 4)
                for s4 in range(4):
                    tl = tg * 4 + s4
                    ve_.eng(v.tensor_copy(out=vtm[:, tl * HC: tl * HC + 128],
                                          in_=pabf[2][:, s4 * 128:(s4 + 1) * 128]))
                v.wait_ge(te.h, 8 * tg + 8)
                for s4 in range(4):
                    tl = tg * 4 + s4
                    ve_.eng(v.tensor_copy(out=vtm[:, tl * HC + 128: tl * HC + 256],
                                          in_=pabf[3][:, s4 * 128:(s4 + 1) * 128]))

    lz = Sem2(nc, "lz")
    vz = p.sem("vz")
    sz = p.sem("sz")
    zo = Sem2(nc, "zo")
    with nc.Block() as blk:
        @blk.vector
        def _(v):
            vz.eng(v.memset(zzero[:], 0.0))

        @blk.sync
        def _(sync):
            sync.wait_ge(vz.h, 1)
            for j in range(HL):
                zo.dma(sync.dma_start(out=zbuf_d[j][0:1, 0:SEQL], in_=zzero[:]), j)
        for j in range(HL):
            @blk.sync
            def _(sync, j=j):
                if j >= 1:
                    sync.wait_ge(sz.h, 2 * j)
                lz.dma(sync.dma_start(out=zrow[:], in_=z_in[j:j + 1, :]), j)

            @blk.vector
            def _(v, j=j):
                lz.wait(v, j)
                v.wait_ge(vz.h, vz.v)
                vz.eng(v.tensor_reduce(out=zst[:, 0:1], in_=zrow[:], axis=AX.X,
                                       op=ALU.max))
                v.wait_ge(vz.h, vz.v)
                vz.eng(v.tensor_scalar_mul(zst[:, 1:2], zst[:, 0:1], -1.0))
                v.wait_ge(sz.h, 2 * j + 1)
                vz.eng(v.reciprocal(zst[:, 3:4], zst[:, 2:3]))
                v.wait_ge(vz.h, vz.v)
                vz.eng(v.tensor_scalar_mul(zrow[:], zrow[:], zst[:, 3:4]))

            @blk.scalar
            def _(s, j=j):
                s.wait_ge(vz.h, 4 * j + 3)
                sz.eng(s.activation(out=zrow[:], in_=zrow[:], func=AF.Exp,
                                    bias=zst[:, 1:2], accum_out=zst[:, 2:3]))
                s.wait_ge(vz.h, 4 * j + 5)
                if j >= 1:
                    zo.wait(s, HL + j - 1)
                sz.eng(s.activation(out=zbf[:], in_=zrow[:], func=AF.Identity))

            @blk.sync
            def _(sync, j=j):
                sync.wait_ge(sz.h, 2 * j + 2)
                zo.dma(sync.dma_start(out=zbuf_d[j][0:1, SEQL:2 * SEQL],
                                      in_=zbf[:]), HL + j)

        @blk.sync
        def _(sync):
            sync.wait_ge(zo.s[0].h, zo.s[0].v)
            sync.wait_ge(zo.s[1].h, zo.s[1].v)

    gi = p.sem("gi")
    with nc.Block() as blk:
        @blk.gpsimd
        def _(g):
            for ps in range((SEQL // 512 + 7) // 8):
                gi.eng(g.iota(idxb[:, ps * ntl:(ps + 1) * ntl],
                              [[-128, ntl]], base=SEQL + ps * 8 * 512,
                              channel_multiplier=-1,
                              allow_small_or_imprecise_dtypes=True))
                g.wait_ge(gi.h, gi.v)

    def mm_upto(J, ig0, nJ):
        return sum((ig0 + 8) - max(ig0, Jp // 4) for Jp in range(J + 1))

    sd = Sem2(nc, "sd")
    tcv = p.sem("tcv")
    acv = p.sem("acv")
    ob = Sem2(nc, "ob")
    nig = SEQL // 512
    npass = (nig + 7) // 8
    from concourse.bass_core_rust import IndirectOffsetOnAxis as IOA
    for j in range(HL):
        for ps in range(npass):
            ig0 = ps * 8
            ig1 = min(nig, ig0 + 8)
            i0 = ig0 * 512
            pw = (ig1 - ig0) * 512
            nJ = min(ntl, ig1 * 4)

            def mm_upto2(J, ig0=ig0, ig1=ig1):
                return sum(ig1 - max(ig0, Jp // 4) for Jp in range(J + 1))

            with nc.Block() as blk:
                tcv_b = tcv.v

                @blk.gpsimd
                def _(g, j=j, ps=ps, nJ=nJ, ig0=ig0, pw=pw, mm_upto2=mm_upto2):
                    for J in range(nJ):
                        if J >= 2:
                            g.wait_ge(tcv.h, tcv_b + mm_upto2(J - 2))
                        sd.dma(g.indirect_dma_start(
                            out=strips[:, (J % 2) * 4096:(J % 2) * 4096 + pw],
                            out_offset=None,
                            in_=bass.AP(zbuf_d[j].tensor, 0,
                                        [[1, 2 * SEQL], [1, 1]]),
                            in_offset=IOA(ap=idxb[:, ps * ntl + J:ps * ntl + J + 1],
                                          axis=0)), J)

                @blk.tensor
                def _(t, j=j, ps=ps, nJ=nJ, ig0=ig0, ig1=ig1, i0=i0):
                    for J in range(nJ):
                        sd.wait(t, J)
                        for ig in range(max(ig0, J // 4), ig1):
                            icol = ig - ig0
                            w0 = ig * 512 - i0
                            last_J = min(nJ, (ig + 1) * 4) - 1
                            tcv.eng(t.matmul(
                                out=pa[icol][0:HD, 0:512],
                                lhsT=vtm[:, J * HC + j * HD: J * HC + (j + 1) * HD],
                                rhs=strips[:, (J % 2) * 4096 + w0:(J % 2) * 4096 + w0 + 512],
                                start=(J == 0), stop=(J == last_J)))

                @blk.scalar
                def _(s, j=j, ps=ps, ig0=ig0, ig1=ig1):
                    s.wait_ge(tcv.h, tcv.v)
                    if ps == 0 and j >= 1:
                        ob.wait(s, j - 1)
                    for icol in range(ig1 - ig0):
                        acv.eng(s.activation(
                            out=otb[0:HD, (ig0 + icol) * 512:(ig0 + icol + 1) * 512],
                            in_=pa[icol][0:HD, 0:512], func=AF.Identity))

        with nc.Block() as blk:
            @blk.sync
            def _(sync, j=j):
                sync.wait_ge(acv.h, acv.v)
                ob.dma(sync.dma_start(out=ctxT_s[j * HD:(j + 1) * HD, :],
                                      in_=otb[:]), j)
    with nc.Block() as blk:
        @blk.sync
        def _(sync):
            sync.wait_ge(ob.s[0].h, ob.s[0].v)
            sync.wait_ge(ob.s[1].h, ob.s[1].v)
    return p


def build_lff(Rr=R, Dm=D, HID=2730, final=False):
    p = Prog("lff")
    nc = p.nc
    nt = Rr // 128
    dch = Dm // 128
    F2 = 2 * HID
    nfg = (F2 + 511) // 512
    hch = (HID + 127) // 128
    HIDP = hch * 128
    ndg = Dm // 512

    zin = p.inp("zin", [Rr, Dm])
    zhalo = p.inp("zhalo", [1, Dm])
    w1_in = p.inp("w1", [Dm, F2], BF16 if pre else F32)
    b1_in = p.inp("b1", [1, F2])
    w2_in = p.inp("w2", [HID, Dm], BF16 if pre else F32)
    b2_in = p.inp("b2", [1, Dm])
    lng_in = p.inp("lng", [1, Dm])
    lnb_in = p.inp("lnb", [1, Dm])
    if final:
        xr_in = p.inp("xr", [Rr, Dm])
    zout = p.out("zout", [Rr, Dm])

    w1sb = p.sb("w1sb", [128, dch * F2], BF16)
    w2sb = p.sb("w2sb", [128, hch * Dm], BF16)
    b1e = p.sb("b1e", [1, F2], BF16)
    b1f = p.sb("b1f", [1, F2], F32)
    b2r = p.sb("b2r", [1, Dm], BF16)
    lgc = p.sb("lgc", [128, dch], F32)
    lbc = p.sb("lbc", [128, dch], F32)
    lgb = p.sb("lgb", [128, dch], BF16)
    lbb = p.sb("lbb", [128, dch], BF16)
    ht = p.sb("ht", [128, Dm], F32)
    hnb = p.sb("hnb", [128, Dm], BF16)
    hnT = p.sb("hnT", [128, Dm], BF16)
    agb = p.sb("agb", [128, F2], BF16)
    h2 = p.sb("h2", [128, HIDP], BF16)
    h2T = p.sb("h2T", [128, HIDP], BF16)
    zfin = p.sb("zfin", [128, Dm], F32)
    xfin = p.sb("xfin", [128, Dm], F32) if final else None
    xob = p.sb("xob", [128, Dm], F32)
    stat = p.sb("stat", [128, 8], F32)

    pp = psum_pool(p)
    consts = emit_consts(p)

    lw = p.sem("lw")
    ls = p.sem("ls")
    vv0 = p.sem("vv0")
    tb = p.sem("tb")
    ab = p.sem("ab")
    lw2 = p.sem("lw2")
    with nc.Block() as blk:
        if pre:
            @blk.scalar
            def _(sc):
                for d in range(dch):
                    lw2.dma(sc.dma_start(out=w1sb[:, d * F2:(d + 1) * F2],
                                         in_=w1_in[d * 128:(d + 1) * 128, :]))

            @blk.gpsimd
            def _(g):
                for c in range(hch):
                    r0 = c * 128
                    r1 = min(HID, r0 + 128)
                    if r1 - r0 < 128:
                        g.wait_ge(vv0.h, 1)
                    lw.dma(g.dma_start(out=w2sb[0:r1 - r0, c * Dm:(c + 1) * Dm],
                                       in_=w2_in[r0:r1, :]))
                lw.dma(g.dma_start(out=b1e[:], in_=b1_in[:]))
                lw.dma(g.dma_start(out=b2r[:], in_=b2_in[:]))
        else:
            @blk.gpsimd
            def _(g):
                for d in range(dch):
                    lw.dma(g.dma_start(out=w1sb[:, d * F2:(d + 1) * F2],
                                       in_=w1_in[d * 128:(d + 1) * 128, :]))
                for c in range(hch):
                    r0 = c * 128
                    r1 = min(HID, r0 + 128)
                    if r1 - r0 < 128:
                        g.wait_ge(vv0.h, 1)
                    lw.dma(g.dma_start(out=w2sb[0:r1 - r0, c * Dm:(c + 1) * Dm],
                                       in_=w2_in[r0:r1, :]))
                lw.dma(g.dma_start(out=b1e[:], in_=b1_in[:]))
                lw.dma(g.dma_start(out=b2r[:], in_=b2_in[:]))

        @blk.sync
        def _(sync):
            for d in range(dch):
                ls.dma(sync.dma_start(out=lgc[:, d:d + 1],
                                      in_=lng_in[0:1, d * 128:(d + 1) * 128]))
                ls.dma(sync.dma_start(out=lbc[:, d:d + 1],
                                      in_=lnb_in[0:1, d * 128:(d + 1) * 128]))

        @blk.vector
        def _(v):
            if HIDP > HID:
                vv0.eng(v.memset(w2sb[:, (hch - 1) * Dm:hch * Dm], 0.0))
            v.wait_ge(ls.h, ls.v)
            vv0.eng(v.tensor_copy(out=lgb[:], in_=lgc[:]))
            v.wait_ge(vv0.h, vv0.v)
            vv0.eng(v.tensor_copy(out=lbb[:], in_=lbc[:]))
            v.wait_ge(vv0.h, vv0.v)
            vv0.eng(v.memset(h2[:], 0.0))
            v.wait_ge(vv0.h, vv0.v)
            vv0.eng(v.memset(h2T[:], 0.0))

    vv0_pre = vv0.v
    with nc.Block() as blk:
        @blk.tensor
        def _(t):
            t.wait_ge(lw.h, lw.v)
            t.wait_ge(lw2.h, lw2.v)
            t.wait_ge(vv0.h, vv0_pre)
            for fg in range(nfg):
                fw = min(512, F2 - fg * 512)
                if fg >= 2:
                    t.wait_ge(ab.h, fg - 1)
                pairs = [(lbb[:, d:d + 1],
                          w1sb[:, d * F2 + fg * 512: d * F2 + fg * 512 + fw])
                         for d in range(dch)]
                tb.eng(mm_acc(t, pp["a"][fg % 2][0:1, 0:fw], pairs,
                              rank1=(consts["ones"][0:1, 0:1],
                                     b1e[0:1, fg * 512:fg * 512 + fw])))

        @blk.scalar
        def _(s):
            for fg in range(nfg):
                fw = min(512, F2 - fg * 512)
                s.wait_ge(tb.h, fg + 1)
                if fg >= 1:
                    s.wait_ge(ab.h, ab.v)
                ab.eng(s.activation(out=b1f[0:1, fg * 512:fg * 512 + fw],
                                    in_=pp["a"][fg % 2][0:1, 0:fw],
                                    func=AF.Identity))

        @blk.vector
        def _(v):
            v.wait_ge(ab.h, nfg)
            vv0.eng(v.tensor_copy(out=b1e[:], in_=b1f[:]))
            for d in range(dch):
                v.wait_ge(vv0.h, vv0.v)
                vv0.eng(v.tensor_scalar_mul(w1sb[:, d * F2:(d + 1) * F2],
                                            w1sb[:, d * F2:(d + 1) * F2],
                                            lgc[:, d:d + 1]))

    lz = Sem2(nc, "lz")
    lf = Sem2(nc, "lf")
    lxr = Sem2(nc, "lxr")
    cs = p.sem("cs")
    cv = p.sem("cv")
    tt = p.sem("tt")
    vh = p.sem("vh")
    tm = p.sem("tm")
    aq = p.sem("aq")
    vg = p.sem("vg")
    vo = p.sem("vo")
    oo = Sem2(nc, "oo")
    for ti in range(nt):
        r0 = ti * 128
        with nc.Block() as blk:
            cs_b, cv_b, tt_b, vh_b = cs.v, cv.v, tt.v, vh.v
            tm_b, aq_b, vg_b, vo_b = tm.v, aq.v, vg.v, vo.v

            @blk.sync
            def _(sync, ti=ti, r0=r0):
                if ti == 0:
                    lz.dma(sync.dma_start(out=ht[0:128, 0:512],
                                          in_=zin[0:128, 0:512]), 0)
                    lz.dma(sync.dma_start(out=ht[0:1, 512:Dm],
                                          in_=zhalo[0:1, 512:Dm]), 0)
                    lz.dma(sync.dma_start(out=ht[1:128, 512:Dm],
                                          in_=zin[0:127, 512:Dm]), 0)
                else:
                    lz.dma(sync.dma_start(out=ht[0:128, 0:512],
                                          in_=zin[r0:r0 + 128, 0:512]), 0)
                    lz.dma(sync.dma_start(out=ht[0:128, 512:Dm],
                                          in_=zin[r0 - 1:r0 + 127, 512:Dm]), 0)
                lz.mark(0)
                lf.dma(sync.dma_start(out=zfin[:], in_=zin[r0:r0 + 128, :]), 0)
                if final:
                    lxr.dma(sync.dma_start(out=xfin[:], in_=xr_in[r0:r0 + 128, :]), 0)

            @blk.scalar
            def _(s, ti=ti):
                lz.wait(s, 0)
                cs.eng(s.activation(out=pp["big"][:], in_=ht[:], func=AF.Identity,
                                    accum_out=stat[:, 0:1]))
                s.wait_ge(cs.h, cs.v)
                cs.eng(s.activation(out=pp["big"][:], in_=ht[:], func=AF.Square,
                                    accum_out=stat[:, 1:2]))
                s.wait_ge(cv.h, cv_b + 4)
                cs.eng(s.activation(out=stat[:, 6:7], in_=stat[:, 4:5], func=AF.Ln))
                s.wait_ge(cs.h, cs.v)
                cs.eng(s.activation(out=stat[:, 4:5], in_=stat[:, 6:7], func=AF.Exp,
                                    scale=-0.5))
                s.wait_ge(cv.h, cv_b + 5)
                cs.eng(s.activation(out=hnb[:], in_=ht[:], func=AF.Identity,
                                    scale=stat[:, 4:5], bias=stat[:, 5:6]))
                for fg in range(nfg):
                    fw = min(512, F2 - fg * 512)
                    s.wait_ge(tm.h, tm_b + fg + 1)
                    aq.eng(s.activation(out=agb[:, fg * 512:fg * 512 + fw],
                                        in_=pp["a"][fg % 2][:, 0:fw],
                                        func=AF.Identity))
                s.wait_ge(aq.h, aq_b + nfg)
                aq.eng(s.activation(out=h2[:, 0:HID], in_=agb[:, HID:F2],
                                    func=AF.Gelu))

            @blk.vector
            def _(v, ti=ti):
                v.wait_ge(cs.h, cs_b + 2)
                cv.eng(v.tensor_scalar_mul(stat[:, 2:3], stat[:, 0:1], 1.0 / Dm))
                v.wait_ge(cv.h, cv.v)
                cv.eng(v.scalar_tensor_tensor(out=stat[:, 4:5], in0=stat[:, 2:3],
                                              scalar=-1.0, in1=stat[:, 2:3],
                                              op0=ALU.mult, op1=ALU.mult))
                v.wait_ge(cv.h, cv.v)
                cv.eng(v.scalar_tensor_tensor(out=stat[:, 4:5], in0=stat[:, 1:2],
                                              scalar=1.0 / Dm, in1=stat[:, 4:5],
                                              op0=ALU.mult, op1=ALU.add))
                v.wait_ge(cv.h, cv.v)
                cv.eng(v.tensor_scalar_add(stat[:, 4:5], stat[:, 4:5], 1e-5))
                v.wait_ge(cs.h, cs_b + 4)
                cv.eng(v.scalar_tensor_tensor(out=stat[:, 5:6], in0=stat[:, 2:3],
                                              scalar=-1.0, in1=stat[:, 4:5],
                                              op0=ALU.mult, op1=ALU.mult))
                for g2 in range(2):
                    v.wait_ge(tt.h, tt_b + 4 * (g2 + 1))
                    vh.eng(v.tensor_copy(out=hnT[:, g2 * 512:(g2 + 1) * 512],
                                         in_=pp["bf"][g2 % 2][:, 0:512]))
                v.wait_ge(aq.h, aq_b + nfg + 1)
                vg.eng(v.tensor_mul(h2[:, 0:HID], agb[:, 0:HID], h2[:, 0:HID]))
                ngr = (hch + 3) // 4
                for g2 in range(ngr):
                    c0 = g2 * 4
                    c1 = min(hch, c0 + 4)
                    v.wait_ge(tt.h, tt_b + 8 + c1)
                    vh.eng(v.tensor_copy(out=h2T[:, c0 * 128:c1 * 128],
                                         in_=pp["bf"][g2 % 2][:, 0:(c1 - c0) * 128]))
                if ti >= 1:
                    oo.wait(v, ti - 1)
                lf.wait(v, 0)
                if final:
                    lxr.wait(v, 0)
                for dg in range(ndg):
                    v.wait_ge(tm.h, tm_b + nfg + dg + 1)
                    sl = slice(dg * 512, (dg + 1) * 512)
                    osl = sl
                    if final:
                        cv.eng(v.tensor_add(xob[:, osl], pp["a"][2 + dg % 2][:],
                                            zfin[:, sl]))
                        v.wait_ge(cv.h, cv.v)
                        vo.eng(v.scalar_tensor_tensor(
                            out=xob[:, osl], in0=xob[:, osl], scalar=0.5,
                            in1=xfin[:, sl], op0=ALU.mult, op1=ALU.add))
                    else:
                        vo.eng(v.tensor_add(xob[:, osl], pp["a"][2 + dg % 2][:],
                                            zfin[:, sl]))

            @blk.tensor
            def _(t, ti=ti):
                if ti == 0:
                    t.wait_ge(lw.h, lw.v)
                    t.wait_ge(vv0.h, vv0.v)
                t.wait_ge(cs.h, cs_b + 5)
                for g2 in range(2):
                    for j in range(4):
                        d = g2 * 4 + j
                        ins = t.transpose(out=pp["bf"][g2 % 2][:, j * 128:(j + 1) * 128],
                                          in_=hnb[:, d * 128:(d + 1) * 128],
                                          identity=consts["ident"][:])
                        tt.eng(ins)
                t.wait_ge(vh.h, vh_b + 2)
                for fg in range(nfg):
                    fw = min(512, F2 - fg * 512)
                    if fg >= 2:
                        t.wait_ge(aq.h, aq_b + fg - 1)
                    pairs = [(hnT[:, d * 128:(d + 1) * 128],
                              w1sb[:, d * F2 + fg * 512: d * F2 + fg * 512 + fw])
                             for d in range(dch)]
                    tm.eng(mm_acc(t, pp["a"][fg % 2][:, 0:fw], pairs,
                                  rank1=(consts["ones"][0:1, 0:128],
                                         b1e[0:1, fg * 512:fg * 512 + fw])))
                t.wait_ge(vg.h, vg_b + 1)
                ngr = (hch + 3) // 4
                for g2 in range(ngr):
                    c0 = g2 * 4
                    c1 = min(hch, c0 + 4)
                    if g2 >= 2:
                        t.wait_ge(vh.h, vh_b + 2 + g2 - 1)
                    for j in range(c1 - c0):
                        ins = t.transpose(out=pp["bf"][g2 % 2][:, j * 128:(j + 1) * 128],
                                          in_=h2[:, (c0 + j) * 128:(c0 + j + 1) * 128],
                                          identity=consts["ident"][:])
                        tt.eng(ins)
                t.wait_ge(vh.h, vh_b + 2 + ngr)
                for dg in range(ndg):
                    pairs = [(h2T[:, c * 128:(c + 1) * 128],
                              w2sb[:, c * Dm + dg * 512: c * Dm + (dg + 1) * 512])
                             for c in range(hch)]
                    tm.eng(mm_acc(t, pp["a"][2 + dg % 2][:], pairs,
                                  rank1=(consts["ones"][0:1, 0:128],
                                         b2r[0:1, dg * 512:(dg + 1) * 512])))

            @blk.gpsimd
            def _(g, ti=ti):
                g.wait_ge(vo.h, vo_b + ndg)
                oo.dma(g.dma_start(out=zout[ti * 128:(ti + 1) * 128, :],
                                   in_=xob[:]), ti)
    with nc.Block() as blk:
        @blk.gpsimd
        def _(g):
            g.wait_ge(oo.s[0].h, oo.s[0].v)
            g.wait_ge(oo.s[1].h, oo.s[1].v)
    return p


def build_lfa(Rr=R, Dm=D, H=8, HD=64, WIN=512, with_ln=True):
    p = Prog("lfa")
    nc = p.nc
    HB = 640
    INNER = H * HD
    IC = INNER // 128
    nt = Rr // 128
    nkt = (HB + Rr) // 128 - 1
    dch = Dm // 128
    nw = Rr // WIN

    zin = p.inp("zin", [Rr, Dm])
    zhalo = p.inp("zhalo", [HB, Dm])
    wqkv_in = p.inp("wqkv", [Dm, 3 * INNER], BF16 if pre else F32)
    wo_in = p.inp("wo", [INNER, Dm], BF16 if pre else F32)
    qs_in = p.inp("qs4", [1, 4 * INNER])
    rq_in = p.inp("rq", [Rr, 2 * INNER], BF16)
    rk_in = p.inp("rk", [HB + Rr - 128, 2 * INNER], BF16)
    mk_in = p.inp("masks", [16 * 128, 512])
    zout = p.out("zout", [Rr, Dm])

    z0_d = p.dram("z0_d", [HB + Rr, Dm])

    wqkv = p.sb("wqkv_sb", [128, dch * 3 * INNER], BF16)
    wo = p.sb("wo_sb", [64, H * Dm], BF16)
    msk = p.sb("msk", [128, 16 * 512], F32)
    smf = p.sb("smf", [128, 2 * 512], F32)
    qs4 = p.sb("qs4_x", [1, 4 * INNER], F32)
    qs4b = p.sb("qs4b", [1, 4 * INNER], BF16)
    qsb = p.sb("qsb", [128, 4 * INNER], BF16)
    hb = p.sb("hb", [128, Dm], BF16)
    hT = p.sb("hT", [128, Dm], BF16)
    qbuf = p.sb("qbuf", [128, nt * INNER], BF16)
    kbuf = p.sb("kbuf", [128, nkt * INNER], BF16)
    vbuf = p.sb("vbuf", [128, nkt * (INNER + H)], BF16)
    rqt = p.sb("rqt", [128, 2 * INNER], BF16)
    rkt = p.sb("rkt", [128, 2 * INNER], BF16)
    sqf = p.sb("sqf", [128, 512], F32)
    ss8 = p.sb("ss8", [128, 16], F32)
    qhat = p.sb("qhat", [128, 512], BF16)
    rot = p.sb("rot", [128, 2 * 512], BF16)
    qhT = p.sb("qhT", [64, 512], BF16)
    khT = p.sb("khT", [64, 1024], BF16)
    expb = p.sb("expb", [128, 2 * 512], BF16)
    ctf = p.sb("ctf", [65, 512], F32)
    rb = p.sb("rb", [65, 512], BF16)
    onesT = p.sb("onesT", [65, 64], BF16)
    ctxTw = p.sb("ctxTw", [64, H * 512], BF16)
    zres = p.sb("zres", [128, Dm], F32)
    xob = p.sb("xob", [128, Dm], F32)
    stat = p.sb("stat", [128, 8], F32)

    pp = psum_pool(p)
    consts = emit_consts(p)

    la = Sem2(nc, "la")
    ca = p.sem("ca")
    va = p.sem("va")
    za = Sem2(nc, "za")
    if with_ln:
        zbt = p.sb("zbt", [128, Dm], F32)
        for j in range(nt + 5):
            with nc.Block() as blk:
                ca_b, va_b = ca.v, va.v

                @blk.sync
                def _(sync, j=j):
                    if j < 5:
                        la.dma(sync.dma_start(out=zbt[:],
                                              in_=zhalo[j * 128:(j + 1) * 128, :]), 0)
                    else:
                        la.dma(sync.dma_start(out=zbt[:],
                                              in_=zin[(j - 5) * 128:(j - 4) * 128, :]), 0)

                @blk.scalar
                def _(s, j=j):
                    la.wait(s, 0)
                    ca.eng(s.activation(out=pp["big"][:], in_=zbt[:],
                                        func=AF.Identity, accum_out=stat[:, 0:1]))
                    s.wait_ge(ca.h, ca.v)
                    ca.eng(s.activation(out=pp["big"][:], in_=zbt[:],
                                        func=AF.Square, accum_out=stat[:, 1:2]))
                    s.wait_ge(va.h, va_b + 4)
                    ca.eng(s.activation(out=stat[:, 6:7], in_=stat[:, 4:5], func=AF.Ln))
                    s.wait_ge(ca.h, ca.v)
                    ca.eng(s.activation(out=stat[:, 4:5], in_=stat[:, 6:7],
                                        func=AF.Exp, scale=-0.5))
                    s.wait_ge(va.h, va_b + 5)
                    ca.eng(s.activation(out=zbt[:], in_=zbt[:], func=AF.Identity,
                                        scale=stat[:, 4:5], bias=stat[:, 5:6]))

                @blk.vector
                def _(v, j=j):
                    v.wait_ge(ca.h, ca_b + 2)
                    va.eng(v.tensor_scalar_mul(stat[:, 2:3], stat[:, 0:1], 1.0 / Dm))
                    v.wait_ge(va.h, va.v)
                    va.eng(v.scalar_tensor_tensor(out=stat[:, 4:5], in0=stat[:, 2:3],
                                                  scalar=-1.0, in1=stat[:, 2:3],
                                                  op0=ALU.mult, op1=ALU.mult))
                    v.wait_ge(va.h, va.v)
                    va.eng(v.scalar_tensor_tensor(out=stat[:, 4:5], in0=stat[:, 1:2],
                                                  scalar=1.0 / Dm, in1=stat[:, 4:5],
                                                  op0=ALU.mult, op1=ALU.add))
                    v.wait_ge(va.h, va.v)
                    va.eng(v.tensor_scalar_add(stat[:, 4:5], stat[:, 4:5], 1e-5))
                    v.wait_ge(ca.h, ca_b + 4)
                    va.eng(v.scalar_tensor_tensor(out=stat[:, 5:6], in0=stat[:, 2:3],
                                                  scalar=-1.0, in1=stat[:, 4:5],
                                                  op0=ALU.mult, op1=ALU.mult))

                @blk.sync
                def _(sync, j=j):
                    sync.wait_ge(ca.h, ca_b + 5)
                    za.dma(sync.dma_start(out=z0_d[j * 128:(j + 1) * 128, :],
                                          in_=zbt[:]), 0)
                    sync.wait_ge(za.s[0].h, za.s[0].v)
    else:
        with nc.Block() as blk:
            @blk.gpsimd
            def _(g):
                za.dma(g.dma_start(out=z0_d[0:HB, :], in_=zhalo[:]), 0)
                za.dma(g.dma_start(out=z0_d[HB:HB + Rr, :], in_=zin[:]), 0)
                g.wait_ge(za.s[0].h, za.s[0].v)

    lw = p.sem("lw")
    ls = p.sem("ls")
    bv_ = p.sem("bv")
    tb0 = p.sem("tb0")
    with nc.Block() as blk:
        @blk.gpsimd
        def _(g):
            for d in range(dch):
                lw.dma(g.dma_start(out=wqkv[:, d * 3 * INNER:(d + 1) * 3 * INNER],
                                   in_=wqkv_in[d * 128:(d + 1) * 128, :]))
            for hh in range(H):
                lw.dma(g.dma_start(out=wo[:, hh * Dm:(hh + 1) * Dm],
                                   in_=wo_in[hh * HD:(hh + 1) * HD, :]))

        @blk.sync
        def _(sync):
            for g2 in range(16):
                ls.dma(sync.dma_start(out=msk[:, g2 * 512:(g2 + 1) * 512],
                                      in_=mk_in[g2 * 128:(g2 + 1) * 128, :]))
            ls.dma(sync.dma_start(out=qs4[:], in_=qs_in[:]))

        @blk.vector
        def _(v):
            v.wait_ge(ls.h, ls.v)
            bv_.eng(v.tensor_copy(out=qs4b[:], in_=qs4[:]))
            v.wait_ge(bv_.h, bv_.v)
            bv_.eng(v.memset(onesT[:], 1.0))
            v.wait_ge(bv_.h, bv_.v)
            bv_.eng(v.memset(vbuf[:], 1.0))

        @blk.tensor
        def _(t):
            t.wait_ge(bv_.h, 1)
            for i in range(4):
                ins = t.matmul(out=pp["a"][i][:],
                               lhsT=consts["ones"][0:1, 0:128],
                               rhs=qs4b[0:1, i * 512:(i + 1) * 512],
                               start=True, stop=True)
            tb0.eng(ins)

        @blk.vector
        def _(v):
            v.wait_ge(tb0.h, 1)
            for i in range(4):
                v.wait_ge(bv_.h, bv_.v)
                bv_.eng(v.tensor_copy(out=qsb[:, i * 512:(i + 1) * 512],
                                      in_=pp["a"][i][:]))

    lh = Sem2(nc, "lh")
    lr = Sem2(nc, "lr")
    tq = p.sem("tq")
    vq = p.sem("vq")
    sq_ = p.sem("sq")
    for j in range(1, nkt + 1):
        own = j >= 5
        with nc.Block() as blk:
            tq_b, vq_b, sq_b = tq.v, vq.v, sq_.v

            @blk.sync
            def _(sync, j=j, own=own):
                if own:
                    lr.dma(sync.dma_start(out=rqt[:],
                                          in_=rq_in[(j - 5) * 128:(j - 4) * 128, :]), 0)
                lr.dma(sync.dma_start(out=rkt[:],
                                      in_=rk_in[(j - 1) * 128:j * 128, :]), 0)
                lr.mark(0)

            @blk.gpsimd
            def _(g, j=j):
                r0 = j * 128
                lh.dma(g.dma_start(out=hb[:, 0:512], in_=z0_d[r0:r0 + 128, 0:512]), 0)
                lh.dma(g.dma_start(out=hb[:, 512:Dm],
                                   in_=z0_d[r0 - 1:r0 + 127, 512:Dm]), 0)
                lh.mark(0)

            @blk.tensor
            def _(t, j=j, own=own):
                if j == 1:
                    t.wait_ge(lw.h, lw.v)
                    t.wait_ge(bv_.h, 4)
                lh.wait(t, 0)
                for d in range(dch):
                    ins = t.transpose(out=pp["bf"][(d // 4) % 2][:, (d % 4) * 128:(d % 4 + 1) * 128],
                                      in_=hb[:, d * 128:(d + 1) * 128],
                                      identity=consts["ident"][:])
                    if d % 4 == 3:
                        tq.eng(ins)
                t.wait_ge(vq.h, vq_b + 2)
                for part, bank in ((0, 0), (1, 1), (2, 2)):
                    if part == 0 and not own:
                        continue
                    pairs = [(hT[:, d * 128:(d + 1) * 128],
                              wqkv[:, d * 3 * INNER + part * INNER:
                                   d * 3 * INNER + part * INNER + INNER])
                             for d in range(dch)]
                    tq.eng(mm_acc(t, pp["a"][bank][:], pairs))

            @blk.vector
            def _(v, j=j, own=own):
                for g2 in range(2):
                    v.wait_ge(tq.h, tq_b + g2 + 1)
                    vq.eng(v.tensor_copy(out=hT[:, g2 * 512:(g2 + 1) * 512],
                                         in_=pp["bf"][g2 % 2][:, 0:512]))
            vq_hT = vq.v

            nmm = (3 if own else 2)
            for part, bank, dst_col, tbl, scol in (
                    [(0, 0, ("q", j - 5), rqt, 0)] if own else []) + [
                    (1, 1, ("k", j - 1), rkt, 2)]:
                @blk.scalar
                def _(s, part=part, bank=bank):
                    s.wait_ge(tq.h, tq_b + 2 + (1 if part == 0 else nmm - 1))
                    s.wait_ge(vq.h, vq.v)
                    sq_.eng(s.activation(out=sqf[:], in_=pp["a"][bank][:],
                                         func=AF.Square))
                    s.wait_ge(vq.h, vq.v + 9)
                    sq_.eng(s.activation(out=ss8[:, 8:16], in_=ss8[:, 0:8], func=AF.Ln))
                    s.wait_ge(sq_.h, sq_.v)
                    sq_.eng(s.activation(out=ss8[:, 0:8], in_=ss8[:, 8:16],
                                         func=AF.Exp, scale=-0.5))

                @blk.vector
                def _(v, part=part, bank=bank, dst_col=dst_col, tbl=tbl, scol=scol):
                    v.wait_ge(sq_.h, sq_.v - 2)
                    for h in range(H):
                        v.wait_ge(vq.h, vq.v)
                        vq.eng(v.tensor_reduce(out=ss8[:, h:h + 1],
                                               in_=sqf[:, h * HD:(h + 1) * HD],
                                               axis=AX.X, op=ALU.add))
                    v.wait_ge(vq.h, vq.v)
                    vq.eng(v.tensor_scalar_max(ss8[:, 0:8], ss8[:, 0:8], 1e-24))
                    v.wait_ge(sq_.h, sq_.v)
                    for h in range(H):
                        v.wait_ge(vq.h, vq.v)
                        vq.eng(v.tensor_scalar(out=qhat[:, h * HD:(h + 1) * HD],
                                               in0=pp["a"][bank][:, h * HD:(h + 1) * HD],
                                               scalar1=ss8[:, h:h + 1], scalar2=None,
                                               op0=ALU.mult))
                    lr.wait(v, 0)
                    v.wait_ge(vq.h, vq.v)
                    vq.eng(v.tensor_tensor(out=rot[:, 0:512], in0=tbl[:, 0:512],
                                           in1=qsb[:, scol * 512:(scol + 1) * 512],
                                           op=ALU.mult))
                    v.wait_ge(vq.h, vq.v)
                    vq.eng(v.tensor_tensor(out=rot[:, 512:1024], in0=tbl[:, 512:1024],
                                           in1=qsb[:, (scol + 1) * 512:(scol + 2) * 512],
                                           op=ALU.mult))
                    v.wait_ge(vq.h, vq.v)
                    vq.eng(v.tensor_mul(rot[:, 0:512], rot[:, 0:512], qhat[:]))
                    swap = bass.AP(qhat, 32, [[512, 128], [64, 8], [-32, 2], [1, 32]])
                    v.wait_ge(vq.h, vq.v)
                    vq.eng(v.tensor_mul(rot[:, 512:1024], rot[:, 512:1024], swap))
                    kind, idx = dst_col
                    dst = (qbuf[:, idx * INNER:(idx + 1) * INNER] if kind == "q"
                           else kbuf[:, idx * INNER:(idx + 1) * INNER])
                    v.wait_ge(vq.h, vq.v)
                    vq.eng(v.tensor_add(dst, rot[:, 0:512], rot[:, 512:1024]))

            @blk.vector
            def _(v, j=j):
                v.wait_ge(tq.h, tq_b + 2 + nmm)
                vdst = bass.AP(vbuf, (j - 1) * (INNER + H),
                               [[nkt * (INNER + H), 128], [HD + 1, H], [1, HD]])
                vsrc = bass.AP(pp["a"][2], 0, [[512, 128], [HD, H], [1, HD]])
                v.wait_ge(vq.h, vq.v)
                vq.eng(v.tensor_copy(out=vdst, in_=vsrc))

    tS = p.sem("tS2")
    aS = p.sem("aS2")
    gm = p.sem("gm2")
    tc = p.sem("tc2")
    ac2 = p.sem("ac2")
    vn = p.sem("vn2")
    tr = p.sem("tr2")
    vno = p.sem("vno2")
    tw_ = p.sem("tw2")
    vw = p.sem("vw2")
    lzr = Sem2(nc, "lzr")
    oo = Sem2(nc, "oo")
    for w in range(nw):
        for h in range(H):
            with nc.Block() as blk:
                tS_b, aS_b, gm_b, tc_b = tS.v, aS.v, gm.v, tc.v
                vn_b, tr_b, vno_b = vn.v, tr.v, vno.v

                @blk.tensor
                def _(t, w=w, h=h):
                    for i in range(4):
                        ins = t.transpose(
                            out=pp["bf"][0][0:64, i * 128:(i + 1) * 128],
                            in_=qbuf[:, (4 * w + i) * INNER + h * HD:
                                     (4 * w + i) * INNER + (h + 1) * HD],
                            identity=consts["ident"][:])
                    tS.eng(ins)
                    for i in range(4):
                        ins = t.transpose(
                            out=pp["bf"][1][0:64, i * 128:(i + 1) * 128],
                            in_=kbuf[:, (4 * w + i) * INNER + h * HD:
                                     (4 * w + i) * INNER + (h + 1) * HD],
                            identity=consts["ident"][:])
                    tS.eng(ins)
                    t.wait_ge(vn.h, vn_b + 2)
                    for i in range(4, 8):
                        ins = t.transpose(
                            out=pp["bf"][1][0:64, (i % 4) * 128:(i % 4 + 1) * 128],
                            in_=kbuf[:, (4 * w + i) * INNER + h * HD:
                                     (4 * w + i) * INNER + (h + 1) * HD],
                            identity=consts["ident"][:])
                    tS.eng(ins)
                    t.wait_ge(vn.h, vn_b + 3)
                    for kt in range(8):
                        i = kt
                        if i >= 2:
                            t.wait_ge(aS.h, aS_b + i - 1)
                        tS.eng(t.matmul(out=pp["a"][i % 2][:, 0:512],
                                        lhsT=khT[:, kt * 128:(kt + 1) * 128],
                                        rhs=qhT[:],
                                        start=True, stop=True))
                        t.wait_ge(aS.h, aS_b + i + 1)
                        tc.eng(t.matmul(
                            out=pp["a"][2][0:65, 0:512],
                            lhsT=bass.AP(vbuf, (4 * w + kt) * (INNER + H) + h * (HD + 1),
                                         [[nkt * (INNER + H), 128], [1, HD + 1]]),
                            rhs=expb[:, (i % 2) * 512:(i % 2 + 1) * 512],
                            start=(kt == 0), stop=(kt == 7)))
                    t.wait_ge(vn.h, vn_b + 3 + 2)
                    tr.eng(t.matmul(out=pp["a"][3][0:64, 0:512],
                                    lhsT=onesT[64:65, 0:64],
                                    rhs=rb[64:65, 0:512],
                                    start=True, stop=True))

                ac2_b = ac2.v

                @blk.vector
                def _(v, w=w, h=h):
                    v.wait_ge(tS.h, tS_b + 1)
                    vn.eng(v.tensor_copy(out=qhT[:], in_=pp["bf"][0][0:64, 0:512]))
                    v.wait_ge(tS.h, tS_b + 2)
                    v.wait_ge(vn.h, vn.v)
                    vn.eng(v.tensor_copy(out=khT[:, 0:512], in_=pp["bf"][1][0:64, 0:512]))
                    v.wait_ge(tS.h, tS_b + 3)
                    v.wait_ge(vn.h, vn.v)
                    vn.eng(v.tensor_copy(out=khT[:, 512:1024], in_=pp["bf"][1][0:64, 0:512]))
                    moff = 0 if w == 0 else 8
                    for kt in range(8):
                        i = kt
                        v.wait_ge(tS.h, tS_b + 3 + i + 1)
                        if i >= 2:
                            v.wait_ge(aS.h, aS_b + i - 1)
                        gm.eng(v.tensor_add(
                            smf[:, (i % 2) * 512:(i % 2 + 1) * 512],
                            pp["a"][i % 2][:, 0:512],
                            msk[:, (moff + kt) * 512:(moff + kt + 1) * 512]))
                    v.wait_ge(ac2.h, ac2_b + 1)
                    v.wait_ge(vn.h, vn.v)
                    vn.eng(v.reciprocal(ctf[64:65, :], ctf[64:65, :]))
                    v.wait_ge(vn.h, vn.v)
                    vn.eng(v.tensor_copy(out=rb[64:65, :], in_=ctf[64:65, :]))
                    v.wait_ge(tr.h, tr_b + 1)
                    vno.eng(v.tensor_tensor(out=ctxTw[:, h * 512:(h + 1) * 512],
                                            in0=ctf[0:64, :],
                                            in1=pp["a"][3][0:64, 0:512],
                                            op=ALU.mult))

                @blk.scalar
                def _(s, w=w, h=h):
                    for kt in range(8):
                        i = kt
                        s.wait_ge(gm.h, gm_b + i + 1)
                        if i >= 2:
                            s.wait_ge(tc.h, tc_b + i - 1)
                        aS.eng(s.activation(out=expb[:, (i % 2) * 512:(i % 2 + 1) * 512],
                                            in_=smf[:, (i % 2) * 512:(i % 2 + 1) * 512],
                                            func=AF.Exp, scale=8.0))
                    s.wait_ge(tc.h, tc_b + 8)
                    ac2.eng(s.activation(out=ctf[:], in_=pp["a"][2][0:65, 0:512],
                                         func=AF.Identity))
        with nc.Block() as blk:
            tw_b, vw_b = tw_.v, vw.v

            @blk.sync
            def _(sync, w=w):
                for ts in range(4):
                    ti = 4 * w + ts
                    if ts >= 1:
                        sync.wait_ge(vw.h, vw_b + 2 * ts)
                    lzr.dma(sync.dma_start(out=zres[:],
                                           in_=z0_d[HB + ti * 128:HB + (ti + 1) * 128, :]), ts)
                    lzr.mark(ts)

            @blk.tensor
            def _(t, w=w):
                for ts in range(4):
                    for dg in range(2):
                        im = ts * 2 + dg
                        if im >= 2:
                            t.wait_ge(vw.h, vw_b + im - 1)
                        pairs = [(ctxTw[:, hh * 512 + ts * 128: hh * 512 + (ts + 1) * 128],
                                  wo[:, hh * Dm + dg * 512: hh * Dm + dg * 512 + 512])
                                 for hh in range(H)]
                        tw_.eng(mm_acc(t, pp["a"][2 + im % 2][:], pairs))

            @blk.vector
            def _(v, w=w):
                for ts in range(4):
                    lzr.wait(v, ts)
                    for dg in range(2):
                        im = ts * 2 + dg
                        v.wait_ge(tw_.h, tw_b + im + 1)
                        if dg == 0 and (w > 0 or ts > 0):
                            oo.wait(v, 4 * w + ts - 1)
                        vw.eng(v.tensor_add(
                            xob[:, dg * 512:(dg + 1) * 512],
                            pp["a"][2 + im % 2][:],
                            zres[:, dg * 512:(dg + 1) * 512]))

            @blk.gpsimd
            def _(g, w=w):
                for ts in range(4):
                    ti = 4 * w + ts
                    g.wait_ge(vw.h, vw_b + 2 * (ts + 1))
                    oo.dma(g.dma_start(out=zout[ti * 128:(ti + 1) * 128, :],
                                       in_=xob[:]), ti)
    with nc.Block() as blk:
        @blk.gpsimd
        def _(g):
            g.wait_ge(oo.s[0].h, oo.s[0].v)
            g.wait_ge(oo.s[1].h, oo.s[1].v)
    return p


def build_lfa2(Rr=R, Dm=D, H=8, HD=64, WIN=512, with_ln=True, pre=False):
    p = Prog("lfa2")
    nc = p.nc
    HB = 640
    INNER = H * HD
    nt = Rr // 128
    nat = (HB + Rr) // 128
    nkt = nat - 1
    dch = Dm // 128
    nw = Rr // WIN
    LN = with_ln

    zin = p.inp("zin", [Rr, Dm])
    zhalo = p.inp("zhalo", [HB, Dm])
    wqkv_in = p.inp("wqkv", [Dm, 3 * INNER], BF16 if pre else F32)
    wo_in = p.inp("wo", [INNER, Dm], BF16 if pre else F32)
    qs_in = p.inp("qs4", [1, 4 * INNER])
    rq_in = p.inp("rq", [Rr, 2 * INNER], BF16)
    rk_in = p.inp("rk", [HB + Rr - 128, 2 * INNER], BF16)
    mk_in = p.inp("masks", [16 * 128, 512], BF16)
    zout = p.out("zout", [Rr, Dm])

    z0_d = p.dram("z0_d", [HB + Rr, Dm]) if LN else None

    wqkv = p.sb("wqkv_sb", [128, dch * 3 * INNER], BF16)
    msk = p.sb("msk", [128, 16 * 512], BF16)
    qs4b = p.sb("qs4b", [1, 4 * INNER], BF16)
    qsb = p.sb("qsb", [128, 4 * INNER], BF16)
    zbt = p.sb("zbt", [128, 2 * Dm], F32)
    smf = zbt
    zbo = p.sb("zbo", [128, 2 * Dm], F32) if LN else None
    zbball = p.sb("zbball", [128, nat * Dm], BF16)
    hb2 = p.sb("hb2", [128, 2 * 512], BF16)
    hT = p.sb("hT", [128, 2 * Dm], BF16)
    rqt = p.sb("rqt", [128, 2 * 2 * INNER], BF16)
    rkt = p.sb("rkt", [128, 2 * 2 * INNER], BF16)
    sqq = p.sb("sqq", [128, 512], F32)
    sqk = p.sb("sqk", [128, 512], F32)
    ss8 = p.sb("ss8", [128, 64], F32)
    qhat = p.sb("qhat", [128, 2 * 512], BF16)
    khat = p.sb("khat", [128, 2 * 512], BF16)
    qbuf = p.sb("qbuf", [128, nt * INNER], BF16)
    kbuf = p.sb("kbuf", [128, nkt * INNER], BF16)
    vbuf = p.sb("vbuf", [128, nkt * (INNER + H)], BF16)
    stat = p.sb("stat", [128, 16], F32)
    qhT = p.sb("qhT", [64, 2 * 512], BF16)
    khT = p.sb("khT", [64, 2 * 1024], BF16)
    ctf = p.sb("ctf", [65, 2 * 512], F32)
    rb = p.sb("rb", [65, 2 * 512], BF16)
    onesT = p.sb("onesT", [65, 64], BF16)
    ctxTw = p.sb("ctxTw", [64, H * 512], BF16)
    expq = qhat
    expk = khat
    scrapA = msk.bitcast(F32)
    wqf = wqkv.bitcast(F32)

    pp = psum_pool(p)
    consts = emit_consts(p)
    woal = zbball

    cnt = {}

    def own_(j):
        return j >= 5

    A_ops = 5 if LN else 0
    ca_v = 0
    ln_stream = [0, 1] + [j + 1 for j in range(1, nat - 1)]
    qk_after = {j + 1: j for j in range(1, nat - 1)}
    ca_v = 0
    for j in (0, 1):
        if LN:
            cnt[("stats", j)] = ca_v + 2
            cnt[("rstd", j)] = ca_v + 4
            cnt[("apply", j)] = ca_v + 5
        ca_v += A_ops
    for j in range(1, nat):
        if j + 1 < nat:
            jj = j + 1
            if LN:
                cnt[("stats", jj)] = ca_v + 2
                cnt[("rstd", jj)] = ca_v + 4
                cnt[("apply", jj)] = ca_v + 5
            ca_v += A_ops
        o = own_(j)
        cnt[("sq", j)] = ca_v + (2 if o else 1)
        cnt[("rstd2", j)] = ca_v + (4 if o else 3)
        ca_v += 4 if o else 3
    def _vq_ln(j, vq_v):
        if LN:
            cnt[("var", j)] = vq_v + 4
            cnt[("nmr", j)] = vq_v + 5
            cnt[("zbb", j)] = vq_v + 6
            return vq_v + 6
        cnt[("zbb", j)] = vq_v + 1
        return vq_v + 1

    vq_v = 0
    for j in (0, 1):
        vq_v = _vq_ln(j, vq_v)
    for j in range(1, nat):
        if j + 1 < nat:
            vq_v = _vq_ln(j + 1, vq_v)
        o = own_(j)
        cnt[("evac", j)] = vq_v + 2
        vq_v += 2
        vq_v += (2 if o else 1) + 1
        cnt[("rmax", j)] = vq_v
        if o:
            vq_v += 1
        cnt[("qhat", j)] = vq_v
        vq_v += 1
        cnt[("khat", j)] = vq_v
        vq_v += (3 if o else 0) + 3
        cnt[("rotmul", j)] = vq_v
        vq_v += 1
        cnt[("vevac", j)] = vq_v
    tq_v = 0
    for j in range(1, nat):
        o = own_(j)
        cnt[("tp1", j)] = tq_v + 1
        cnt[("tp2", j)] = tq_v + 2
        tq_v += 2
        if o:
            cnt[("mmq", j)] = tq_v + 1
            cnt[("mmk", j)] = tq_v + 2
            cnt[("mmv", j)] = tq_v + 3
            tq_v += 3
        else:
            cnt[("mmk", j)] = tq_v + 1
            cnt[("mmv", j)] = tq_v + 2
            tq_v += 2
    gq_v = 0
    for j in range(1, nat):
        o = own_(j)
        gq_v += 2 if o else 0
        cnt[("rotq", j)] = gq_v
        gq_v += 2
        cnt[("rotk", j)] = gq_v
    for it in range(nw * H):
        w, h = it // H, it % H
        cnt[("tSq", it)] = 11 * it + 1
        cnt[("tSk1", it)] = 11 * it + 2
        cnt[("tSk2", it)] = 11 * it + 3
        for kt in range(8):
            cnt[("S", it, kt)] = 11 * it + 4 + kt
            cnt[("ctx", it, kt)] = 8 * it + kt + 1
            cnt[("exp", it, kt)] = 8 * it + kt + 1
            cnt[("gm", it, kt)] = 8 * it + kt + 1
        cnt[("khT1e", it)] = 5 * it + 2
        cnt[("khTe", it)] = 5 * it + 3
        cnt[("rb", it)] = 5 * it + 5
        cnt[("ctf", it)] = it + 1
        cnt[("tr", it)] = it + 1
        cnt[("ctxTw", it)] = it + 1
    for w in range(nw):
        for im in range(8):
            cnt[("wo", w, im)] = 8 * w + im + 1
            cnt[("woadd", w, im)] = 8 * w + im + 1
        cnt[("wo_mm", w)] = 8 * (w + 1)

    lww = p.sem("lww")
    bv_ = p.sem("bv")
    tb0 = p.sem("tb0")
    with nc.Block() as blk:
        @blk.gpsimd
        def _(g):
            for d in range(dch):
                lww.dma(g.dma_start(out=wqkv[:, d * 3 * INNER:(d + 1) * 3 * INNER],
                                    in_=wqkv_in[d * 128:(d + 1) * 128, :]))
            lww.dma(g.dma_start(out=qs4b[:], in_=qs_in[:]))

        @blk.vector
        def _(v):
            bv_.eng(v.memset(onesT[:], 1.0))
            v.wait_ge(bv_.h, bv_.v)
            bv_.eng(v.memset(vbuf[:], 1.0))
            v.wait_ge(bv_.h, bv_.v)
            bv_.eng(v.memset(ss8[:], 1.0))

        @blk.tensor
        def _(t):
            t.wait_ge(lww.h, lww.v)
            for i in range(4):
                ins = t.matmul(out=pp["a"][i][:],
                               lhsT=consts["ones"][0:1, 0:128],
                               rhs=qs4b[0:1, i * 512:(i + 1) * 512],
                               start=True, stop=True)
            tb0.eng(ins)

        @blk.vector
        def _(v):
            v.wait_ge(tb0.h, 1)
            for i in range(4):
                v.wait_ge(bv_.h, bv_.v)
                bv_.eng(v.tensor_copy(out=qsb[:, i * 512:(i + 1) * 512],
                                      in_=pp["a"][i][:]))

    la = Sem2(nc, "la")
    za = Sem2(nc, "za")
    lr = Sem2(nc, "lr")
    lh = Sem2(nc, "lh")
    for _jb in range(1, nat):
        lh.marks[_jb] = 32 * ((_jb + 1) // 2)
    ca = p.sem("ca")
    vq = p.sem("vq")
    tq = p.sem("tq")
    gq = p.sem("gq")

    def src_rows(j):
        if j < 5:
            return zhalo[j * 128:(j + 1) * 128, :]
        return zin[(j - 5) * 128:(j - 4) * 128, :]

    with nc.Block() as blk:
        @blk.scalar
        def _(s):
            def ln_part(j):
                sl = j % 2
                la.wait(s, j)
                s.wait_ge(ca.h, ca.v)
                ca.eng(s.activation(out=scrapA[:, 0:Dm], in_=zbt[:, sl * Dm:(sl + 1) * Dm],
                                    func=AF.Identity, accum_out=stat[:, sl * 8:sl * 8 + 1]))
                s.wait_ge(ca.h, ca.v)
                ca.eng(s.activation(out=scrapA[:, 0:Dm], in_=zbt[:, sl * Dm:(sl + 1) * Dm],
                                    func=AF.Square, accum_out=stat[:, sl * 8 + 1:sl * 8 + 2]))
                assert ca.v == cnt[("stats", j)]
                s.wait_ge(vq.h, cnt[("var", j)])
                ca.eng(s.activation(out=stat[:, sl * 8 + 6:sl * 8 + 7],
                                    in_=stat[:, sl * 8 + 4:sl * 8 + 5], func=AF.Ln))
                s.wait_ge(ca.h, ca.v)
                ca.eng(s.activation(out=stat[:, sl * 8 + 4:sl * 8 + 5],
                                    in_=stat[:, sl * 8 + 6:sl * 8 + 7],
                                    func=AF.Exp, scale=-0.5))
                assert ca.v == cnt[("rstd", j)]
                s.wait_ge(vq.h, cnt[("nmr", j)])
                if j >= 2:
                    za.wait(s, j - 2)
                ca.eng(s.activation(out=zbo[:, sl * Dm:(sl + 1) * Dm],
                                    in_=zbt[:, sl * Dm:(sl + 1) * Dm],
                                    func=AF.Identity,
                                    scale=stat[:, sl * 8 + 4:sl * 8 + 5],
                                    bias=stat[:, sl * 8 + 5:sl * 8 + 6]))
                assert ca.v == cnt[("apply", j)]

            def qk_part(jb):
                t2, o = jb % 2, own_(jb)
                if jb >= 2:
                    s.wait_ge(vq.h, cnt[("rmax", jb - 1)])
                if o:
                    s.wait_ge(tq.h, cnt[("mmq", jb)])
                    s.wait_ge(ca.h, ca.v)
                    ca.eng(s.activation(out=sqq[:],
                                        in_=pp["a"][jb % 2][:], func=AF.Square))
                s.wait_ge(tq.h, cnt[("mmk", jb)])
                s.wait_ge(ca.h, ca.v)
                ca.eng(s.activation(out=sqk[:],
                                    in_=pp["a"][2 + jb % 2][:], func=AF.Square))
                assert ca.v == cnt[("sq", jb)]
                s.wait_ge(vq.h, cnt[("rmax", jb)])
                s.wait_ge(ca.h, ca.v)
                ca.eng(s.activation(out=ss8[:, t2 * 32 + 16:t2 * 32 + 32],
                                    in_=ss8[:, t2 * 32:t2 * 32 + 16], func=AF.Ln))
                s.wait_ge(ca.h, ca.v)
                ca.eng(s.activation(out=ss8[:, t2 * 32:t2 * 32 + 16],
                                    in_=ss8[:, t2 * 32 + 16:t2 * 32 + 32],
                                    func=AF.Exp, scale=-0.5))
                assert ca.v == cnt[("rstd2", jb)]

            def rope_part(jr):
                tr2 = jr % 2
                if jr >= 3:
                    s.wait_ge(vq.h, cnt[("rotmul", jr - 2)])
                if own_(jr):
                    lr.dma(s.dma_start(
                        out=rqt[:, tr2 * 1024:(tr2 + 1) * 1024],
                        in_=rq_in[(jr - 5) * 128:(jr - 4) * 128, :]), jr)
                lr.dma(s.dma_start(
                    out=rkt[:, tr2 * 1024:(tr2 + 1) * 1024],
                    in_=rk_in[(jr - 1) * 128:jr * 128, :]), jr)
                lr.mark(jr)

            if LN:
                ln_part(0)
                ln_part(1)
            rope_part(1)
            for j in range(1, nat):
                if LN and j + 1 < nat:
                    ln_part(j + 1)
                qk_part(j)
                if j + 1 < nat:
                    rope_part(j + 1)

        @blk.vector
        def _(v):
            def lnv_part(j):
                sl = j % 2
                if LN:
                    v.wait_ge(ca.h, cnt[("stats", j)])
                    vq.eng(v.tensor_scalar_mul(stat[:, sl * 8 + 2:sl * 8 + 3],
                                               stat[:, sl * 8:sl * 8 + 1], 1.0 / Dm))
                    v.wait_ge(vq.h, vq.v)
                    vq.eng(v.scalar_tensor_tensor(out=stat[:, sl * 8 + 4:sl * 8 + 5],
                                                  in0=stat[:, sl * 8 + 2:sl * 8 + 3],
                                                  scalar=-1.0,
                                                  in1=stat[:, sl * 8 + 2:sl * 8 + 3],
                                                  op0=ALU.mult, op1=ALU.mult))
                    v.wait_ge(vq.h, vq.v)
                    vq.eng(v.scalar_tensor_tensor(out=stat[:, sl * 8 + 4:sl * 8 + 5],
                                                  in0=stat[:, sl * 8 + 1:sl * 8 + 2],
                                                  scalar=1.0 / Dm,
                                                  in1=stat[:, sl * 8 + 4:sl * 8 + 5],
                                                  op0=ALU.mult, op1=ALU.add))
                    v.wait_ge(vq.h, vq.v)
                    vq.eng(v.tensor_scalar_add(stat[:, sl * 8 + 4:sl * 8 + 5],
                                               stat[:, sl * 8 + 4:sl * 8 + 5], 1e-5))
                    assert vq.v == cnt[("var", j)]
                    v.wait_ge(ca.h, cnt[("rstd", j)])
                    vq.eng(v.scalar_tensor_tensor(out=stat[:, sl * 8 + 5:sl * 8 + 6],
                                                  in0=stat[:, sl * 8 + 2:sl * 8 + 3],
                                                  scalar=-1.0,
                                                  in1=stat[:, sl * 8 + 4:sl * 8 + 5],
                                                  op0=ALU.mult, op1=ALU.mult))
                    assert vq.v == cnt[("nmr", j)]
                    v.wait_ge(ca.h, cnt[("apply", j)])
                    v.wait_ge(vq.h, vq.v)
                    vq.eng(v.tensor_copy(out=zbball[:, j * Dm:(j + 1) * Dm],
                                         in_=zbo[:, sl * Dm:(sl + 1) * Dm]))
                else:
                    la.wait(v, j)
                    v.wait_ge(vq.h, vq.v)
                    vq.eng(v.tensor_copy(out=zbball[:, j * Dm:(j + 1) * Dm],
                                         in_=zbt[:, sl * Dm:(sl + 1) * Dm]))
                assert vq.v == cnt[("zbb", j)]

            def workv_part(jb):
                if True:
                    t2, o = jb % 2, own_(jb)
                    v.wait_ge(tq.h, cnt[("tp1", jb)])
                    v.wait_ge(vq.h, vq.v)
                    vq.eng(v.tensor_copy(out=hT[:, t2 * Dm:t2 * Dm + 512],
                                         in_=pp["bf"][0][:, 0:512]))
                    v.wait_ge(tq.h, cnt[("tp2", jb)])
                    v.wait_ge(vq.h, vq.v)
                    vq.eng(v.tensor_copy(out=hT[:, t2 * Dm + 512:t2 * Dm + 1024],
                                         in_=pp["bf"][1][:, 0:512]))
                    assert vq.v == cnt[("evac", jb)]
                    if o:
                        v.wait_ge(ca.h, cnt[("sq", jb)])
                        v.wait_ge(vq.h, vq.v)
                        vq.eng(v.tensor_reduce(
                            out=ss8[:, t2 * 32:t2 * 32 + 8],
                            in_=bass.AP(sqq, 0, [[512, 128], [64, 8], [1, 64]]),
                            axis=AX.X, op=ALU.add))
                    v.wait_ge(ca.h, cnt[("sq", jb)])
                    v.wait_ge(vq.h, vq.v)
                    vq.eng(v.tensor_reduce(
                        out=ss8[:, t2 * 32 + 8:t2 * 32 + 16],
                        in_=bass.AP(sqk, 0, [[512, 128], [64, 8], [1, 64]]),
                        axis=AX.X, op=ALU.add))
                    v.wait_ge(vq.h, vq.v)
                    vq.eng(v.tensor_scalar_max(ss8[:, t2 * 32:t2 * 32 + 16],
                                               ss8[:, t2 * 32:t2 * 32 + 16], 1e-24))
                    assert vq.v == cnt[("rmax", jb)]
                    v.wait_ge(ca.h, cnt[("rstd2", jb)])
                    if o:
                        v.wait_ge(vq.h, vq.v)
                        vq.eng(v.tensor_tensor(
                            out=qhat[:, t2 * 512:(t2 + 1) * 512],
                            in0=pp["a"][jb % 2][:],
                            in1=bass.AP(ss8, t2 * 32, [[64, 128], [1, 8], [0, 64]]),
                            op=ALU.mult))
                    assert vq.v == cnt[("qhat", jb)]
                    v.wait_ge(vq.h, vq.v)
                    vq.eng(v.tensor_tensor(
                        out=khat[:, t2 * 512:(t2 + 1) * 512],
                        in0=pp["a"][2 + jb % 2][:],
                        in1=bass.AP(ss8, t2 * 32 + 8, [[64, 128], [1, 8], [0, 64]]),
                        op=ALU.mult))
                    assert vq.v == cnt[("khat", jb)]
                    if o:
                        v.wait_ge(gq.h, cnt[("rotq", jb)])
                        v.wait_ge(vq.h, vq.v)
                        vq.eng(v.tensor_mul(rqt[:, t2 * 1024:t2 * 1024 + 512],
                                            rqt[:, t2 * 1024:t2 * 1024 + 512],
                                            qhat[:, t2 * 512:(t2 + 1) * 512]))
                        swq = bass.AP(qhat, t2 * 512 + 32,
                                      [[1024, 128], [64, 8], [-32, 2], [1, 32]])
                        v.wait_ge(vq.h, vq.v)
                        vq.eng(v.tensor_mul(rqt[:, t2 * 1024 + 512:(t2 + 1) * 1024],
                                            rqt[:, t2 * 1024 + 512:(t2 + 1) * 1024],
                                            swq))
                        v.wait_ge(vq.h, vq.v)
                        vq.eng(v.tensor_add(qbuf[:, (jb - 5) * INNER:(jb - 4) * INNER],
                                            rqt[:, t2 * 1024:t2 * 1024 + 512],
                                            rqt[:, t2 * 1024 + 512:(t2 + 1) * 1024]))
                    v.wait_ge(gq.h, cnt[("rotk", jb)])
                    v.wait_ge(vq.h, vq.v)
                    vq.eng(v.tensor_mul(rkt[:, t2 * 1024:t2 * 1024 + 512],
                                        rkt[:, t2 * 1024:t2 * 1024 + 512],
                                        khat[:, t2 * 512:(t2 + 1) * 512]))
                    swk = bass.AP(khat, t2 * 512 + 32,
                                  [[1024, 128], [64, 8], [-32, 2], [1, 32]])
                    v.wait_ge(vq.h, vq.v)
                    vq.eng(v.tensor_mul(rkt[:, t2 * 1024 + 512:(t2 + 1) * 1024],
                                        rkt[:, t2 * 1024 + 512:(t2 + 1) * 1024],
                                        swk))
                    v.wait_ge(vq.h, vq.v)
                    vq.eng(v.tensor_add(kbuf[:, (jb - 1) * INNER:jb * INNER],
                                        rkt[:, t2 * 1024:t2 * 1024 + 512],
                                        rkt[:, t2 * 1024 + 512:(t2 + 1) * 1024]))
                    assert vq.v == cnt[("rotmul", jb)]
                    v.wait_ge(tq.h, cnt[("mmv", jb)])
                    vdst = bass.AP(vbuf, (jb - 1) * (INNER + H),
                                   [[nkt * (INNER + H), 128], [HD + 1, H], [1, HD]])
                    vsrc = bass.AP(pp["big"], (jb % 2) * 512,
                                   [[1024, 128], [64, 8], [1, 64]])
                    v.wait_ge(vq.h, vq.v)
                    vq.eng(v.tensor_copy(out=vdst, in_=vsrc))
                    assert vq.v == cnt[("vevac", jb)]

            lnv_part(0)
            lnv_part(1)
            for j in range(1, nat):
                if j + 1 < nat:
                    lnv_part(j + 1)
                workv_part(j)

        @blk.sync
        def _(sync):
            la.dma(sync.dma_start(out=zbt[:, 0:Dm], in_=src_rows(0)), 0)
            la.mark(0)
            la.dma(sync.dma_start(out=zbt[:, Dm:2 * Dm], in_=src_rows(1)), 1)
            la.mark(1)
            for j in range(2, nat):
                sl = j % 2
                if LN:
                    sync.wait_ge(ca.h, cnt[("apply", j - 2)])
                else:
                    sync.wait_ge(vq.h, cnt[("zbb", j - 2)])
                la.dma(sync.dma_start(out=zbt[:, sl * Dm:(sl + 1) * Dm],
                                      in_=src_rows(j)), j)
                la.mark(j)

        @blk.tensor
        def _(t):
            for jb in range(1, nat):
                t2, o = jb % 2, own_(jb)
                lh.wait(t, jb)
                if jb >= 2:
                    t.wait_ge(vq.h, cnt[("evac", jb - 1)])
                for d in range(4):
                    ins = t.transpose(out=pp["bf"][0][:, d * 128:(d + 1) * 128],
                                      in_=zbball[:, jb * Dm + d * 128:jb * Dm + (d + 1) * 128],
                                      identity=consts["ident"][:])
                tq.eng(ins)
                assert tq.v == cnt[("tp1", jb)]
                for d in range(4):
                    ins = t.transpose(out=pp["bf"][1][:, d * 128:(d + 1) * 128],
                                      in_=hb2[:, t2 * 512 + d * 128:t2 * 512 + (d + 1) * 128],
                                      identity=consts["ident"][:])
                tq.eng(ins)
                assert tq.v == cnt[("tp2", jb)]
                t.wait_ge(vq.h, cnt[("evac", jb)])
                parts = ([(0, pp["a"][jb % 2][:])] if o else []) + \
                    [(1, pp["a"][2 + jb % 2][:]),
                     (2, pp["big"][:, (jb % 2) * 512:(jb % 2 + 1) * 512])]
                for part, bank in parts:
                    if jb >= 3:
                        if part == 0:
                            t.wait_ge(vq.h, cnt[("qhat", jb - 2)])
                        elif part == 1:
                            t.wait_ge(vq.h, cnt[("khat", jb - 2)])
                        else:
                            t.wait_ge(vq.h, cnt[("vevac", jb - 2)])
                    pairs = [(hT[:, t2 * Dm + d * 128:t2 * Dm + (d + 1) * 128],
                              wqkv[:, d * 3 * INNER + part * INNER:
                                   d * 3 * INNER + (part + 1) * INNER])
                             for d in range(dch)]
                    tq.eng(mm_acc(t, bank, pairs))
                    assert tq.v == cnt[("mm" + "qkv"[part], jb)]

        @blk.gpsimd
        def _(g):
            for jb in range(1, nat):
                t2, o = jb % 2, own_(jb)
                if jb == 1 and LN:
                    g.wait_ge(ca.h, cnt[("apply", 0)])
                    za.dma(g.dma_start(out=z0_d[0:128, :],
                                       in_=zbo[:, 0:Dm]), 0)
                    za.mark(0)
                g.wait_ge(vq.h, cnt[("zbb", jb)])
                lh.dma(g.dma_start(
                    out=hb2[1:128, t2 * 512:(t2 + 1) * 512],
                    in_=zbball[0:127, jb * Dm + 512:jb * Dm + 1024]), jb)
                lh.dma(g.dma_start(
                    out=hb2[0:1, t2 * 512:(t2 + 1) * 512],
                    in_=zbball[127:128, (jb - 1) * Dm + 512:jb * Dm]), jb)
                lh.mark(jb)
                if LN:
                    za.dma(g.dma_start(out=z0_d[jb * 128:(jb + 1) * 128, :],
                                       in_=zbo[:, t2 * Dm:(t2 + 1) * Dm]), jb)
                    za.mark(jb)
                lr.wait(g, jb)
                if jb >= 3:
                    g.wait_ge(vq.h, cnt[("rotmul", jb - 2)])
                if o:
                    gq.eng(g.tensor_tensor(out=rqt[:, t2 * 1024:t2 * 1024 + 512],
                                           in0=rqt[:, t2 * 1024:t2 * 1024 + 512],
                                           in1=qsb[:, 0:512], op=ALU.mult))
                    gq.eng(g.tensor_tensor(out=rqt[:, t2 * 1024 + 512:(t2 + 1) * 1024],
                                           in0=rqt[:, t2 * 1024 + 512:(t2 + 1) * 1024],
                                           in1=qsb[:, 512:1024], op=ALU.mult))
                assert gq.v == cnt[("rotq", jb)]
                gq.eng(g.tensor_tensor(out=rkt[:, t2 * 1024:t2 * 1024 + 512],
                                       in0=rkt[:, t2 * 1024:t2 * 1024 + 512],
                                       in1=qsb[:, 1024:1536], op=ALU.mult))
                gq.eng(g.tensor_tensor(out=rkt[:, t2 * 1024 + 512:(t2 + 1) * 1024],
                                       in0=rkt[:, t2 * 1024 + 512:(t2 + 1) * 1024],
                                       in1=qsb[:, 1536:2048], op=ALU.mult))
                assert gq.v == cnt[("rotk", jb)]

    lwo = p.sem("lwo")
    lmk = p.sem("lmk")
    tS = p.sem("tS2")
    aS = p.sem("aS2")
    gm = p.sem("gm2")
    tc = p.sem("tc2")
    ac2 = p.sem("ac2")
    vn = p.sem("vn2")
    tr = p.sem("tr2")
    vno = p.sem("vno2")
    tw_ = p.sem("tw2")
    vw = p.sem("vw2")
    lzr = Sem2(nc, "lzr")
    oo = Sem2(nc, "oo")

    with nc.Block() as blk:
        @blk.gpsimd
        def _(g):
            for hh in range(H):
                lwo.dma(g.dma_start(out=woal[0:HD, hh * Dm:(hh + 1) * Dm],
                                    in_=wo_in[hh * HD:(hh + 1) * HD, :]))

        @blk.vector
        def _(v):
            for it in range(nw * H):
                w, h, u = it // H, it % H, it % 2
                v.wait_ge(tS.h, cnt[("tSq", it)])
                v.wait_ge(vn.h, vn.v)
                vn.eng(v.tensor_copy(out=qhT[:, u * 512:(u + 1) * 512],
                                     in_=pp["bf"][0][0:64, 0:512]))
                v.wait_ge(tS.h, cnt[("tSk1", it)])
                v.wait_ge(vn.h, vn.v)
                vn.eng(v.tensor_copy(out=khT[:, u * 1024:u * 1024 + 512],
                                     in_=pp["bf"][1][0:64, 0:512]))
                assert vn.v == cnt[("khT1e", it)]
                v.wait_ge(tS.h, cnt[("tSk2", it)])
                v.wait_ge(vn.h, vn.v)
                vn.eng(v.tensor_copy(out=khT[:, u * 1024 + 512:(u + 1) * 1024],
                                     in_=pp["bf"][1][0:64, 0:512]))
                assert vn.v == cnt[("khTe", it)]
                moff = 0 if w == 0 else 8
                if it == 0:
                    v.wait_ge(lmk.h, 16 * 16)
                for kt in range(8):
                    v.wait_ge(tS.h, cnt[("S", it, kt)])
                    gi = it * 8 + kt
                    if gi >= 2:
                        v.wait_ge(aS.h, cnt[("exp", (gi - 2) // 8, (gi - 2) % 8)])
                    gm.eng(v.tensor_add(
                        smf[:, (kt % 2) * 512:(kt % 2 + 1) * 512],
                        pp["a"][kt % 2][:, 0:512],
                        msk[:, (moff + kt) * 512:(moff + kt + 1) * 512]))
                    assert gm.v == cnt[("gm", it, kt)]
                v.wait_ge(ac2.h, cnt[("ctf", it)])
                v.wait_ge(vn.h, vn.v)
                vn.eng(v.reciprocal(ctf[64:65, u * 512:(u + 1) * 512],
                                    ctf[64:65, u * 512:(u + 1) * 512]))
                v.wait_ge(vn.h, vn.v)
                vn.eng(v.tensor_copy(out=rb[64:65, u * 512:(u + 1) * 512],
                                     in_=ctf[64:65, u * 512:(u + 1) * 512]))
                assert vn.v == cnt[("rb", it)]
                v.wait_ge(tr.h, cnt[("tr", it)])
                vno.eng(v.tensor_tensor(out=ctxTw[:, h * 512:(h + 1) * 512],
                                        in0=ctf[0:64, u * 512:(u + 1) * 512],
                                        in1=pp["big"][0:64, u * 512:(u + 1) * 512],
                                        op=ALU.mult))
                assert vno.v == cnt[("ctxTw", it)]
                if h == H - 1:
                    for ts in range(4):
                        ti = 4 * w + ts
                        v2 = ti % 2
                        lzr.wait(v, ti)
                        for dg in range(2):
                            im = ts * 2 + dg
                            v.wait_ge(tw_.h, cnt[("wo", w, im)])
                            if dg == 0 and ti >= 2:
                                oo.wait(v, ti - 2)
                            vw.eng(v.tensor_add(
                                wqf[:, 2048 + v2 * Dm + dg * 512:
                                    2048 + v2 * Dm + (dg + 1) * 512],
                                pp["big"][:, (im % 2) * 512:(im % 2 + 1) * 512],
                                wqf[:, v2 * Dm + dg * 512:v2 * Dm + (dg + 1) * 512]))
                            assert vw.v == cnt[("woadd", w, im)]

        @blk.tensor
        def _(t):
            for it in range(nw * H):
                w, h, u = it // H, it % H, it % 2
                if it >= 2:
                    t.wait_ge(vn.h, cnt[("khTe", it - 1)])
                for i in range(4):
                    ins = t.transpose(
                        out=pp["bf"][0][0:64, i * 128:(i + 1) * 128],
                        in_=qbuf[:, (4 * w + i) * INNER + h * HD:
                                 (4 * w + i) * INNER + (h + 1) * HD],
                        identity=consts["ident"][:])
                tS.eng(ins)
                assert tS.v == cnt[("tSq", it)]
                for i in range(4):
                    ins = t.transpose(
                        out=pp["bf"][1][0:64, i * 128:(i + 1) * 128],
                        in_=kbuf[:, (4 * w + i) * INNER + h * HD:
                                 (4 * w + i) * INNER + (h + 1) * HD],
                        identity=consts["ident"][:])
                tS.eng(ins)
                assert tS.v == cnt[("tSk1", it)]
                t.wait_ge(vn.h, cnt[("khT1e", it)])
                for i in range(4, 8):
                    ins = t.transpose(
                        out=pp["bf"][1][0:64, (i % 4) * 128:(i % 4 + 1) * 128],
                        in_=kbuf[:, (4 * w + i) * INNER + h * HD:
                                 (4 * w + i) * INNER + (h + 1) * HD],
                        identity=consts["ident"][:])
                tS.eng(ins)
                assert tS.v == cnt[("tSk2", it)]
                t.wait_ge(vn.h, cnt[("khTe", it)])
                for kt in range(8):
                    gi = it * 8 + kt
                    if gi >= 2:
                        t.wait_ge(gm.h, cnt[("gm", (gi - 2) // 8, (gi - 2) % 8)])
                    tS.eng(t.matmul(out=pp["a"][kt % 2][:, 0:512],
                                    lhsT=khT[:, u * 1024 + kt * 128:u * 1024 + (kt + 1) * 128],
                                    rhs=qhT[:, u * 512:(u + 1) * 512],
                                    start=True, stop=True))
                    assert tS.v == cnt[("S", it, kt)]
                for kt in range(8):
                    if kt == 0 and it >= 2:
                        t.wait_ge(ac2.h, cnt[("ctf", it - 2)])
                    t.wait_ge(aS.h, cnt[("exp", it, kt)])
                    eb = expq if kt % 4 < 2 else expk
                    tc.eng(t.matmul(
                        out=pp["a"][2 + u][0:65, 0:512],
                        lhsT=bass.AP(vbuf, (4 * w + kt) * (INNER + H) + h * (HD + 1),
                                     [[nkt * (INNER + H), 128], [1, HD + 1]]),
                        rhs=eb[:, (kt % 2) * 512:(kt % 2 + 1) * 512],
                        start=(kt == 0), stop=(kt == 7)))
                    assert tc.v == cnt[("ctx", it, kt)]
                t.wait_ge(vn.h, cnt[("rb", it)])
                if w > 0:
                    t.wait_ge(vw.h, cnt[("woadd", w - 1, 6 + u)])
                elif it >= 2:
                    t.wait_ge(vno.h, cnt[("ctxTw", it - 2)])
                tr.eng(t.matmul(out=pp["big"][0:64, u * 512:(u + 1) * 512],
                                lhsT=onesT[64:65, 0:64],
                                rhs=rb[64:65, u * 512:(u + 1) * 512],
                                start=True, stop=True))
                assert tr.v == cnt[("tr", it)]
                if h == H - 1:
                    if w == 0:
                        t.wait_ge(lwo.h, H * 16)
                    t.wait_ge(vno.h, cnt[("ctxTw", it)])
                    for ts in range(4):
                        for dg in range(2):
                            im = ts * 2 + dg
                            if im >= 2:
                                t.wait_ge(vw.h, cnt[("woadd", w, im - 2)])
                            elif w > 0:
                                t.wait_ge(vw.h, cnt[("woadd", w - 1, im + 6)])
                            pairs = [(ctxTw[:, hh * 512 + ts * 128:hh * 512 + (ts + 1) * 128],
                                      woal[0:HD, hh * Dm + dg * 512:hh * Dm + dg * 512 + 512])
                                     for hh in range(H)]
                            tw_.eng(mm_acc(t, pp["big"][:, (im % 2) * 512:(im % 2 + 1) * 512],
                                           pairs))
                            assert tw_.v == cnt[("wo", w, im)]

        @blk.scalar
        def _(s):
            for g2 in range(16):
                lmk.dma(s.dma_start(out=msk[:, g2 * 512:(g2 + 1) * 512],
                                    in_=mk_in[g2 * 128:(g2 + 1) * 128, :]))
            for it in range(nw * H):
                w, h, u = it // H, it % H, it % 2
                for kt in range(8):
                    s.wait_ge(gm.h, cnt[("gm", it, kt)])
                    gi = it * 8 + kt
                    if gi >= 4:
                        s.wait_ge(tc.h, cnt[("ctx", (gi - 4) // 8, (gi - 4) % 8)])
                    eb = expq if kt % 4 < 2 else expk
                    aS.eng(s.activation(out=eb[:, (kt % 2) * 512:(kt % 2 + 1) * 512],
                                        in_=smf[:, (kt % 2) * 512:(kt % 2 + 1) * 512],
                                        func=AF.Exp, scale=8.0))
                    assert aS.v == cnt[("exp", it, kt)]
                s.wait_ge(tc.h, cnt[("ctx", it, 7)])
                ac2.eng(s.activation(out=ctf[:, u * 512:(u + 1) * 512],
                                     in_=pp["a"][2 + u][0:65, 0:512],
                                     func=AF.Identity))
                assert ac2.v == cnt[("ctf", it)]
                if h == H - 1:
                    for ts in range(4):
                        ti = 4 * w + ts
                        v2 = ti % 2
                        if ti >= 2:
                            pw, pts = (ti - 2) // 4, (ti - 2) % 4
                            s.wait_ge(vw.h, cnt[("woadd", pw, pts * 2 + 1)])
                        src = (z0_d[HB + ti * 128:HB + (ti + 1) * 128, :] if LN
                               else zin[ti * 128:(ti + 1) * 128, :])
                        if LN:
                            za.wait(s, ti + 5)
                        lzr.dma(s.dma_start(out=wqf[:, v2 * Dm:(v2 + 1) * Dm],
                                            in_=src), ti)
                        lzr.mark(ti)

        @blk.sync
        def _(sync):
            for w in range(nw):
                for ts in range(4):
                    ti = 4 * w + ts
                    v2 = ti % 2
                    sync.wait_ge(vw.h, cnt[("woadd", w, ts * 2 + 1)])
                    oo.dma(sync.dma_start(out=zout[ti * 128:(ti + 1) * 128, :],
                                          in_=wqf[:, 2048 + v2 * Dm:
                                                   2048 + v2 * Dm + Dm]), ti)
                    oo.mark(ti)

    with nc.Block() as blk:
        @blk.gpsimd
        def _(g):
            g.wait_ge(oo.s[0].h, oo.s[0].v)
            g.wait_ge(oo.s[1].h, oo.s[1].v)
            g.wait_ge(la.s[0].h, la.s[0].v)
            g.wait_ge(la.s[1].h, la.s[1].v)
            if LN:
                g.wait_ge(za.s[0].h, za.s[0].v)
                g.wait_ge(za.s[1].h, za.s[1].v)
            g.wait_ge(lr.s[0].h, lr.s[0].v)
            g.wait_ge(lr.s[1].h, lr.s[1].v)
            g.wait_ge(lzr.s[0].h, lzr.s[0].v)
            g.wait_ge(lzr.s[1].h, lzr.s[1].v)
            g.wait_ge(lwo.h, lwo.v)
    return p


def build_lff2(Rr=R, Dm=D, HID=2730, final=False, pre=False):
    p = Prog("lff2")
    nc = p.nc
    nt = Rr // 128
    dch = Dm // 128
    F2 = 2 * HID
    nfg = (F2 + 511) // 512
    hch = (HID + 127) // 128
    HIDP = hch * 128
    ndg = Dm // 512
    ngr = (hch + 3) // 4

    zin = p.inp("zin", [Rr, Dm])
    zhalo = p.inp("zhalo", [1, Dm])
    w1_in = p.inp("w1", [Dm, F2], BF16 if pre else F32)
    b1_in = p.inp("b1", [1, F2])
    w2_in = p.inp("w2", [HID, Dm], BF16 if pre else F32)
    b2_in = p.inp("b2", [1, Dm])
    lng_in = p.inp("lng", [1, Dm])
    lnb_in = p.inp("lnb", [1, Dm])
    if final:
        xr_in = p.inp("xr", [Rr, Dm])
    zout = p.out("zout", [Rr, Dm])

    w1sb = p.sb("w1sb", [128, dch * F2], BF16)
    w2sb = p.sb("w2sb", [128, hch * Dm], BF16)
    b1e = p.sb("b1e", [1, F2], BF16)
    b2r = p.sb("b2r", [1, Dm], BF16)
    lgc = p.sb("lgc", [128, dch], F32)
    lbc = p.sb("lbc", [128, dch], F32)
    lbb = p.sb("lbb", [128, dch], BF16)
    ht = p.sb("ht", [128, 2 * Dm], F32)
    hnb = p.sb("hnb", [128, Dm], BF16)
    hnT = p.sb("hnT", [128, 2 * Dm], BF16)
    agb = p.sb("agb", [128, F2], BF16)
    h2 = p.sb("h2", [128, HIDP], BF16)
    h2T = p.sb("h2T", [128, HIDP], BF16)
    zfin = p.sb("zfin", [128, 2 * Dm], F32)
    xfin = p.sb("xfin", [128, 2 * Dm], F32) if final else None
    xob = p.sb("xob", [128, 2 * Dm], F32)
    scrap = p.sb("scrap", [128, Dm], F32)
    stat = p.sb("stat", [128, 16], F32)

    pp = psum_pool(p)
    consts = emit_consts(p)

    lw = p.sem("lw")
    ls = p.sem("ls")
    vv0 = p.sem("vv0")
    tb = p.sem("tb")
    ab = p.sem("ab")
    lw2 = p.sem("lw2")
    with nc.Block() as blk:
        if pre:
            @blk.scalar
            def _(sc):
                for d in range(dch):
                    lw2.dma(sc.dma_start(out=w1sb[:, d * F2:(d + 1) * F2],
                                         in_=w1_in[d * 128:(d + 1) * 128, :]))

            @blk.gpsimd
            def _(g):
                for c in range(hch):
                    r0 = c * 128
                    r1 = min(HID, r0 + 128)
                    if r1 - r0 < 128:
                        g.wait_ge(vv0.h, 1)
                    lw.dma(g.dma_start(out=w2sb[0:r1 - r0, c * Dm:(c + 1) * Dm],
                                       in_=w2_in[r0:r1, :]))
                lw.dma(g.dma_start(out=b1e[:], in_=b1_in[:]))
                lw.dma(g.dma_start(out=b2r[:], in_=b2_in[:]))
        else:
            @blk.gpsimd
            def _(g):
                for d in range(dch):
                    lw.dma(g.dma_start(out=w1sb[:, d * F2:(d + 1) * F2],
                                       in_=w1_in[d * 128:(d + 1) * 128, :]))
                for c in range(hch):
                    r0 = c * 128
                    r1 = min(HID, r0 + 128)
                    if r1 - r0 < 128:
                        g.wait_ge(vv0.h, 1)
                    lw.dma(g.dma_start(out=w2sb[0:r1 - r0, c * Dm:(c + 1) * Dm],
                                       in_=w2_in[r0:r1, :]))
                lw.dma(g.dma_start(out=b1e[:], in_=b1_in[:]))
                lw.dma(g.dma_start(out=b2r[:], in_=b2_in[:]))

        @blk.sync
        def _(sync):
            for d in range(dch):
                ls.dma(sync.dma_start(out=lgc[:, d:d + 1],
                                      in_=lng_in[0:1, d * 128:(d + 1) * 128]))
                ls.dma(sync.dma_start(out=lbc[:, d:d + 1],
                                      in_=lnb_in[0:1, d * 128:(d + 1) * 128]))

        @blk.vector
        def _(v):
            if HIDP > HID:
                vv0.eng(v.memset(w2sb[:, (hch - 1) * Dm:hch * Dm], 0.0))
            v.wait_ge(ls.h, ls.v)
            vv0.eng(v.tensor_copy(out=lbb[:], in_=lbc[:]))
            v.wait_ge(vv0.h, vv0.v)
            vv0.eng(v.memset(h2[:], 0.0))
            v.wait_ge(vv0.h, vv0.v)
            vv0.eng(v.memset(h2T[:], 0.0))

    vv0_pre = vv0.v
    with nc.Block() as blk:
        @blk.tensor
        def _(t):
            t.wait_ge(lw.h, lw.v)
            t.wait_ge(lw2.h, lw2.v)
            t.wait_ge(vv0.h, vv0_pre)
            for fg in range(nfg):
                fw = min(512, F2 - fg * 512)
                if fg >= 2:
                    t.wait_ge(ab.h, fg - 1)
                pairs = [(lbb[:, d:d + 1],
                          w1sb[:, d * F2 + fg * 512: d * F2 + fg * 512 + fw])
                         for d in range(dch)]
                tb.eng(mm_acc(t, pp["a"][fg % 2][0:1, 0:fw], pairs,
                              rank1=(consts["ones"][0:1, 0:1],
                                     b1e[0:1, fg * 512:fg * 512 + fw])))

        @blk.scalar
        def _(s):
            for fg in range(nfg):
                fw = min(512, F2 - fg * 512)
                s.wait_ge(tb.h, fg + 1)
                if fg >= 1:
                    s.wait_ge(ab.h, ab.v)
                ab.eng(s.activation(out=b1e[0:1, fg * 512:fg * 512 + fw],
                                    in_=pp["a"][fg % 2][0:1, 0:fw],
                                    func=AF.Identity))

        @blk.vector
        def _(v):
            v.wait_ge(ab.h, nfg)
            for d in range(dch):
                v.wait_ge(vv0.h, vv0.v)
                vv0.eng(v.tensor_scalar_mul(w1sb[:, d * F2:(d + 1) * F2],
                                            w1sb[:, d * F2:(d + 1) * F2],
                                            lgc[:, d:d + 1]))
    vv0_all = vv0.v

    cnt = {}
    cvpt = 5 + (2 if final else 0)
    for ti in range(nt):
        cnt[("stats", ti)] = 5 * ti + 2
        cnt[("rstd", ti)] = 5 * ti + 4
        cnt[("apply", ti)] = 5 * ti + 5
        cnt[("var", ti)] = cvpt * ti + 4
        cnt[("nmr", ti)] = cvpt * ti + 5
        ttpt = 2 + ngr
        tmpt = nfg + ndg
        cnt[("tp_hn1", ti)] = ttpt * ti + 1
        cnt[("tp_hn2", ti)] = ttpt * ti + 2
        for g2 in range(ngr):
            cnt[("tp_h2", ti, g2)] = ttpt * ti + 3 + g2
        for fg in range(nfg):
            cnt[("w1g", ti, fg)] = tmpt * ti + fg + 1
        for dg in range(ndg):
            cnt[("w2g", ti, dg)] = tmpt * ti + nfg + dg + 1
        cnt[("hnTe", ti)] = ttpt * ti + 2
        for g2 in range(ngr):
            cnt[("h2Te", ti, g2)] = ttpt * ti + 3 + g2
        nev = (nfg + 1) // 2
        nod = nfg // 2
        aqpt = nev + 1
        vept = nod
        for fg in range(nfg):
            if fg % 2 == 0:
                cnt[("agbe", ti, fg)] = aqpt * ti + fg // 2 + 1
            else:
                cnt[("agbe", ti, fg)] = vept * ti + fg // 2 + 1
        cnt[("geluq", ti)] = aqpt * ti + nev + 1
        cnt[("agbo", ti)] = vept * (ti + 1)
        cnt[("mul", ti)] = ti + 1
        for dg in range(ndg):
            cnt[("add", ti, dg)] = ndg * ti + dg + 1
        cnt[("adds", ti)] = ndg * (ti + 1)

    lz = Sem2(nc, "lz")
    lf = Sem2(nc, "lf")
    lxr = Sem2(nc, "lxr")
    cs = p.sem("cs")
    cv = p.sem("cv")
    tt = p.sem("tt")
    vh = p.sem("vh")
    tm = p.sem("tm")
    aq = p.sem("aq")
    vg = p.sem("vg")
    vo = p.sem("vo")
    ve = p.sem("ve")
    oo = Sem2(nc, "oo")

    with nc.Block() as blk:
        @blk.sync
        def _(sync):
            for ti in range(nt):
                t2 = ti % 2
                r0 = ti * 128
                if ti >= 2:
                    sync.wait_ge(cs.h, cnt[("apply", ti - 2)])
                if ti == 0:
                    lz.dma(sync.dma_start(out=ht[0:128, 0:512],
                                          in_=zin[0:128, 0:512]), ti)
                    lz.dma(sync.dma_start(out=ht[0:1, 512:Dm],
                                          in_=zhalo[0:1, 512:Dm]), ti)
                    lz.dma(sync.dma_start(out=ht[1:128, 512:Dm],
                                          in_=zin[0:127, 512:Dm]), ti)
                else:
                    lz.dma(sync.dma_start(out=ht[:, t2 * Dm:t2 * Dm + 512],
                                          in_=zin[r0:r0 + 128, 0:512]), ti)
                    lz.dma(sync.dma_start(out=ht[:, t2 * Dm + 512:(t2 + 1) * Dm],
                                          in_=zin[r0 - 1:r0 + 127, 512:Dm]), ti)
                lz.mark(ti)
                if ti >= 2:
                    sync.wait_ge(vo.h, cnt[("adds", ti - 2)])
                lf.dma(sync.dma_start(out=zfin[:, t2 * Dm:(t2 + 1) * Dm],
                                      in_=zin[r0:r0 + 128, :]), ti)
                lf.mark(ti)
                if final:
                    lxr.dma(sync.dma_start(out=xfin[:, t2 * Dm:(t2 + 1) * Dm],
                                           in_=xr_in[r0:r0 + 128, :]), ti)
                    lxr.mark(ti)

        @blk.scalar
        def _(s):
            for ti in range(nt):
                t2 = ti % 2
                lz.wait(s, ti)
                s.wait_ge(cs.h, cs.v)
                cs.eng(s.activation(out=scrap[:], in_=ht[:, t2 * Dm:(t2 + 1) * Dm],
                                    func=AF.Identity, accum_out=stat[:, t2 * 8:t2 * 8 + 1]))
                s.wait_ge(cs.h, cs.v)
                cs.eng(s.activation(out=scrap[:], in_=ht[:, t2 * Dm:(t2 + 1) * Dm],
                                    func=AF.Square, accum_out=stat[:, t2 * 8 + 1:t2 * 8 + 2]))
                assert cs.v == cnt[("stats", ti)]
                s.wait_ge(cv.h, cnt[("var", ti)])
                cs.eng(s.activation(out=stat[:, t2 * 8 + 6:t2 * 8 + 7],
                                    in_=stat[:, t2 * 8 + 4:t2 * 8 + 5], func=AF.Ln))
                s.wait_ge(cs.h, cs.v)
                cs.eng(s.activation(out=stat[:, t2 * 8 + 4:t2 * 8 + 5],
                                    in_=stat[:, t2 * 8 + 6:t2 * 8 + 7],
                                    func=AF.Exp, scale=-0.5))
                assert cs.v == cnt[("rstd", ti)]
                s.wait_ge(cv.h, cnt[("nmr", ti)])
                if ti >= 1:
                    s.wait_ge(tt.h, cnt[("tp_hn2", ti - 1)])
                cs.eng(s.activation(out=hnb[:], in_=ht[:, t2 * Dm:(t2 + 1) * Dm],
                                    func=AF.Identity,
                                    scale=stat[:, t2 * 8 + 4:t2 * 8 + 5],
                                    bias=stat[:, t2 * 8 + 5:t2 * 8 + 6]))
                assert cs.v == cnt[("apply", ti)]
                for fg in range(0, nfg, 2):
                    fw = min(512, F2 - fg * 512)
                    s.wait_ge(tm.h, cnt[("w1g", ti, fg)])
                    if fg == 0 and ti >= 1:
                        s.wait_ge(vg.h, cnt[("mul", ti - 1)])
                    aq.eng(s.activation(out=agb[:, fg * 512:fg * 512 + fw],
                                        in_=pp["a"][0][:, 0:fw],
                                        func=AF.Identity))
                    assert aq.v == cnt[("agbe", ti, fg)]
                if ti >= 1:
                    s.wait_ge(tt.h, cnt[("tp_h2", ti - 1, ngr - 1)])
                s.wait_ge(aq.h, cnt[("agbe", ti, nfg - 1 - (nfg % 2 == 0))])
                s.wait_ge(ve.h, cnt[("agbo", ti)])
                aq.eng(s.activation(out=h2[:, 0:HID], in_=agb[:, HID:F2],
                                    func=AF.Gelu))
                assert aq.v == cnt[("geluq", ti)]

        @blk.vector
        def _(v):
            for ti in range(nt):
                t2 = ti % 2
                v.wait_ge(cs.h, cnt[("stats", ti)])
                cv.eng(v.tensor_scalar_mul(stat[:, t2 * 8 + 2:t2 * 8 + 3],
                                           stat[:, t2 * 8:t2 * 8 + 1], 1.0 / Dm))
                v.wait_ge(cv.h, cv.v)
                cv.eng(v.scalar_tensor_tensor(out=stat[:, t2 * 8 + 4:t2 * 8 + 5],
                                              in0=stat[:, t2 * 8 + 2:t2 * 8 + 3],
                                              scalar=-1.0,
                                              in1=stat[:, t2 * 8 + 2:t2 * 8 + 3],
                                              op0=ALU.mult, op1=ALU.mult))
                v.wait_ge(cv.h, cv.v)
                cv.eng(v.scalar_tensor_tensor(out=stat[:, t2 * 8 + 4:t2 * 8 + 5],
                                              in0=stat[:, t2 * 8 + 1:t2 * 8 + 2],
                                              scalar=1.0 / Dm,
                                              in1=stat[:, t2 * 8 + 4:t2 * 8 + 5],
                                              op0=ALU.mult, op1=ALU.add))
                v.wait_ge(cv.h, cv.v)
                cv.eng(v.tensor_scalar_add(stat[:, t2 * 8 + 4:t2 * 8 + 5],
                                           stat[:, t2 * 8 + 4:t2 * 8 + 5], 1e-5))
                assert cv.v == cnt[("var", ti)]
                v.wait_ge(cs.h, cnt[("rstd", ti)])
                cv.eng(v.scalar_tensor_tensor(out=stat[:, t2 * 8 + 5:t2 * 8 + 6],
                                              in0=stat[:, t2 * 8 + 2:t2 * 8 + 3],
                                              scalar=-1.0,
                                              in1=stat[:, t2 * 8 + 4:t2 * 8 + 5],
                                              op0=ALU.mult, op1=ALU.mult))
                assert cv.v == cnt[("nmr", ti)]
                for g2 in range(2):
                    v.wait_ge(tt.h, cnt[("tp_hn1", ti)] + g2)
                    if ti >= 2 and g2 == 0:
                        v.wait_ge(tm.h, cnt[("w1g", ti - 2, nfg - 1)])
                    v.wait_ge(vh.h, vh.v)
                    vh.eng(v.tensor_copy(out=hnT[:, t2 * Dm + g2 * 512:
                                                 t2 * Dm + (g2 + 1) * 512],
                                         in_=pp["bf"][g2 % 2][:, 0:512]))
                assert vh.v == cnt[("hnTe", ti)]
                for fg in range(1, nfg, 2):
                    fw = min(512, F2 - fg * 512)
                    v.wait_ge(tm.h, cnt[("w1g", ti, fg)])
                    if fg == 1 and ti >= 1:
                        v.wait_ge(vg.h, cnt[("mul", ti - 1)])
                    v.wait_ge(ve.h, ve.v)
                    ve.eng(v.tensor_copy(out=agb[:, fg * 512:fg * 512 + fw],
                                         in_=pp["a"][1][:, 0:fw]))
                    assert ve.v == cnt[("agbe", ti, fg)]
                v.wait_ge(aq.h, cnt[("geluq", ti)])
                vg.eng(v.tensor_mul(h2[:, 0:HID], agb[:, 0:HID], h2[:, 0:HID]))
                assert vg.v == cnt[("mul", ti)]
                for g2 in range(ngr):
                    c0 = g2 * 4
                    c1 = min(hch, c0 + 4)
                    v.wait_ge(tt.h, cnt[("tp_h2", ti, g2)])
                    if g2 == 0 and ti >= 1:
                        v.wait_ge(tm.h, cnt[("w2g", ti - 1, ndg - 1)])
                    v.wait_ge(vh.h, vh.v)
                    vh.eng(v.tensor_copy(out=h2T[:, c0 * 128:c1 * 128],
                                         in_=pp["bf"][g2 % 2][:, 0:(c1 - c0) * 128]))
                    assert vh.v == cnt[("h2Te", ti, g2)]
                if ti >= 2:
                    oo.wait(v, ti - 2)
                lf.wait(v, ti)
                if final:
                    lxr.wait(v, ti)
                for dg in range(ndg):
                    v.wait_ge(tm.h, cnt[("w2g", ti, dg)])
                    sl = slice(t2 * Dm + dg * 512, t2 * Dm + (dg + 1) * 512)
                    if final:
                        cv.eng(v.tensor_add(xob[:, sl], pp["a"][2 + dg % 2][:],
                                            zfin[:, sl]))
                        v.wait_ge(cv.h, cv.v)
                        vo.eng(v.scalar_tensor_tensor(
                            out=xob[:, sl], in0=xob[:, sl], scalar=0.5,
                            in1=xfin[:, sl], op0=ALU.mult, op1=ALU.add))
                    else:
                        vo.eng(v.tensor_add(xob[:, sl], pp["a"][2 + dg % 2][:],
                                            zfin[:, sl]))
                assert vo.v == cnt[("adds", ti)]

        @blk.tensor
        def _(t):
            for ti in range(nt):
                t2 = ti % 2
                if ti == 0:
                    t.wait_ge(lw.h, lw.v)
                    t.wait_ge(vv0.h, vv0_all)
                t.wait_ge(cs.h, cnt[("apply", ti)])
                if ti >= 1:
                    t.wait_ge(vh.h, cnt[("h2Te", ti - 1, ngr - 1)])
                for g2 in range(2):
                    for j in range(4):
                        d = g2 * 4 + j
                        ins = t.transpose(out=pp["bf"][g2 % 2][:, j * 128:(j + 1) * 128],
                                          in_=hnb[:, d * 128:(d + 1) * 128],
                                          identity=consts["ident"][:])
                    tt.eng(ins)
                assert tt.v == cnt[("tp_hn2", ti)]
                t.wait_ge(vh.h, cnt[("hnTe", ti)])
                for fg in range(nfg):
                    fw = min(512, F2 - fg * 512)
                    sem_g = aq if fg % 2 == 0 else ve
                    if fg >= 2:
                        t.wait_ge(sem_g.h, cnt[("agbe", ti, fg - 2)])
                    elif ti >= 1:
                        pfg = nfg - 1 - ((nfg - 1) % 2 != fg % 2)
                        t.wait_ge(sem_g.h, cnt[("agbe", ti - 1, pfg)])
                    pairs = [(hnT[:, t2 * Dm + d * 128:t2 * Dm + (d + 1) * 128],
                              w1sb[:, d * F2 + fg * 512: d * F2 + fg * 512 + fw])
                             for d in range(dch)]
                    tm.eng(mm_acc(t, pp["a"][fg % 2][:, 0:fw], pairs,
                                  rank1=(consts["ones"][0:1, 0:128],
                                         b1e[0:1, fg * 512:fg * 512 + fw])))
                    assert tm.v == cnt[("w1g", ti, fg)]
                t.wait_ge(vg.h, cnt[("mul", ti)])
                for g2 in range(ngr):
                    c0 = g2 * 4
                    c1 = min(hch, c0 + 4)
                    if g2 >= 2:
                        t.wait_ge(vh.h, cnt[("h2Te", ti, g2 - 2)])
                    for j in range(c1 - c0):
                        ins = t.transpose(out=pp["bf"][g2 % 2][:, j * 128:(j + 1) * 128],
                                          in_=h2[:, (c0 + j) * 128:(c0 + j + 1) * 128],
                                          identity=consts["ident"][:])
                    tt.eng(ins)
                    assert tt.v == cnt[("tp_h2", ti, g2)]
                t.wait_ge(vh.h, cnt[("h2Te", ti, ngr - 1)])
                for dg in range(ndg):
                    if ti >= 1:
                        t.wait_ge(vo.h, cnt[("add", ti - 1, dg)])
                    pairs = [(h2T[:, c * 128:(c + 1) * 128],
                              w2sb[:, c * Dm + dg * 512: c * Dm + (dg + 1) * 512])
                             for c in range(hch)]
                    tm.eng(mm_acc(t, pp["a"][2 + dg % 2][:], pairs,
                                  rank1=(consts["ones"][0:1, 0:128],
                                         b2r[0:1, dg * 512:(dg + 1) * 512])))
                    assert tm.v == cnt[("w2g", ti, dg)]

        @blk.gpsimd
        def _(g):
            for ti in range(nt):
                t2 = ti % 2
                g.wait_ge(vo.h, cnt[("adds", ti)])
                oo.dma(g.dma_start(out=zout[ti * 128:(ti + 1) * 128, :],
                                   in_=xob[:, t2 * Dm:(t2 + 1) * Dm]), ti)
                oo.mark(ti)

    with nc.Block() as blk:
        @blk.gpsimd
        def _(g):
            g.wait_ge(oo.s[0].h, oo.s[0].v)
            g.wait_ge(oo.s[1].h, oo.s[1].v)
            g.wait_ge(lz.s[0].h, lz.s[0].v)
            g.wait_ge(lz.s[1].h, lz.s[1].v)
            g.wait_ge(lf.s[0].h, lf.s[0].v)
            g.wait_ge(lf.s[1].h, lf.s[1].v)
            if final:
                g.wait_ge(lxr.s[0].h, lxr.s[0].v)
                g.wait_ge(lxr.s[1].h, lxr.s[1].v)
    return p


def build_lff3(Rr=R, Dm=D, HID=2730, final=False, pre=False):
    p = Prog("lff3")
    nc = p.nc
    nt = Rr // 128
    dch = Dm // 128
    F2 = 2 * HID
    nfg = (F2 + 511) // 512
    hch = (HID + 127) // 128
    HIDP = hch * 128
    ndg = Dm // 512
    ngr = (hch + 3) // 4

    zin = p.inp("zin", [Rr, Dm])
    zhalo = p.inp("zhalo", [1, Dm])
    w1_in = p.inp("w1", [Dm, F2], BF16 if pre else F32)
    b1_in = p.inp("b1", [1, F2])
    w2_in = p.inp("w2", [HID, Dm], BF16 if pre else F32)
    b2_in = p.inp("b2", [1, Dm])
    lng_in = p.inp("lng", [1, Dm])
    lnb_in = p.inp("lnb", [1, Dm])
    if final:
        xr_in = p.inp("xr", [Rr, Dm])
    zout = p.out("zout", [Rr, Dm])

    w1sb = p.sb("w1sb", [128, dch * F2], BF16)
    w2sb = p.sb("w2sb", [128, hch * Dm], BF16)
    b1e = p.sb("b1e", [1, F2], BF16)
    b2r = p.sb("b2r", [1, Dm], BF16)
    lgc = p.sb("lgc", [128, dch], F32)
    lbc = p.sb("lbc", [128, dch], F32)
    lbb = p.sb("lbb", [128, dch], BF16)
    ht = p.sb("ht", [128, 2 * Dm], F32)
    hnb = p.sb("hnb", [128, Dm], BF16)
    hnT = p.sb("hnT", [128, 2 * Dm], BF16)
    agb = p.sb("agb", [128, F2], BF16)
    h2 = p.sb("h2", [128, HIDP], BF16)
    h2T = p.sb("h2T", [128, HIDP], BF16)
    zfin = p.sb("zfin", [128, 2 * Dm], F32)
    xfin = p.sb("xfin", [128, 2 * Dm], F32) if final else None
    xob = p.sb("xob", [128, 2 * Dm], F32)
    scrap = p.sb("scrap", [128, Dm], F32)
    stat = p.sb("stat", [128, 16], F32)

    pp = psum_pool(p)
    consts = emit_consts(p)

    lw = p.sem("lw")
    ls = p.sem("ls")
    vv0 = p.sem("vv0")
    tb = p.sem("tb")
    ab = p.sem("ab")
    lw2 = p.sem("lw2")
    with nc.Block() as blk:
        if pre:
            @blk.scalar
            def _(sc):
                for d in range(dch):
                    lw2.dma(sc.dma_start(out=w1sb[:, d * F2:(d + 1) * F2],
                                         in_=w1_in[d * 128:(d + 1) * 128, :]))

            @blk.gpsimd
            def _(g):
                for c in range(hch):
                    r0 = c * 128
                    r1 = min(HID, r0 + 128)
                    if r1 - r0 < 128:
                        g.wait_ge(vv0.h, 1)
                    lw.dma(g.dma_start(out=w2sb[0:r1 - r0, c * Dm:(c + 1) * Dm],
                                       in_=w2_in[r0:r1, :]))
                lw.dma(g.dma_start(out=b1e[:], in_=b1_in[:]))
                lw.dma(g.dma_start(out=b2r[:], in_=b2_in[:]))
        else:
            @blk.gpsimd
            def _(g):
                for d in range(dch):
                    lw.dma(g.dma_start(out=w1sb[:, d * F2:(d + 1) * F2],
                                       in_=w1_in[d * 128:(d + 1) * 128, :]))
                for c in range(hch):
                    r0 = c * 128
                    r1 = min(HID, r0 + 128)
                    if r1 - r0 < 128:
                        g.wait_ge(vv0.h, 1)
                    lw.dma(g.dma_start(out=w2sb[0:r1 - r0, c * Dm:(c + 1) * Dm],
                                       in_=w2_in[r0:r1, :]))
                lw.dma(g.dma_start(out=b1e[:], in_=b1_in[:]))
                lw.dma(g.dma_start(out=b2r[:], in_=b2_in[:]))

        @blk.sync
        def _(sync):
            for d in range(dch):
                ls.dma(sync.dma_start(out=lgc[:, d:d + 1],
                                      in_=lng_in[0:1, d * 128:(d + 1) * 128]))
                ls.dma(sync.dma_start(out=lbc[:, d:d + 1],
                                      in_=lnb_in[0:1, d * 128:(d + 1) * 128]))

        @blk.vector
        def _(v):
            if HIDP > HID:
                vv0.eng(v.memset(w2sb[:, (hch - 1) * Dm:hch * Dm], 0.0))
            v.wait_ge(ls.h, ls.v)
            vv0.eng(v.tensor_copy(out=lbb[:], in_=lbc[:]))
            v.wait_ge(vv0.h, vv0.v)
            vv0.eng(v.memset(h2[:], 0.0))
            v.wait_ge(vv0.h, vv0.v)
            vv0.eng(v.memset(h2T[:], 0.0))

    vv0_pre = vv0.v
    with nc.Block() as blk:
        @blk.tensor
        def _(t):
            t.wait_ge(lw.h, lw.v)
            t.wait_ge(lw2.h, lw2.v)
            t.wait_ge(vv0.h, vv0_pre)
            for fg in range(nfg):
                fw = min(512, F2 - fg * 512)
                if fg >= 2:
                    t.wait_ge(ab.h, fg - 1)
                pairs = [(lbb[:, d:d + 1],
                          w1sb[:, d * F2 + fg * 512: d * F2 + fg * 512 + fw])
                         for d in range(dch)]
                tb.eng(mm_acc(t, pp["a"][fg % 2][0:1, 0:fw], pairs,
                              rank1=(consts["ones"][0:1, 0:1],
                                     b1e[0:1, fg * 512:fg * 512 + fw])))

        @blk.scalar
        def _(s):
            for fg in range(nfg):
                fw = min(512, F2 - fg * 512)
                s.wait_ge(tb.h, fg + 1)
                if fg >= 1:
                    s.wait_ge(ab.h, ab.v)
                ab.eng(s.activation(out=b1e[0:1, fg * 512:fg * 512 + fw],
                                    in_=pp["a"][fg % 2][0:1, 0:fw],
                                    func=AF.Identity))

    vv0_all = vv0.v

    cnt = {}
    for ti in range(nt):
        cnt[("stats", ti)] = 5 * ti + 2
        cnt[("rstd", ti)] = 5 * ti + 4
        cnt[("apply", ti)] = 5 * ti + 5
    cv_v = 0
    cnt[("var", 0)] = 4
    cnt[("nmr", 0)] = 5
    cv_v = 5
    for ti in range(nt):
        if ti + 1 < nt:
            cnt[("var", ti + 1)] = cv_v + 4
            cnt[("nmr", ti + 1)] = cv_v + 5
            cv_v += 5
        if final and ti >= 1:
            cv_v += ndg
    tt_v = tm_v = vh_v = 0
    for it in range(nt + 1):
        if it < nt:
            cnt[("tp_hn1", it)] = tt_v + 1
            cnt[("tp_hn2", it)] = tt_v + 2
            tt_v += 2
            for fg in range(nfg):
                cnt[("w1g", it, fg)] = tm_v + fg + 1
            tm_v += nfg
            cnt[("hnTe", it)] = vh_v + 2
            vh_v += 2
        if it >= 1:
            tj = it - 1
            for g2 in range(ngr):
                cnt[("tp_h2", tj, g2)] = tt_v + g2 + 1
            tt_v += ngr
            for dg in range(ndg):
                cnt[("w2g", tj, dg)] = tm_v + dg + 1
            tm_v += ndg
            for g2 in range(ngr):
                cnt[("h2Te", tj, g2)] = vh_v + g2 + 1
            vh_v += ngr
    nev = (nfg + 1) // 2
    nod = nfg // 2
    aqpt = nev + 1
    vept = nod
    for ti in range(nt):
        for fg in range(nfg):
            if fg % 2 == 0:
                cnt[("agbe", ti, fg)] = aqpt * ti + fg // 2 + 1
            else:
                cnt[("agbe", ti, fg)] = vept * ti + fg // 2 + 1
        cnt[("geluq", ti)] = aqpt * ti + nev + 1
        cnt[("agbo", ti)] = vept * (ti + 1)
        cnt[("mul", ti)] = ti + 1
        for dg in range(ndg):
            cnt[("add", ti, dg)] = ndg * ti + dg + 1
        cnt[("adds", ti)] = ndg * (ti + 1)

    lz = Sem2(nc, "lz")
    lf = Sem2(nc, "lf")
    lxr = Sem2(nc, "lxr")
    cs = p.sem("cs")
    cv = p.sem("cv")
    tt = p.sem("tt")
    vh = p.sem("vh")
    tm = p.sem("tm")
    aq = p.sem("aq")
    vg = p.sem("vg")
    vo = p.sem("vo")
    ve = p.sem("ve")
    oo = Sem2(nc, "oo")

    with nc.Block() as blk:
        @blk.sync
        def _(sync):
            for ti in range(nt):
                t2 = ti % 2
                r0 = ti * 128
                if ti >= 2:
                    sync.wait_ge(cs.h, cnt[("apply", ti - 2)])
                if ti == 0:
                    lz.dma(sync.dma_start(out=ht[0:128, 0:512],
                                          in_=zin[0:128, 0:512]), ti)
                    lz.dma(sync.dma_start(out=ht[0:1, 512:Dm],
                                          in_=zhalo[0:1, 512:Dm]), ti)
                    lz.dma(sync.dma_start(out=ht[1:128, 512:Dm],
                                          in_=zin[0:127, 512:Dm]), ti)
                else:
                    lz.dma(sync.dma_start(out=ht[:, t2 * Dm:t2 * Dm + 512],
                                          in_=zin[r0:r0 + 128, 0:512]), ti)
                    lz.dma(sync.dma_start(out=ht[:, t2 * Dm + 512:(t2 + 1) * Dm],
                                          in_=zin[r0 - 1:r0 + 127, 512:Dm]), ti)
                lz.mark(ti)
                if ti >= 2:
                    sync.wait_ge(vo.h, cnt[("adds", ti - 2)])
                lf.dma(sync.dma_start(out=zfin[:, t2 * Dm:(t2 + 1) * Dm],
                                      in_=zin[r0:r0 + 128, :]), ti)
                lf.mark(ti)
                if final:
                    lxr.dma(sync.dma_start(out=xfin[:, t2 * Dm:(t2 + 1) * Dm],
                                           in_=xr_in[r0:r0 + 128, :]), ti)
                    lxr.mark(ti)

        @blk.scalar
        def _(s):
            def stats_part(ti):
                t2 = ti % 2
                lz.wait(s, ti)
                s.wait_ge(cs.h, cs.v)
                cs.eng(s.activation(out=scrap[:], in_=ht[:, t2 * Dm:(t2 + 1) * Dm],
                                    func=AF.Identity,
                                    accum_out=stat[:, t2 * 8:t2 * 8 + 1]))
                s.wait_ge(cs.h, cs.v)
                cs.eng(s.activation(out=scrap[:], in_=ht[:, t2 * Dm:(t2 + 1) * Dm],
                                    func=AF.Square,
                                    accum_out=stat[:, t2 * 8 + 1:t2 * 8 + 2]))
                assert cs.v == cnt[("stats", ti)]

            def ladder_part(ti):
                t2 = ti % 2
                s.wait_ge(cv.h, cnt[("var", ti)])
                cs.eng(s.activation(out=stat[:, t2 * 8 + 6:t2 * 8 + 7],
                                    in_=stat[:, t2 * 8 + 4:t2 * 8 + 5], func=AF.Ln))
                s.wait_ge(cs.h, cs.v)
                cs.eng(s.activation(out=stat[:, t2 * 8 + 4:t2 * 8 + 5],
                                    in_=stat[:, t2 * 8 + 6:t2 * 8 + 7],
                                    func=AF.Exp, scale=-0.5))
                assert cs.v == cnt[("rstd", ti)]
                s.wait_ge(cv.h, cnt[("nmr", ti)])
                if ti >= 1:
                    s.wait_ge(tt.h, cnt[("tp_hn2", ti - 1)])
                cs.eng(s.activation(out=hnb[:], in_=ht[:, t2 * Dm:(t2 + 1) * Dm],
                                    func=AF.Identity,
                                    scale=stat[:, t2 * 8 + 4:t2 * 8 + 5],
                                    bias=stat[:, t2 * 8 + 5:t2 * 8 + 6]))
                assert cs.v == cnt[("apply", ti)]

            stats_part(0)
            ladder_part(0)
            for ti in range(nt):
                for idx, fg in enumerate(range(0, nfg, 2)):
                    fw = min(512, F2 - fg * 512)
                    s.wait_ge(tm.h, cnt[("w1g", ti, fg)])
                    if fg == 0 and ti >= 1:
                        s.wait_ge(vg.h, cnt[("mul", ti - 1)])
                    aq.eng(s.activation(out=agb[:, fg * 512:fg * 512 + fw],
                                        in_=pp["a"][0][:, 0:fw],
                                        func=AF.Identity))
                    assert aq.v == cnt[("agbe", ti, fg)]
                    if idx == 2 and ti + 1 < nt:
                        stats_part(ti + 1)
                if ti + 1 < nt:
                    ladder_part(ti + 1)
                if ti >= 1:
                    s.wait_ge(tt.h, cnt[("tp_h2", ti - 1, ngr - 1)])
                s.wait_ge(aq.h, cnt[("agbe", ti, nfg - 1 - (nfg % 2 == 0))])
                s.wait_ge(ve.h, cnt[("agbo", ti)])
                aq.eng(s.activation(out=h2[:, 0:HID], in_=agb[:, HID:F2],
                                    func=AF.Gelu))
                assert aq.v == cnt[("geluq", ti)]

        @blk.vector
        def _(v):
            def h2t_adds(tj):
                t2 = tj % 2
                for g2 in range(ngr):
                    c0 = g2 * 4
                    c1 = min(hch, c0 + 4)
                    v.wait_ge(tt.h, cnt[("tp_h2", tj, g2)])
                    if g2 == 0 and tj >= 1:
                        v.wait_ge(tm.h, cnt[("w2g", tj - 1, ndg - 1)])
                    v.wait_ge(vh.h, vh.v)
                    vh.eng(v.tensor_copy(out=h2T[:, c0 * 128:c1 * 128],
                                         in_=pp["bf"][g2 % 2][:, 0:(c1 - c0) * 128]))
                    assert vh.v == cnt[("h2Te", tj, g2)]
                if tj >= 2:
                    oo.wait(v, tj - 2)
                lf.wait(v, tj)
                if final:
                    lxr.wait(v, tj)
                for dg in range(ndg):
                    v.wait_ge(tm.h, cnt[("w2g", tj, dg)])
                    sl = slice(t2 * Dm + dg * 512, t2 * Dm + (dg + 1) * 512)
                    if final:
                        cv.eng(v.tensor_add(xob[:, sl], pp["a"][2 + dg % 2][:],
                                            zfin[:, sl]))
                        v.wait_ge(cv.h, cv.v)
                        vo.eng(v.scalar_tensor_tensor(
                            out=xob[:, sl], in0=xob[:, sl], scalar=0.5,
                            in1=xfin[:, sl], op0=ALU.mult, op1=ALU.add))
                    else:
                        vo.eng(v.tensor_add(xob[:, sl], pp["a"][2 + dg % 2][:],
                                            zfin[:, sl]))
                assert vo.v == cnt[("adds", tj)]

            def var_chain(ti):
                t2 = ti % 2
                v.wait_ge(cs.h, cnt[("stats", ti)])
                cv.eng(v.tensor_scalar_mul(stat[:, t2 * 8 + 2:t2 * 8 + 3],
                                           stat[:, t2 * 8:t2 * 8 + 1], 1.0 / Dm))
                v.wait_ge(cv.h, cv.v)
                cv.eng(v.scalar_tensor_tensor(out=stat[:, t2 * 8 + 4:t2 * 8 + 5],
                                              in0=stat[:, t2 * 8 + 2:t2 * 8 + 3],
                                              scalar=-1.0,
                                              in1=stat[:, t2 * 8 + 2:t2 * 8 + 3],
                                              op0=ALU.mult, op1=ALU.mult))
                v.wait_ge(cv.h, cv.v)
                cv.eng(v.scalar_tensor_tensor(out=stat[:, t2 * 8 + 4:t2 * 8 + 5],
                                              in0=stat[:, t2 * 8 + 1:t2 * 8 + 2],
                                              scalar=1.0 / Dm,
                                              in1=stat[:, t2 * 8 + 4:t2 * 8 + 5],
                                              op0=ALU.mult, op1=ALU.add))
                v.wait_ge(cv.h, cv.v)
                cv.eng(v.tensor_scalar_add(stat[:, t2 * 8 + 4:t2 * 8 + 5],
                                           stat[:, t2 * 8 + 4:t2 * 8 + 5], 1e-5))
                assert cv.v == cnt[("var", ti)]
                v.wait_ge(cs.h, cnt[("rstd", ti)])
                cv.eng(v.scalar_tensor_tensor(out=stat[:, t2 * 8 + 5:t2 * 8 + 6],
                                              in0=stat[:, t2 * 8 + 2:t2 * 8 + 3],
                                              scalar=-1.0,
                                              in1=stat[:, t2 * 8 + 4:t2 * 8 + 5],
                                              op0=ALU.mult, op1=ALU.mult))
                assert cv.v == cnt[("nmr", ti)]

            var_chain(0)
            for ti in range(nt):
                t2 = ti % 2
                for g2 in range(2):
                    v.wait_ge(tt.h, cnt[("tp_hn1", ti)] + g2)
                    if ti >= 2 and g2 == 0:
                        v.wait_ge(tm.h, cnt[("w1g", ti - 2, nfg - 1)])
                    v.wait_ge(vh.h, vh.v)
                    for j in range(4):
                        d = g2 * 4 + j
                        c0 = t2 * Dm + g2 * 512 + j * 128
                        ins = v.tensor_scalar_mul(hnT[:, c0:c0 + 128],
                                                  pp["bf"][g2 % 2][:, j * 128:(j + 1) * 128],
                                                  lgc[:, d:d + 1])
                    vh.eng(ins)
                assert vh.v == cnt[("hnTe", ti)]
                for fg in range(1, nfg, 2):
                    fw = min(512, F2 - fg * 512)
                    v.wait_ge(tm.h, cnt[("w1g", ti, fg)])
                    if fg == 1 and ti >= 1:
                        v.wait_ge(vg.h, cnt[("mul", ti - 1)])
                    v.wait_ge(ve.h, ve.v)
                    ve.eng(v.tensor_copy(out=agb[:, fg * 512:fg * 512 + fw],
                                         in_=pp["a"][1][:, 0:fw]))
                    assert ve.v == cnt[("agbe", ti, fg)]
                if ti + 1 < nt:
                    var_chain(ti + 1)
                if ti >= 1:
                    h2t_adds(ti - 1)
                v.wait_ge(aq.h, cnt[("geluq", ti)])
                vg.eng(v.tensor_mul(h2[:, 0:HID], agb[:, 0:HID], h2[:, 0:HID]))
                assert vg.v == cnt[("mul", ti)]
            h2t_adds(nt - 1)

        @blk.tensor
        def _(t):
            def h2_w2(tj):
                t2 = tj % 2
                t.wait_ge(vg.h, cnt[("mul", tj)])
                t.wait_ge(vh.h, cnt[("hnTe", tj + 1)] if tj + 1 < nt
                          else cnt[("hnTe", nt - 1)])
                for g2 in range(ngr):
                    c0 = g2 * 4
                    c1 = min(hch, c0 + 4)
                    if g2 >= 2:
                        t.wait_ge(vh.h, cnt[("h2Te", tj, g2 - 2)])
                    for j in range(c1 - c0):
                        ins = t.transpose(out=pp["bf"][g2 % 2][:, j * 128:(j + 1) * 128],
                                          in_=h2[:, (c0 + j) * 128:(c0 + j + 1) * 128],
                                          identity=consts["ident"][:])
                    tt.eng(ins)
                    assert tt.v == cnt[("tp_h2", tj, g2)]
                t.wait_ge(vh.h, cnt[("h2Te", tj, ngr - 1)])
                for dg in range(ndg):
                    if tj >= 1:
                        t.wait_ge(vo.h, cnt[("add", tj - 1, dg)])
                    pairs = [(h2T[:, c * 128:(c + 1) * 128],
                              w2sb[:, c * Dm + dg * 512: c * Dm + (dg + 1) * 512])
                             for c in range(hch)]
                    tm.eng(mm_acc(t, pp["a"][2 + dg % 2][:], pairs,
                                  rank1=(consts["ones"][0:1, 0:128],
                                         b2r[0:1, dg * 512:(dg + 1) * 512])))
                    assert tm.v == cnt[("w2g", tj, dg)]

            for ti in range(nt):
                t2 = ti % 2
                if ti == 0:
                    t.wait_ge(lw.h, lw.v)
                    t.wait_ge(lw2.h, lw2.v)
                    t.wait_ge(vv0.h, vv0_all)
                t.wait_ge(cs.h, cnt[("apply", ti)])
                if ti >= 2:
                    t.wait_ge(vh.h, cnt[("h2Te", ti - 2, ngr - 1)])
                elif ti == 1:
                    t.wait_ge(vh.h, cnt[("hnTe", 0)])
                for g2 in range(2):
                    for j in range(4):
                        d = g2 * 4 + j
                        ins = t.transpose(out=pp["bf"][g2 % 2][:, j * 128:(j + 1) * 128],
                                          in_=hnb[:, d * 128:(d + 1) * 128],
                                          identity=consts["ident"][:])
                    tt.eng(ins)
                assert tt.v == cnt[("tp_hn2", ti)]
                t.wait_ge(vh.h, cnt[("hnTe", ti)])
                for fg in range(nfg):
                    fw = min(512, F2 - fg * 512)
                    sem_g = aq if fg % 2 == 0 else ve
                    if fg >= 2:
                        t.wait_ge(sem_g.h, cnt[("agbe", ti, fg - 2)])
                    elif ti >= 1:
                        pfg = nfg - 1 - ((nfg - 1) % 2 != fg % 2)
                        t.wait_ge(sem_g.h, cnt[("agbe", ti - 1, pfg)])
                    pairs = [(hnT[:, t2 * Dm + d * 128:t2 * Dm + (d + 1) * 128],
                              w1sb[:, d * F2 + fg * 512: d * F2 + fg * 512 + fw])
                             for d in range(dch)]
                    tm.eng(mm_acc(t, pp["a"][fg % 2][:, 0:fw], pairs,
                                  rank1=(consts["ones"][0:1, 0:128],
                                         b1e[0:1, fg * 512:fg * 512 + fw])))
                    assert tm.v == cnt[("w1g", ti, fg)]
                if ti >= 1:
                    h2_w2(ti - 1)
            h2_w2(nt - 1)

        @blk.gpsimd
        def _(g):
            for ti in range(nt):
                t2 = ti % 2
                g.wait_ge(vo.h, cnt[("adds", ti)])
                oo.dma(g.dma_start(out=zout[ti * 128:(ti + 1) * 128, :],
                                   in_=xob[:, t2 * Dm:(t2 + 1) * Dm]), ti)
                oo.mark(ti)

    with nc.Block() as blk:
        @blk.gpsimd
        def _(g):
            g.wait_ge(oo.s[0].h, oo.s[0].v)
            g.wait_ge(oo.s[1].h, oo.s[1].v)
            g.wait_ge(lz.s[0].h, lz.s[0].v)
            g.wait_ge(lz.s[1].h, lz.s[1].v)
            g.wait_ge(lf.s[0].h, lf.s[0].v)
            g.wait_ge(lf.s[1].h, lf.s[1].v)
            if final:
                g.wait_ge(lxr.s[0].h, lxr.s[0].v)
                g.wait_ge(lxr.s[1].h, lxr.s[1].v)
    return p


def build_prep():
    p = Prog("prep")
    nc = p.nc
    specs = [
        ("pw1a", [2048, 1024]),
        ("pw2a", [1024, 2048]),
        ("pw1b", [2, 1024, 2048]),
        ("pw2b", [2, 2048, 1024]),
        ("pwf1", [512, 5460]),
        ("pwf2", [1365, 1024]),
        ("pqkv", [512, 1536]),
        ("pwo", [256, 1024]),
    ]
    sems = []
    ins, outs = [], []
    for name, shp in specs:
        ins.append(p.inp(name, shp))
        outs.append(p.out(name + "_b", shp, BF16))
    ws = p.sem("ws")
    with nc.Block() as blk:
        @blk.gpsimd
        def _(g):
            for a, b in zip(ins, outs):
                ws.dma(g.dma_start(out=b[:], in_=a[:]))
            g.wait_ge(ws.h, ws.v)
    return p

_PROG_CACHE = {}


def _get(key, builder, *args, **kw):
    if key not in _PROG_CACHE:
        _PROG_CACHE[key] = builder(*args, **kw)
    return _PROG_CACHE[key]


def _run(p, ims):
    from concourse.bass_utils import run_bass_kernel_spmd
    return run_bass_kernel_spmd(p.nc, ims, core_ids=list(range(NCORES))).results



_LFA_WIN = 512
_LFA_HB = 640
_LFA_H = 8
_LFA_HD = 64


def _lfa_tables(pos, n):
    H, HD, WIN = _LFA_H, _LFA_HD, _LFA_WIN
    half = np.arange(0, HD, 2, dtype=np.float64)
    inv = 1.0 / 10000.0 ** (half / HD)
    freqs = pos[:, None] * inv[None, :]
    freqs = np.concatenate([freqs, freqs], -1)
    base = (half + 0.4 * HD) / (1.4 * HD)
    power = (pos - n // 2) / float(WIN // 2)
    scale = base[None, :] ** power[:, None]
    scale = np.concatenate([scale, scale], -1)
    C, S = np.cos(freqs), np.sin(freqs)
    sign = np.where(np.arange(HD) < HD // 2, -1.0, 1.0)
    qA, qB = C * scale, sign * S * scale
    kA, kB = C / scale, sign * S / scale
    t8 = lambda a: np.tile(a, (1, H))
    return (np.concatenate([t8(qA), t8(qB)], 1).astype(BF),
            np.concatenate([t8(kA), t8(kB)], 1).astype(BF))


def _lfa_masks(first_invalid):
    WIN = _LFA_WIN
    m = np.zeros((16, 128, 512), np.float32)
    for g in range(2):
        for kt in range(8):
            pr = np.arange(128)[:, None]
            fr = np.arange(512)[None, :]
            d = WIN + fr - (kt * 128 + pr)
            v = np.where((d >= 0) & (d <= WIN), 0.0, -1e30).astype(np.float32)
            if g == 0 and kt < 4 and first_invalid:
                v[:] = -1e30
            m[g * 8 + kt] = v
    return m.reshape(16 * 128, 512)


def _rope_scale(T, dim=32):
    t = np.arange(T, dtype=np.float64)
    inv = 1.0 / 10000.0 ** (np.arange(0, dim, 2, dtype=np.float64) / dim)
    freqs = t[:, None] * inv[None, :]
    emb = np.concatenate([freqs, freqs], -1)
    return (np.cos(emb) + np.sin(emb)).astype(np.float32)




def _rows(x, c):
    return np.ascontiguousarray(x[c * R:(c + 1) * R])


def _prep_step(inp):
    p = _get("prep", build_prep)
    l0w1 = np.asarray(inp["L0_ffn_w1"], np.float32)
    l0w2 = np.asarray(inp["L0_ffn_w2"], np.float32)
    l2w1 = np.asarray(inp["L2_ffn_w1"], np.float32)
    l2w2 = np.asarray(inp["L2_ffn_w2"], np.float32)
    f1 = np.concatenate([np.asarray(inp["L1_ffn_w1"], np.float32),
                         np.asarray(inp["L3_ffn_w1"], np.float32)])
    f2 = np.concatenate([np.asarray(inp["L1_ffn_w2"], np.float32),
                         np.asarray(inp["L3_ffn_w2"], np.float32)])
    qv = np.concatenate([np.asarray(inp["L1_ffn_wqkv"], np.float32),
                         np.asarray(inp["L3_ffn_wqkv"], np.float32)])
    wo = np.concatenate([np.asarray(inp["L1_ffn_wo"], np.float32),
                         np.asarray(inp["L3_ffn_wo"], np.float32)])
    f1p = np.ascontiguousarray(f1).reshape(8, 512, 5460)
    f2p = np.ascontiguousarray(f2).reshape(8, 1365, 1024)
    qvp = np.ascontiguousarray(qv).reshape(8, 512, 1536)
    wop = np.ascontiguousarray(wo).reshape(8, 256, 1024)
    ims = []
    for c in range(NCORES):
        e0 = min(2 * c, 12)
        ims.append(dict(
            pw1a=np.ascontiguousarray(l0w1[c]),
            pw2a=np.ascontiguousarray(l0w2[c]),
            pw1b=np.ascontiguousarray(l2w1[e0:e0 + 2]),
            pw2b=np.ascontiguousarray(l2w2[e0:e0 + 2]),
            pwf1=np.ascontiguousarray(f1p[c]),
            pwf2=np.ascontiguousarray(f2p[c]),
            pqkv=np.ascontiguousarray(qvp[c]),
            pwo=np.ascontiguousarray(wop[c])))
    res = _run(p, ims)
    pr = {}
    pr["l0w1b"] = np.stack([res[c]["pw1a_b"] for c in range(8)])
    pr["l0w2b"] = np.stack([res[c]["pw2a_b"] for c in range(8)])
    pr["l2w1b"] = np.concatenate([res[c]["pw1b_b"] for c in range(7)])
    pr["l2w2b"] = np.concatenate([res[c]["pw2b_b"] for c in range(7)])
    pr["lffw1b"] = np.stack([res[c]["pwf1_b"] for c in range(8)]).reshape(4, 1024, 5460)
    pr["lffw2b"] = np.stack([res[c]["pwf2_b"] for c in range(8)]).reshape(4, 2730, 1024)
    pr["qkvb"] = np.stack([res[c]["pqkv_b"] for c in range(8)]).reshape(4, 1024, 1536)
    pr["wob"] = np.stack([res[c]["pwo_b"] for c in range(8)]).reshape(4, 512, 1024)
    return pr


def _l0ffn_step(x, inp, ident, pr):
    p = _get("l0ffn", build_l0ffn, R, 8, 12, 2048, D, pre=True)
    gw = np.asarray(inp["L0_ffn_gate_w"], np.float32)
    eb = np.asarray(inp["L0_ffn_expert_bias"], np.float32)[None]
    ims = [dict(x=_rows(x, c), gate_w=gw, ebias=eb, w1=pr["l0w1b"],
                w2=pr["l0w2b"], c_ident=ident) for c in range(NCORES)]
    res = _run(p, ims)
    return np.concatenate([res[c]["xout"] for c in range(NCORES)], axis=0)


def _l0attn_step(x, inp, ident):
    H, HD = 16, 64
    Wqk = np.asarray(inp["L0_attn_Wqk"], np.float32)
    bqk = np.asarray(inp["L0_attn_bqk"], np.float32)
    Wv = np.asarray(inp["L0_attn_Wv"], np.float32)
    bv = np.asarray(inp["L0_attn_bv"], np.float32)
    Wn = np.asarray(inp["L0_attn_Wn"], np.float32)
    bn = np.asarray(inp["L0_attn_bn"], np.float32)
    Wo = np.asarray(inp["L0_attn_Wo"], np.float32)
    bo = np.asarray(inp["L0_attn_bo"], np.float32)

    pA = _get("attA", build_l0attA, R, D)
    resA = _run(pA, [dict(x=_rows(x, c), c_ident=ident) for c in range(NCORES)])
    xnT = np.concatenate([resA[c]["xnT_s"] for c in range(NCORES)], axis=1)

    pB = _get("l0attB", build_l0attB, NTOK, D, 2, HD, SEQ)
    imsB = []
    for c in range(NCORES):
        h0 = c * 2
        sl = slice(h0 * HD, (h0 + 2) * HD)
        ksl = slice(H * HD + h0 * HD, H * HD + (h0 + 2) * HD)
        imsB.append(dict(
            xnT_g=xnT, Wq=np.ascontiguousarray(Wqk[:, sl]),
            Wk=np.ascontiguousarray(Wqk[:, ksl]),
            Wv=np.ascontiguousarray(Wv[:, sl]),
            Wn=np.ascontiguousarray(Wn[:, h0:h0 + 2]),
            bq=np.ascontiguousarray(bqk[None, sl]),
            bk=np.ascontiguousarray(bqk[None, ksl]),
            bv=np.ascontiguousarray(bv[None, sl]),
            bn=np.ascontiguousarray(bn[None, h0:h0 + 2]), c_ident=ident))
    resB = _run(pB, imsB)
    ctx = np.concatenate([resB[c]["ctx_s"] for c in range(NCORES)], axis=1)

    pC = _get("l0attC", build_l0attC, R, D, H * HD)
    imsC = [dict(x=_rows(x, c), ctx=_rows(ctx, c), Wo=Wo, bo=bo[None],
                 c_ident=ident) for c in range(NCORES)]
    resC = _run(pC, imsC)
    return np.concatenate([resC[c]["xout"] for c in range(NCORES)], axis=0)


def _lfa_launch(z, wqkv, qs, ks, wo, ident, with_ln):
    H, HD, HB = _LFA_H, _LFA_HD, _LFA_HB
    p = _get(f"lfa{int(with_ln)}", build_lfa2, R, D, H, HD, _LFA_WIN,
             with_ln=with_ln, pre=True)
    sw = lambda a: np.concatenate([a[HD // 2:], a[:HD // 2]])
    qs4 = np.concatenate([np.tile(qs, H), np.tile(sw(qs), H),
                          np.tile(ks, H), np.tile(sw(ks), H)])[None]
    qs4 = np.ascontiguousarray(qs4, dtype=np.float32)
    zero_halo = np.zeros((HB, D), np.float32)
    ims = []
    for c in range(NCORES):
        cb = c % 4
        t0 = cb * R
        rq, _ = _lfa_tables(np.arange(t0, t0 + R, dtype=np.float64), SEQ)
        _, rk = _lfa_tables(np.arange(t0 - HB + 128, t0 + R, dtype=np.float64),
                            SEQ)
        halo = zero_halo if cb == 0 else \
            np.ascontiguousarray(z[c * R - HB:c * R])
        ims.append(dict(zin=_rows(z, c), zhalo=halo, wqkv=wqkv, wo=wo,
                        qs4=qs4, rq=rq, rk=rk,
                        masks=_lfa_masks(cb == 0).astype(BF), c_ident=ident))
    res = _run(p, ims)
    return np.concatenate([res[c]["zout"] for c in range(NCORES)], axis=0)


def _lff_launch(z, w1, b1, w2, b2, lng, lnb, ident, final, xres):
    p = _get(f"lff{int(final)}", build_lff3, R, D, 2730, final=final, pre=True)
    zero_row = np.zeros((1, D), np.float32)
    ims = []
    for c in range(NCORES):
        cb = c % 4
        halo = zero_row if cb == 0 else \
            np.ascontiguousarray(z[c * R - 1:c * R])
        im = dict(zin=_rows(z, c), zhalo=halo, w1=w1, b1=b1[None], w2=w2,
                  b2=b2[None], lng=lng[None], lnb=lnb[None], c_ident=ident)
        if final:
            im["xr"] = _rows(xres, c)
        ims.append(im)
    res = _run(p, ims)
    return np.concatenate([res[c]["zout"] for c in range(NCORES)], axis=0)


def _lffn_step(x, inp, pfx, ident, pr):
    g = lambda n: np.asarray(inp[f"{pfx}_ffn_{n}"], np.float32)
    qs, ks = g("q_scale"), g("k_scale")
    lng, lnb = g("ln_g"), g("ln_b")
    b1, b2 = g("b1"), g("b2")
    po = 0 if pfx == "L1" else 2
    z = x
    for l in range(2):
        z = _lfa_launch(z, pr["qkvb"][po + l], qs[l], ks[l],
                        pr["wob"][po + l], ident, with_ln=(l == 0))
        z = _lff_launch(z, pr["lffw1b"][po + l], b1[l],
                        pr["lffw2b"][po + l], b2[l], lng[l], lnb[l],
                        ident, final=(l == 1), xres=x)
    return z


def _l1attn_step(x, inp, ident):
    H, HD, HperCore = 16, 64, 4
    W_A = np.asarray(inp["L1_attn_W_A"], np.float32)
    W_V = np.asarray(inp["L1_attn_W_V"], np.float32)
    W_O = np.asarray(inp["L1_attn_W_O"], np.float32)
    b_O = np.asarray(inp["L1_attn_b_O"], np.float32)

    pA = _get("l1attA", build_l1attA, R, D, H)
    waT = np.ascontiguousarray(W_A.T)
    resA = _run(pA, [dict(x=_rows(x, c), waT=waT, c_ident=ident)
                     for c in range(NCORES)])
    xnT = np.concatenate([resA[c]["xnT_s"] for c in range(NCORES)], axis=1)
    zT = np.concatenate([resA[c]["zT_s"] for c in range(NCORES)], axis=1)

    pB = _get("l1attB", build_l1attB, SEQ, D, HperCore, HD, H)
    imsB = []
    for c in range(NCORES):
        b = c // 4
        h0 = (c % 4) * HperCore
        cols = slice(b * SEQ, (b + 1) * SEQ)
        imsB.append(dict(xnT_b=np.ascontiguousarray(xnT[:, cols]),
                         zT4=np.ascontiguousarray(zT[h0:h0 + HperCore, cols]),
                         wvr=np.ascontiguousarray(
                             W_V[h0 * HD:(h0 + HperCore) * HD, :]),
                         c_ident=ident))
    resB = _run(pB, imsB)
    ctxT = np.zeros((H * HD, NTOK), BF)
    for c in range(NCORES):
        b = c // 4
        h0 = (c % 4) * HperCore
        ctxT[h0 * HD:(h0 + HperCore) * HD,
             b * SEQ:(b + 1) * SEQ] = resB[c]["ctxT_s"]

    pC = _get("attCT", build_attCT, R, D, H * HD)
    imsC = [dict(x=_rows(x, c),
                 ctxT_r=np.ascontiguousarray(ctxT[:, c * R:(c + 1) * R]),
                 Wo=np.ascontiguousarray(W_O.T), bo=b_O[None], c_ident=ident)
            for c in range(NCORES)]
    resC = _run(pC, imsC)
    return np.concatenate([resC[c]["xout"] for c in range(NCORES)], axis=0)


def _l2ffn_step(x, inp, ident, pr):
    p = _get("l2ffn", build_l2ffn, R, 14, 2048, D, pre=True)
    rw = np.asarray(inp["L2_ffn_router_w"], np.float32)
    b1 = np.asarray(inp["L2_ffn_b1"], np.float32)
    b2 = np.asarray(inp["L2_ffn_b2"], np.float32)
    ims = [dict(x=_rows(x, c), router_w=rw, w1=pr["l2w1b"], b1=b1,
                w2=pr["l2w2b"], b2=b2, c_ident=ident) for c in range(NCORES)]
    res = _run(p, ims)
    return np.concatenate([res[c]["xout"] for c in range(NCORES)], axis=0)


def _l2attn_step(x, inp, ident):
    w = dict(
        W_f=np.asarray(inp["L2_attn_W_f"], np.float32),
        b_f=np.asarray(inp["L2_attn_b_f"], np.float32)[None],
        W_i=np.asarray(inp["L2_attn_W_i"], np.float32),
        b_i=np.asarray(inp["L2_attn_b_i"], np.float32)[None],
        W_h=np.asarray(inp["L2_attn_W_h"], np.float32),
        b_h=np.asarray(inp["L2_attn_b_h"], np.float32)[None],
    )
    p1 = _get("l2scan1", build_l2scan1, R, D)
    ims1 = [dict(x=_rows(x, c), c_ident=ident, **w) for c in range(NCORES)]
    res1 = _run(p1, ims1)

    p2 = _get("l2scan2", build_l2scan2, R, D)
    ident_fc = np.zeros((D, 2), np.float32)
    ident_fc[:, 0] = 1.0
    ims2 = []
    for c in range(NCORES):
        pos = c % 4
        b0 = (c // 4) * 4
        fcprev = np.stack([ident_fc] * (3 - pos) +
                          [res1[b0 + j]["FC"] for j in range(pos)])
        ims2.append(dict(x=_rows(x, c), localout=res1[c]["localout"],
                         prefP=res1[c]["prefP"],
                         FCprev=np.ascontiguousarray(fcprev, np.float32),
                         c_ident=ident))
    res2 = _run(p2, ims2)
    return np.concatenate([res2[c]["xout"] for c in range(NCORES)], axis=0)


def _l3attn_step(x, inp, ident):
    H, HS, DHR, HperCore = 16, 64, 32, 4
    W_DQ = np.asarray(inp["L3_attn_W_DQ"], np.float32)
    W_UQ = np.asarray(inp["L3_attn_W_UQ"], np.float32)
    W_DKV = np.asarray(inp["L3_attn_W_DKV"], np.float32)
    W_UK = np.asarray(inp["L3_attn_W_UK"], np.float32)
    W_UV = np.asarray(inp["L3_attn_W_UV"], np.float32)
    W_QR = np.asarray(inp["L3_attn_W_QR"], np.float32)
    W_KR = np.asarray(inp["L3_attn_W_KR"], np.float32)
    W_O = np.asarray(inp["L3_attn_W_O"], np.float32)
    DC = W_DQ.shape[1]
    DR = H * DHR

    pW = _get("l3attW2", build_l3attW2, 512, DC, 512)
    Bfull = np.concatenate([W_UQ, W_QR], axis=1)
    imsW = []
    for c in range(NCORES):
        blk = c % 6
        dh, og = blk % 2, blk // 2
        imsW.append(dict(
            Arows=np.ascontiguousarray(W_DQ[dh * 512:(dh + 1) * 512]),
            Bcols=np.ascontiguousarray(Bfull[:, og * 512:(og + 1) * 512])))
    resW = _run(pW, imsW)
    W_eff = np.zeros((D, D), BF)
    W_effR = np.zeros((D, DR), BF)
    for blk in range(6):
        dh, og = blk % 2, blk // 2
        part = resW[blk]["Wpart"]
        if og < 2:
            W_eff[dh * 512:(dh + 1) * 512, og * 512:(og + 1) * 512] = part
        else:
            W_effR[dh * 512:(dh + 1) * 512, :] = part

    pA = _get("l3attA", build_l3attA, R, D, DR, 256, DHR)
    scl = np.concatenate([_rope_scale(SEQ, DHR)] * B, axis=0)
    imsA = []
    for c in range(NCORES):
        rsl = slice(c * R, (c + 1) * R)
        sq = np.ascontiguousarray(
            np.concatenate([scl[rsl].T.astype(BF)] * H, axis=0))
        sk = np.ascontiguousarray(scl[rsl].T.astype(BF))
        imsA.append(dict(x=_rows(x, c), W_eff=W_eff, W_effR=W_effR,
                         W_DKV=W_DKV, W_KR=W_KR, scaleq=sq, scalek=sk,
                         c_ident=ident))
    resA = _run(pA, imsA)
    qcT = np.concatenate([resA[c]["qcT_s"] for c in range(NCORES)],
                         axis=1)[:H * HS]
    qrT = np.concatenate([resA[c]["qrT_s"] for c in range(NCORES)], axis=1)
    ckvT = np.concatenate([resA[c]["ckvT_s"] for c in range(NCORES)], axis=1)
    krT = np.concatenate([resA[c]["krT_s"] for c in range(NCORES)], axis=1)

    pB = _get("l3attB", build_l3attB, SEQ, HperCore, HS, DHR, 256, D)
    imsB = []
    for c in range(NCORES):
        b = c // 4
        h0 = (c % 4) * HperCore
        cols = slice(b * SEQ, (b + 1) * SEQ)
        im = dict(ckvT=np.ascontiguousarray(ckvT[:, cols]),
                  krT=np.ascontiguousarray(krT[:, cols]),
                  W_UK4=np.ascontiguousarray(
                      W_UK[:, h0 * HS:(h0 + HperCore) * HS]),
                  W_UV4=np.ascontiguousarray(
                      W_UV[:, h0 * HS:(h0 + HperCore) * HS]), c_ident=ident)
        for j in range(HperCore):
            h = h0 + j
            im[f"qhT_{j}"] = np.ascontiguousarray(np.concatenate(
                [qcT[h * HS:(h + 1) * HS, cols],
                 qrT[h * DHR:(h + 1) * DHR, cols]], axis=0))
        imsB.append(im)
    resB = _run(pB, imsB)
    ctxT = np.zeros((H * HS, NTOK), BF)
    for c in range(NCORES):
        b = c // 4
        h0 = (c % 4) * HperCore
        ctxT[h0 * HS:(h0 + HperCore) * HS,
             b * SEQ:(b + 1) * SEQ] = resB[c]["ctxT_s"]

    pC = _get("attCT", build_attCT, R, D, H * HS)
    zb = np.zeros((1, D), np.float32)
    imsC = [dict(x=_rows(x, c),
                 ctxT_r=np.ascontiguousarray(ctxT[:, c * R:(c + 1) * R]),
                 Wo=W_O, bo=zb, c_ident=ident) for c in range(NCORES)]
    resC = _run(pC, imsC)
    return np.concatenate([resC[c]["xout"] for c in range(NCORES)], axis=0)


def kernel(**inputs):
    ident = host_ident()
    pr = _prep_step(inputs)
    x = np.ascontiguousarray(
        np.asarray(inputs["L0_attn_x"], np.float32).reshape(NTOK, D))
    x = _l0ffn_step(x, inputs, ident, pr)
    x = _l0attn_step(x, inputs, ident)
    x = _l0ffn_step(x, inputs, ident, pr)
    x = _lffn_step(x, inputs, "L1", ident, pr)
    x = _l1attn_step(x, inputs, ident)
    x = _lffn_step(x, inputs, "L1", ident, pr)
    x = _l2ffn_step(x, inputs, ident, pr)
    x = _l2attn_step(x, inputs, ident)
    x = _l2ffn_step(x, inputs, ident, pr)
    x = _lffn_step(x, inputs, "L3", ident, pr)
    x = _l3attn_step(x, inputs, ident)
    x = _lffn_step(x, inputs, "L3", ident, pr)
    return np.ascontiguousarray(x.reshape(B, SEQ, D)).astype(np.float32)
